# version 01
# baseline (speedup 1.0000x reference)
import numpy as np
import ml_dtypes
from contextlib import ExitStack

import concourse.bass as bass
import concourse.mybir as mybir
from concourse.bass_utils import run_bass_kernel_spmd

F32 = mybir.dt.float32
BF16 = mybir.dt.bfloat16
AF = mybir.ActivationFunctionType
ALU = mybir.AluOpType
AX = mybir.AxisListType

T = 2048
NT = 16
D = 1024
ND = 8
FFN = 4096
NEXP = 8
NB = 2
NITER_CD = 50
NITER_TH = 48
NITER_IX = 14
LOGK = float(np.log(np.float32(1152.0)))


class Sem:
    def __init__(self, h):
        self.h = h
        self.n = 0


class Builder:
    ENGINES = ("tensor", "vector", "scalar", "gpsimd", "sync")

    def __init__(self, nc, es):
        self.nc = nc
        self.es = es
        self.q = {e: [] for e in self.ENGINES}
        self._sems = {}
        self.tapnames = set()

    def sem(self, name):
        if name not in self._sems:
            self._sems[name] = Sem(self.es.enter_context(self.nc.semaphore(name)))
        return self._sems[name]

    def sb(self, name, shape, dt=F32, **kw):
        return self.es.enter_context(self.nc.sbuf_tensor(name, shape, dt, **kw))

    def ps(self, name, shape, dt=F32):
        return self.es.enter_context(self.nc.psum_tensor(name, shape, dt))

    def dma_pool(self, tag):
        if not hasattr(self, "_dmapools"):
            self._dmapools = {}
            self._dmarr = {}
        if tag not in self._dmapools:
            self._dmapools[tag] = [self.sem("dmas_%s%d" % (tag, i))
                                   for i in range(8)]
            self._dmarr[tag] = -1
        return self._dmapools[tag]

    def op(self, eng, fn, wait=(), inc=()):
        toks = []
        incs = []
        extra_wait = []
        for s, k in inc:
            if getattr(s, "isdma", False):
                tag = s.tag
                pool = self.dma_pool(tag)
                self._dmarr[tag] = (self._dmarr[tag] + 1) % len(pool)
                slot = pool[self._dmarr[tag]]
                extra_wait.append((slot, slot.n))
                slot.n += 16
                incs.append((slot.h, 16))
                toks.append((slot, slot.n))
            else:
                s.n += k
                incs.append((s.h, k))
                toks.append((s, s.n))
        waits = tuple((s.h, v) for (s, v) in tuple(wait) + tuple(extra_wait)
                      if s is not None)

        def run(engine, fn=fn, waits=waits, incs=incs):
            for h, v in waits:
                engine.wait_ge(h, v)
            r = fn(engine)
            for h, k in incs:
                r = r.then_inc(h, k)

        self.q[eng].append(run)
        return toks

    def wait(self, eng, tokens):
        waits = tuple((s.h, v) for (s, v) in tokens if s is not None)

        def run(engine, waits=waits):
            for h, v in waits:
                engine.wait_ge(h, v)

        self.q[eng].append(run)

    def barrier(self, engines=None):
        engines = engines or self.ENGINES
        s = self.sem("bar")
        for e in engines:
            self.op(e, lambda eng: eng.nop(), inc=[(s, 1)])
        target = s.n
        for e in engines:
            self.wait(e, [(s, target)])
        return (s, target)

    def finalize(self):
        with self.nc.Block() as blk:
            for eng in self.ENGINES:
                def body(engine, eng=eng):
                    for f in self.q[eng]:
                        f(engine)
                getattr(blk, eng)(body)


def bcol(ap, j, w=1):
    return ap[:, j * w:(j + 1) * w]


def build_program(taps=(), stop_after=None):
    nc = bass.Bass("TRN2", target_bir_lowering=False, detect_race_conditions=False)
    es = ExitStack()
    b = Builder(nc, es)
    b.tapnames = set(taps)

    g = {}

    def din(name, shape):
        g[name] = nc.dram_tensor(name, list(shape), F32, kind="ExternalInput").ap()

    din("x", (NB, T, D))
    din("L0_attn_norm_w", (D,))
    din("L0_attn_A_w", (D, D)); din("L0_attn_A_b", (D,))
    din("L0_attn_B_w", (D, D)); din("L0_attn_C_w", (D, D))
    din("L0_attn_D_w", (D, D))
    din("L0_attn_out_w", (D, D)); din("L0_attn_out_b", (D,))
    din("L0_ffn_gamma_light", (D,)); din("L0_ffn_wq_light", (D, 512))
    din("L0_ffn_wkv_light", (D, 1024)); din("L0_ffn_wo_light", (512, D))
    din("L0_ffn_q_routing_token", (D,)); din("L0_ffn_kv_routing_token", (D,))
    din("L0_ffn_gamma_heavy", (D,)); din("L0_ffn_wq_heavy", (D, 512))
    din("L0_ffn_wkv_heavy", (1024, D)); din("L0_ffn_wo_heavy", (512, D))
    din("L1_attn_Wq", (D, 1024)); din("L1_attn_bq", (1024,))
    din("L1_attn_Wk", (D, 64)); din("L1_attn_bk", (64,))
    din("L1_attn_Wv", (D, 64)); din("L1_attn_bv", (64,))
    din("L1_attn_Wo", (1024, D)); din("L1_attn_bo", (D,))
    din("L1_ffn_router_w", (D, NEXP)); din("L1_ffn_router_b", (NEXP,))
    din("L1_ffn_w1", (NEXP, D, 2 * FFN)); din("L1_ffn_w2", (NEXP, FFN, D))

    out = nc.dram_tensor("out", [NB, T, D], F32, kind="ExternalOutput").ap()
    scratch_x1 = nc.dram_tensor("scratch_x1", [NB, T, D], F32, kind="Internal").ap()

    tap_handles = {}

    def tap(name, shape, dt=F32):
        if name in b.tapnames and name not in tap_handles:
            tap_handles[name] = nc.dram_tensor(
                "tap_" + name, list(shape), dt, kind="ExternalOutput").ap()
        return tap_handles.get(name)

    def hbm_tiles(ap2d):
        return ap2d.rearrange("(t p) d -> t p d", p=128)

    S128 = b.sb("S128", [128, NT])
    MASKC = b.sb("MASKC", [128, NT])
    BFC = b.sb("BFC", [128, NT])
    RNC = b.sb("RNC", [128, NT])
    IOTA = b.sb("IOTA", [128, NT])
    SELC = b.sb("SELC", [128, NT * NEXP])
    NWC = b.sb("NWC", [128, ND])
    GLC = b.sb("GLC", [128, ND])
    GHC = b.sb("GHC", [128, ND])
    ABC = b.sb("ABC", [128, ND])
    BQC = b.sb("BQC", [128, ND])
    RB = b.sb("RB", [128, D])
    ONES = b.sb("ONES", [1, 128])
    ONESB = b.sb("ONESB", [1, 128], BF16)
    ROWSB = b.sb("ROWSB", [1, 2 * D], BF16)
    BVROW = b.sb("BVROW", [1, 64], BF16)
    RWF = b.sb("RWF", [128, ND * NEXP])
    RBROW = b.sb("RBROW", [1, NEXP])
    BKC = b.sb("BKC", [64, 1])
    IDB = b.sb("IDB", [128, 128], BF16)
    IDF = b.sb("IDF", [128, 128])
    XC = b.sb("XC", [128, D])
    XN = b.sb("XN", [128, D])
    TBF1 = b.sb("TBF1", [128, D], BF16)
    TBF2 = b.sb("TBF2", [128, D], BF16)
    TCOL = b.sb("TCOL", [128, 16])
    TCOL2 = b.sb("TCOL2", [128, 16])
    RT = b.sb("RT", [128, 64])
    RSC = b.sb("RSC", [1, 32])
    ROW128 = b.sb("ROW128", [1, 128])

    AR_BYTES = 160 * 1024
    ARENA = b.sb("ARENA", [128, AR_BYTES // 4], F32)
    ARB = ARENA.bitcast(BF16)

    def arf(off, cols):
        assert off % 4 == 0 and off + cols * 4 <= AR_BYTES, (off, cols)
        return ARENA[:, off // 4: off // 4 + cols]

    def arb(off, cols):
        assert off % 2 == 0 and off + cols * 2 <= AR_BYTES, (off, cols)
        return ARB[:, off // 2: off // 2 + cols]

    KB = 1024
    A_X = 96 * KB
    X = arf(A_X, NT * D)
    BASET = arb(64 * KB, ND * T)
    XNT = arb(32 * KB, ND * T)

    PSF = b.ps("PSF", [128, 4096], F32)
    PSFB = PSF.bitcast(BF16)

    def psf(bank, w=512, off=0):
        return PSF[:, bank * 512 + off: bank * 512 + off + w]

    def psb(bank, w=1024, off=0):
        return PSFB[:, (6 + bank) * 1024 + off: (6 + bank) * 1024 + off + w]

    s_dma = b.sem("s_dma")
    s_dma.isdma = True
    s_dma.tag = "sy"
    s_gdma = b.sem("s_gdma")
    s_gdma.isdma = True
    s_gdma.tag = "gp"
    s_te = b.sem("s_te")
    s_ve = b.sem("s_ve")
    s_se = b.sem("s_se")
    s_gp = b.sem("s_gp")
    s_out = b.sem("s_out")

    def setup_constants():
        p0 = b.op("gpsimd", lambda e: e.memset(IDB[:], 0.0), inc=[(s_gp, 1)])
        p0 = b.op("gpsimd", lambda e: e.affine_select(
            out=IDB[:], in_=IDB[:], compare_op=ALU.not_equal, fill=1.0, base=0,
            pattern=[[-1, 128]], channel_multiplier=1),
            wait=[p0[0]], inc=[(s_gp, 1)])
        p0 = b.op("gpsimd", lambda e: e.memset(IDF[:], 0.0),
                  wait=[p0[0]], inc=[(s_gp, 1)])
        p0 = b.op("gpsimd", lambda e: e.affine_select(
            out=IDF[:], in_=IDF[:], compare_op=ALU.not_equal, fill=1.0, base=0,
            pattern=[[-1, 128]], channel_multiplier=1),
            wait=[p0[0]], inc=[(s_gp, 1)])
        p0 = b.op("gpsimd", lambda e: e.iota(
            IOTA[:], pattern=[[128, NT]], base=0, channel_multiplier=1,
            allow_small_or_imprecise_dtypes=True), wait=[p0[0]], inc=[(s_gp, 1)])
        p0 = b.op("gpsimd", lambda e: e.memset(ONES[:], 1.0),
                  wait=[p0[0]], inc=[(s_gp, 1)])
        pgp = b.op("gpsimd", lambda e: e.memset(ONESB[:], 1.0),
                   wait=[p0[0]], inc=[(s_gp, 1)])
        b.wait("tensor", [pgp[0]])
        b.wait("vector", [pgp[0]])
        b.wait("scalar", [pgp[0]])

        t1 = b.op("sync", lambda e: e.dma_start(
            out=XC[0:1, 0:D], in_=g["L0_attn_out_b"][None, :]), inc=[(s_dma, 16)])
        tv1 = b.op("vector", lambda e: e.tensor_copy(ROWSB[:, 0:D], XC[0:1, 0:D]),
                   wait=[t1[0]], inc=[(s_ve, 1)])
        t2 = b.op("sync", lambda e: e.dma_start(
            out=XC[0:1, 0:D], in_=g["L1_attn_bo"][None, :]),
            wait=[tv1[0]], inc=[(s_dma, 16)])
        tv2 = b.op("vector", lambda e: e.tensor_copy(ROWSB[:, D:2 * D], XC[0:1, 0:D]),
                   wait=[t2[0]], inc=[(s_ve, 1)])
        b.op("sync", lambda e: e.dma_start(
            out=RBROW[:], in_=g["L1_ffn_router_b"][None, :]), inc=[(s_dma, 16)])
        b.op("gpsimd", lambda e: e.dma_start(
            out=BVROW[:], in_=g["L1_attn_bv"][None, :]), inc=[(s_gdma, 16)])

        def chan_cols(name, dest, scale=1.0, prev=None):
            tl = b.op("sync", lambda e, name=name: e.dma_start(
                out=XN[0:8, 0:128],
                in_=g[name].rearrange("(a p) -> a p", p=128)),
                wait=[prev] if prev else [tv2[0]], inc=[(s_dma, 16)])
            tt = b.op("tensor", lambda e: e.transpose(
                psf(0, 8), XN[0:8, 0:128], IDF[0:8, 0:8]),
                wait=[tl[0]], inc=[(s_te, 1)])
            te_ = b.op("scalar", lambda e, dest=dest, scale=scale: e.activation(
                dest[:, 0:8], psf(0, 8), AF.Copy, scale=scale),
                wait=[tt[0]], inc=[(s_se, 1)])
            b.wait("tensor", [te_[0]])
            return te_[0]

        p = None
        p = chan_cols("L0_attn_norm_w", NWC, prev=p)
        p = chan_cols("L0_ffn_gamma_light", GLC, prev=p)
        p = chan_cols("L0_ffn_gamma_heavy", GHC, prev=p)
        p = chan_cols("L0_attn_A_b", ABC, prev=p)
        p = chan_cols("L1_attn_bq", BQC, scale=0.125, prev=p)

        tk = b.op("sync", lambda e: e.dma_start(
            out=XN[0:1, 0:64], in_=g["L1_attn_bk"][None, :]),
            wait=[p], inc=[(s_dma, 16)])
        tkt = b.op("tensor", lambda e: e.transpose(
            psf(0, 1)[0:64, :], XN[0:1, 0:64], IDF[0:1, 0:1]),
            wait=[tk[0]], inc=[(s_te, 1)])
        tke = b.op("scalar", lambda e: e.copy(BKC[:], psf(0, 1)[0:64, 0:1]),
                   wait=[tkt[0]], inc=[(s_se, 1)])
        b.wait("tensor", [tke[0]])

        tr = b.op("sync", lambda e: e.dma_start(
            out=XN[0:1, 0:D], in_=g["L0_ffn_q_routing_token"][None, :]),
            wait=[tke[0]], inc=[(s_dma, 16)])
        prev_ev = None
        for dt in range(ND):
            bank = dt % 2
            tm = b.op("tensor", lambda e, dt=dt, bank=bank: e.matmul(
                psf(bank, 128), ONES[:], XN[0:1, dt * 128:(dt + 1) * 128],
                start=True, stop=True),
                wait=[tr[0]] + ([prev_ev] if prev_ev else []), inc=[(s_te, 1)])
            tev = b.op("scalar", lambda e, dt=dt, bank=bank: e.copy(
                RB[:, dt * 128:(dt + 1) * 128], psf(bank, 128)),
                wait=[tm[0]], inc=[(s_se, 1)])
            prev_ev = tev[0]
        b.wait("tensor", [prev_ev])

        b.op("sync", lambda e: e.dma_start(
            out=RWF.rearrange("p (a c) -> p a c", a=ND),
            in_=g["L1_ffn_router_w"].rearrange("(a p) c -> p a c", p=128)),
            wait=[prev_ev], inc=[(s_dma, 16)])
        b.barrier()

    def part_reduce(src_col, dst_col, alu, extra_wait=()):
        tt = b.op("tensor", lambda e, src_col=src_col: e.transpose(
            psf(5, 128)[0:1, :], src_col, IDF[:]),
            wait=list(extra_wait), inc=[(s_te, 1)])
        te_ = b.op("scalar", lambda e: e.copy(ROW128[:], psf(5, 128)[0:1, :]),
                   wait=[tt[0]], inc=[(s_se, 1)])
        b.wait("tensor", [te_[0]])
        tr_ = b.op("vector", lambda e, dst_col=dst_col, alu=alu: e.tensor_reduce(
            RSC[0:1, dst_col:dst_col + 1], ROW128[:], axis=AX.X, op=alu),
            wait=[te_[0]], inc=[(s_ve, 1)])
        return tr_[0]

    def bcast_cols(src_row_off, n, dst, extra_wait=()):
        tm = b.op("tensor", lambda e, src_row_off=src_row_off, n=n: e.matmul(
            psf(5, n), ONES[:], RSC[0:1, src_row_off:src_row_off + n],
            start=True, stop=True), wait=list(extra_wait), inc=[(s_te, 1)])
        te_ = b.op("scalar", lambda e, dst=dst, n=n: e.copy(dst, psf(5, n)),
                   wait=[tm[0]], inc=[(s_se, 1)])
        b.wait("tensor", [te_[0]])
        return te_[0]

    def stage_ln(bi, which):
        first = (which == 0)
        tap_xn = tap("xn%d_b%d" % (which, bi), (T, D))
        for t in range(NT):
            if first:
                tl = b.op("sync", lambda e, t=t: e.dma_start(
                    out=bcol(X, t, D), in_=hbm_tiles(g["x"][bi])[t]),
                    inc=[(s_dma, 16)])
                b.wait("vector", [tl[0]])
            xt = bcol(X, t, D)
            tm = b.op("vector", lambda e, xt=xt: e.reduce_sum(
                TCOL[:, 0:1], xt, axis=AX.X), inc=[(s_ve, 1)])
            tnm = b.op("vector", lambda e: e.tensor_scalar(
                TCOL[:, 1:2], TCOL[:, 0:1], -1.0 / D, None, ALU.mult),
                wait=[tm[0]], inc=[(s_ve, 1)])
            txc = b.op("scalar", lambda e, xt=xt: e.activation(
                XC[:], xt, AF.Identity, bias=TCOL[:, 1:2]),
                wait=[tnm[0]], inc=[(s_se, 1)])
            tsq = b.op("scalar", lambda e: e.activation(
                XN[:], XC[:], AF.Square, accum_out=TCOL[:, 2:3]),
                wait=[txc[0]], inc=[(s_se, 1)])
            tv = b.op("vector", lambda e: e.tensor_scalar(
                TCOL[:, 3:4], TCOL[:, 2:3], 1.0 / D, 1e-5, ALU.mult, ALU.add),
                wait=[tsq[0]], inc=[(s_ve, 1)])
            tln = b.op("scalar", lambda e: e.activation(
                TCOL[:, 4:5], TCOL[:, 3:4], AF.Ln), wait=[tv[0]], inc=[(s_se, 1)])
            trs = b.op("scalar", lambda e: e.activation(
                TCOL[:, 5:6], TCOL[:, 4:5], AF.Exp, scale=-0.5), inc=[(s_se, 1)])
            txn = b.op("scalar", lambda e: e.activation(
                XN[:], XC[:], AF.Identity, scale=TCOL[:, 5:6]), inc=[(s_se, 1)])
            if tap_xn is not None:
                b.op("sync", lambda e, t=t, tap_xn=tap_xn: e.dma_start(
                    out=hbm_tiles(tap_xn)[t], in_=XN[:]),
                    wait=[txn[0]], inc=[(s_out, 16)])

            if first:
                tsd0 = b.op("vector", lambda e: e.tensor_mul(
                    XC[:], XN[:], RB[:]), wait=[txn[0]], inc=[(s_ve, 1)])
                tsdot = b.op("vector", lambda e, t=t: e.reduce_sum(
                    bcol(S128, t), XC[:], axis=AX.X), inc=[(s_ve, 1)])
                tl2 = b.op("scalar", lambda e: e.activation(
                    XC[:], XN[:], AF.Square, accum_out=TCOL[:, 6:7]),
                    wait=[tsdot[0]], inc=[(s_se, 1)])
                tv2 = b.op("vector", lambda e: e.tensor_scalar(
                    TCOL[:, 7:8], TCOL[:, 6:7], 1.0 / D, 1e-5, ALU.mult, ALU.add),
                    wait=[tl2[0]], inc=[(s_ve, 1)])
                tln2 = b.op("scalar", lambda e: e.activation(
                    TCOL2[:, 0:1], TCOL[:, 7:8], AF.Ln), wait=[tv2[0]],
                    inc=[(s_se, 1)])
                trn = b.op("scalar", lambda e, t=t: e.activation(
                    bcol(RNC, t), TCOL2[:, 0:1], AF.Exp, scale=-0.5),
                    inc=[(s_se, 1)])
                tmx = b.op("vector", lambda e: e.tensor_scalar(
                    TCOL2[:, 3:4], TCOL[:, 6:7], 1e-24, None, ALU.max),
                    wait=[tl2[0]], inc=[(s_ve, 1)])
                tsl = b.op("scalar", lambda e: e.activation(
                    TCOL2[:, 1:2], TCOL2[:, 3:4], AF.Ln, scale=1.0 / 1024.0),
                    wait=[tmx[0]], inc=[(s_se, 1)])
                tbf = b.op("scalar", lambda e, t=t: e.activation(
                    bcol(BFC, t), TCOL2[:, 1:2], AF.Exp, scale=-0.5),
                    inc=[(s_se, 1)])
                tbase = b.op("scalar", lambda e, t=t: e.activation(
                    TBF1[:], XN[:], AF.Identity, scale=bcol(BFC, t)),
                    inc=[(s_se, 1)])
                txnr = b.op("scalar", lambda e, t=t: e.activation(
                    TBF2[:], XN[:], AF.Identity, scale=bcol(RNC, t)),
                    wait=[trn[0]], inc=[(s_se, 1)])
                b.op("vector", lambda e, xt=xt: e.tensor_add(xt, xt, XN[:]),
                     wait=[txn[0]], inc=[(s_ve, 1)])
                srcdst = [(TBF1, BASET, tbase, 0), (TBF2, XNT, txnr, 1)]
            else:
                tcast = b.op("scalar", lambda e: e.copy(TBF2[:], XN[:]),
                             inc=[(s_se, 1)])
                srcdst = [(TBF2, XNT, tcast, 1)]

            for src, dst, tok, bank in srcdst:
                prev = None
                for dt in range(ND):
                    tt = b.op("tensor", lambda e, src=src, dt=dt, bank=bank:
                              e.transpose(psb(bank, 128),
                                          src[:, dt * 128:(dt + 1) * 128], IDB[:]),
                              wait=[tok[0]] + ([prev] if prev else []),
                              inc=[(s_te, 1)])
                    tev = b.op("scalar", lambda e, dst=dst, dt=dt, t=t, bank=bank:
                               e.copy(dst[:, dt * T + t * 128: dt * T + (t + 1) * 128],
                                      psb(bank, 128)),
                               wait=[tt[0]], inc=[(s_se, 1)])
                    prev = tev[0]
                b.wait("tensor", [prev])

            if not first:
                prevt = None
                for dt in range(ND):
                    fb = 2 + (dt % 2)
                    tt3 = b.op("tensor", lambda e, dt=dt, fb=fb: e.transpose(
                        psf(fb, 128), XN[:, dt * 128:(dt + 1) * 128], IDF[:]),
                        wait=[txn[0]] + ([prevt] if prevt else []),
                        inc=[(s_te, 1)])
                    tev3 = b.op("scalar", lambda e, dt=dt, fb=fb: e.copy(
                        XC[:, dt * 128:(dt + 1) * 128], psf(fb, 128)),
                        wait=[tt3[0]], inc=[(s_se, 1)])
                    prevt = tev3[0]
                lg = psf(4, NEXP)
                for dt in range(ND):
                    b.op("tensor", lambda e, dt=dt, lg=lg: e.matmul(
                        lg, XC[:, dt * 128:(dt + 1) * 128],
                        RWF.rearrange("p (a c) -> p a c", a=ND)[:, dt],
                        start=(dt == 0), stop=False, skip_group_check=True),
                        wait=[prevt] if dt == 0 else [], inc=[(s_te, 1)])
                tbias = b.op("tensor", lambda e, lg=lg: e.matmul(
                    lg, ONES[:], RBROW[:], start=False, stop=True,
                    skip_group_check=True), inc=[(s_te, 1)])
                tnm2 = b.op("vector", lambda e, lg=lg: e.reduce_max(
                    TCOL2[:, 4:5], lg, axis=AX.X, negate=True),
                    wait=[tbias[0]], inc=[(s_ve, 1)])
                tz = b.op("scalar", lambda e, lg=lg: e.activation(
                    TCOL2[:, 8:16], lg, AF.Exp, bias=TCOL2[:, 4:5],
                    accum_out=TCOL2[:, 5:6]),
                    wait=[tnm2[0]], inc=[(s_se, 1)])
                tlz = b.op("scalar", lambda e: e.activation(
                    TCOL2[:, 6:7], TCOL2[:, 5:6], AF.Ln),
                    wait=[tz[0]], inc=[(s_se, 1)])
                tmp_ = b.op("scalar", lambda e: e.activation(
                    TCOL2[:, 7:8], TCOL2[:, 6:7], AF.Exp, scale=-1.0),
                    inc=[(s_se, 1)])
                pv = b.op("vector", lambda e, lg=lg: e.tensor_scalar(
                    TCOL[:, 0:8], lg, TCOL2[:, 4:5], 0.0, ALU.add, ALU.is_equal),
                    wait=[tz[0]], inc=[(s_ve, 1)])
                b.wait("tensor", [pv[0]])
                pv = b.op("vector", lambda e: e.memset(TCOL[:, 8:9], 0.0),
                          wait=[pv[0]], inc=[(s_ve, 1)])
                for j in range(1, NEXP):
                    pv = b.op("vector", lambda e, j=j: e.tensor_add(
                        TCOL[:, 8 + j:9 + j], TCOL[:, 7 + j:8 + j],
                        TCOL[:, j - 1:j]), wait=[pv[0]], inc=[(s_ve, 1)])
                pv = b.op("vector", lambda e: e.tensor_scalar(
                    TCOL2[:, 8:16], TCOL[:, 8:16], 0.0, None, ALU.is_equal),
                    wait=[pv[0]], inc=[(s_ve, 1)])
                pv = b.op("vector", lambda e: e.tensor_mul(
                    TCOL[:, 8:16], TCOL2[:, 8:16], TCOL[:, 0:8]),
                    wait=[pv[0]], inc=[(s_ve, 1)])
                b.op("vector", lambda e, t=t: e.tensor_scalar(
                    SELC[:, t * NEXP:(t + 1) * NEXP], TCOL[:, 8:16],
                    TCOL2[:, 7:8], None, ALU.mult),
                    wait=[tmp_[0], pv[0]], inc=[(s_ve, 1)])
        b.barrier()

        nm = "baset_b%d" % bi
        if first and tap(nm, (128, ND * T), BF16) is not None:
            b.op("sync", lambda e: e.dma_start(out=tap_handles[nm], in_=BASET[:]),
                 inc=[(s_out, 16)])
        nm2 = "xnt%d_b%d" % (which, bi)
        if tap(nm2, (128, ND * T), BF16) is not None:
            b.op("sync", lambda e: e.dma_start(out=tap_handles[nm2], in_=XNT[:]),
                 inc=[(s_out, 16)])
        if first and tap("s128_b%d" % bi, (128, NT)) is not None:
            b.op("sync", lambda e, bi=bi: e.dma_start(
                out=tap_handles["s128_b%d" % bi], in_=S128[:]), inc=[(s_out, 16)])
        if not first and tap("selc_b%d" % bi, (128, NT * NEXP)) is not None:
            b.op("sync", lambda e, bi=bi: e.dma_start(
                out=tap_handles["selc_b%d" % bi], in_=SELC[:]), inc=[(s_out, 16)])

    def offload_X(bi, dst=None):
        dst = scratch_x1 if dst is None else dst
        toks = []
        for t in range(NT):
            tk = b.op("sync", lambda e, t=t, dst=dst: e.dma_start(
                out=hbm_tiles(dst[bi])[t], in_=bcol(X, t, D)),
                inc=[(s_dma, 16)])
            toks.append(tk[0])
        return toks

    HT = arb(96 * KB, ND * T)
    OWT = arb(128 * KB, ND * 1024)
    SA = arf(144 * KB, T)
    SU = arf(152 * KB, T)
    SH = arf(0 * KB, T)
    SCc = arf(8 * KB, T)
    SD = arf(16 * KB, 512)
    WROW = arf(18 * KB, 2048)
    WT4 = arb(128 * KB, 4 * ND * 128)

    def wt(m, dt):
        return WT4[:, (m * ND + dt) * 128:(m * ND + dt + 1) * 128]

    def stage_ssm(bi):
        mats = ["L0_attn_A_w", "L0_attn_B_w", "L0_attn_C_w", "L0_attn_D_w"]
        last_wrow_consumer = None
        evs = []
        for nt in range(ND):
            tl1 = b.op("sync", lambda e, nt=nt: e.dma_start(
                out=WROW[:, 0:1024],
                in_=g[mats[0]][nt * 128:(nt + 1) * 128, :]),
                wait=[last_wrow_consumer] if last_wrow_consumer else [],
                inc=[(s_dma, 16)])
            tl2 = b.op("sync", lambda e, nt=nt: e.dma_start(
                out=WROW[:, 1024:2048],
                in_=g[mats[1]][nt * 128:(nt + 1) * 128, :]), inc=[(s_dma, 16)])
            trans_done = []
            for m in range(4):
                if m == 2:
                    tl1 = b.op("sync", lambda e, nt=nt: e.dma_start(
                        out=WROW[:, 0:1024],
                        in_=g[mats[2]][nt * 128:(nt + 1) * 128, :]),
                        wait=trans_done, inc=[(s_dma, 16)])
                    tl2 = b.op("sync", lambda e, nt=nt: e.dma_start(
                        out=WROW[:, 1024:2048],
                        in_=g[mats[3]][nt * 128:(nt + 1) * 128, :]),
                        inc=[(s_dma, 16)])
                    trans_done = []
                src_off = (m % 2) * 1024
                dtok = tl1 if m % 2 == 0 else tl2
                prev = None
                for dt in range(ND):
                    fb = 4 + (dt % 2)
                    tt = b.op("tensor", lambda e, src_off=src_off, dt=dt, fb=fb:
                              e.transpose(psf(fb, 128),
                                          WROW[:, src_off + dt * 128:
                                               src_off + (dt + 1) * 128], IDF[:]),
                              wait=[dtok[0]] + ([prev] if prev else []),
                              inc=[(s_te, 1)])
                    tev = b.op("scalar", lambda e, m=m, dt=dt, fb=fb: e.activation(
                        wt(m, dt), psf(fb, 128), AF.Identity,
                        scale=bcol(NWC, dt)), wait=[tt[0]], inc=[(s_se, 1)])
                    prev = tev[0]
                trans_done.append(prev)
                b.wait("tensor", [prev])
            last_wrow_consumer = trans_done[-1]

            for tc in range(4):
                gstop = []
                for m in range(4):
                    for dt in range(ND):
                        tmm = b.op(
                            "tensor",
                            lambda e, m=m, dt=dt, tc=tc: e.matmul(
                                psf(m, 512), wt(m, dt),
                                XNT[:, dt * T + tc * 512: dt * T + (tc + 1) * 512],
                                start=(dt == 0), stop=(dt == ND - 1),
                                skip_group_check=True),
                            wait=(evs if dt == 0 else []),
                            inc=[(s_te, 1)])
                    gstop.append(tmm)
                ta_ = b.op("scalar", lambda e, tc=tc, nt=nt: e.activation(
                    SA[:, tc * 512:(tc + 1) * 512], psf(0, 512), AF.Sigmoid,
                    bias=bcol(ABC, nt)), wait=[gstop[0][0]], inc=[(s_se, 1)])
                tc_ = b.op("scalar", lambda e, tc=tc: e.activation(
                    SCc[:, tc * 512:(tc + 1) * 512], psf(2, 512), AF.Sigmoid),
                    wait=[gstop[2][0]], inc=[(s_se, 1)])
                td_ = b.op("scalar", lambda e, tc=tc: e.activation(
                    SD[:], psf(3, 512), AF.Sigmoid),
                    wait=[gstop[3][0]], inc=[(s_se, 1)])
                tu_ = b.op("vector", lambda e, tc=tc: e.tensor_mul(
                    SU[:, tc * 512:(tc + 1) * 512], SD[:], psf(1, 512)),
                    wait=[td_[0], gstop[1][0]], inc=[(s_ve, 1)])
                evs = [ta_[0], tc_[0], td_[0], tu_[0]]
            tsc = b.op("vector", lambda e: e.tensor_tensor_scan(
                SH[:], SA[:], SU[:], 0.0, ALU.mult, ALU.add),
                wait=evs, inc=[(s_ve, 1)])
            thc = b.op("vector", lambda e, nt=nt: e.tensor_mul(
                arb(96 * KB + nt * 2 * T, T), SCc[:], SH[:]),
                wait=[tsc[0]], inc=[(s_ve, 1)])
            evs = evs + [tsc[0], thc[0]]

        if tap("ht_b%d" % bi, (128, ND * T), BF16) is not None:
            b.barrier()
            b.op("sync", lambda e, bi=bi: e.dma_start(
                out=tap_handles["ht_b%d" % bi], in_=HT), inc=[(s_out, 16)])
        if tap("sascu_b%d" % bi, (128, 4 * T)) is not None:
            b.barrier()
            th_ = tap_handles["sascu_b%d" % bi]
            b.op("sync", lambda e, th_=th_: e.dma_start(out=th_[:, 0:T], in_=SA),
                 inc=[(s_out, 16)])
            b.op("sync", lambda e, th_=th_: e.dma_start(out=th_[:, T:2 * T], in_=SU),
                 inc=[(s_out, 16)])
            b.op("sync", lambda e, th_=th_: e.dma_start(out=th_[:, 2 * T:3 * T], in_=SH),
                 inc=[(s_out, 16)])
            b.op("sync", lambda e, th_=th_: e.dma_start(out=th_[:, 3 * T:4 * T], in_=SCc),
                 inc=[(s_out, 16)])

        prev = last_wrow_consumer
        first_extra = list(evs)
        for dtt in range(ND):
            tl = b.op("sync", lambda e, dtt=dtt: e.dma_start(
                out=WROW[:, 0:1024],
                in_=g["L0_attn_out_w"][dtt * 128:(dtt + 1) * 128, :]),
                wait=([prev] if prev else []) + (first_extra if dtt == 0 else []),
                inc=[(s_dma, 16)])
            first_extra = []
            for ntt in range(ND):
                fb = 4 + (ntt % 2)
                tt = b.op("tensor", lambda e, ntt=ntt, fb=fb: e.transpose(
                    psf(fb, 128), WROW[:, ntt * 128:(ntt + 1) * 128], IDF[:]),
                    wait=[tl[0]] + ([prev] if prev else []), inc=[(s_te, 1)])
                tev = b.op("scalar", lambda e, ntt=ntt, dtt=dtt, fb=fb: e.copy(
                    OWT[:, ntt * 1024 + dtt * 128: ntt * 1024 + (dtt + 1) * 128],
                    psf(fb, 128)), wait=[tt[0]], inc=[(s_se, 1)])
                prev = tev[0]
            b.wait("tensor", [prev])
        b.barrier()

    def stage_yadd(bi):
        XSTG = [arf(0 * KB, D), arf(4 * KB, D)]
        slot_free = [None, None]
        prev_add = {}
        for t in range(NT):
            sl = t % 2
            tk = b.op("sync", lambda e, t=t, sl=sl: e.dma_start(
                out=XSTG[sl][:], in_=hbm_tiles(scratch_x1[bi])[t]),
                wait=[slot_free[sl]] if slot_free[sl] else [],
                inc=[(s_dma, 16)])
            adds = []
            for dc in range(2):
                bank = (t * 2 + dc) % 4
                for ntt in range(ND):
                    b.op("tensor", lambda e, t=t, dc=dc, ntt=ntt, bank=bank:
                         e.matmul(psf(bank, 512),
                                  HT[:, ntt * T + t * 128: ntt * T + (t + 1) * 128],
                                  OWT[:, ntt * 1024 + dc * 512:
                                      ntt * 1024 + (dc + 1) * 512],
                                  start=(ntt == 0), stop=False,
                                  skip_group_check=True),
                         wait=([prev_add[bank]] if ntt == 0 and bank in prev_add
                               else []),
                         inc=[(s_te, 1)])
                tb_ = b.op("tensor", lambda e, dc=dc, bank=bank: e.matmul(
                    psf(bank, 512), ONESB[:],
                    ROWSB[0:1, dc * 512:(dc + 1) * 512],
                    start=False, stop=True, skip_group_check=True),
                    inc=[(s_te, 1)])
                tadd = b.op("vector", lambda e, sl=sl, dc=dc, bank=bank:
                            e.tensor_add(
                                XSTG[sl][:, dc * 512:(dc + 1) * 512],
                                XSTG[sl][:, dc * 512:(dc + 1) * 512],
                                psf(bank, 512)),
                            wait=[tb_[0], tk[0]], inc=[(s_ve, 1)])
                prev_add[bank] = tadd[0]
                adds.append(tadd[0])
            tw = b.op("sync", lambda e, t=t, sl=sl: e.dma_start(
                out=hbm_tiles(scratch_x1[bi])[t], in_=XSTG[sl][:]),
                wait=adds, inc=[(s_dma, 16)])
            slot_free[sl] = tw[0]
        b.barrier()

    def routing_steps(bi):
        steps = []
        VV = RT[:, 0:NT]
        TMP = RT[:, 16:32]
        BC1 = RT[:, 32:33]
        BC2 = RT[:, 33:34]
        PC = RT[:, 34:35]
        EQM = RT[:, 35:51]
        ch = {"t": None}
        SEMS = {"vector": s_ve, "scalar": s_se, "tensor": s_te, "sync": s_out}

        def rop(eng, fn, extra=()):
            tok = b.op(eng, fn,
                       wait=([ch["t"]] if ch["t"] else []) + list(extra),
                       inc=[(SEMS[eng], 1)] if eng != "sync" else [(s_out, 16)])
            ch["t"] = tok[0]
            return tok[0]

        def rpart_reduce(src_col, dst_col, alu):
            rop("tensor", lambda e, src_col=src_col: e.transpose(
                psf(5, 128)[0:1, :], src_col, IDF[:]))
            rop("scalar", lambda e: e.copy(ROW128[:], psf(5, 128)[0:1, :]))
            b.wait("tensor", [ch["t"]])
            rop("vector", lambda e, dst_col=dst_col, alu=alu: e.tensor_reduce(
                RSC[0:1, dst_col:dst_col + 1], ROW128[:], axis=AX.X, op=alu))

        def rbcast(src_off, n, dst):
            rop("tensor", lambda e, src_off=src_off, n=n: e.matmul(
                psf(5, n), ONES[:], RSC[0:1, src_off:src_off + n],
                start=True, stop=True))
            rop("scalar", lambda e, dst=dst, n=n: e.copy(dst, psf(5, n)))
            b.wait("tensor", [ch["t"]])

        def s_init():
            rop("vector", lambda e: e.reduce_max(PC, S128[:], axis=AX.X))
            rpart_reduce(PC, 2, ALU.max)
            rop("vector", lambda e: e.memset(RSC[0:1, 0:1], 0.5753641128540039))
        steps.append(s_init)

        def cd_iter(_it):
            rop("vector", lambda e: e.tensor_tensor(
                RSC[0:1, 9:10], RSC[0:1, 2:3], RSC[0:1, 0:1], ALU.min))
            rop("vector", lambda e: e.tensor_scalar(
                RSC[0:1, 1:2], RSC[0:1, 9:10], -1.0, None, ALU.mult))
            rbcast(0, 2, RT[:, 32:34])
            rop("vector", lambda e: e.tensor_scalar(
                TMP, S128[:], BC1, BC2, ALU.min, ALU.add))
            rop("scalar", lambda e: e.activation(
                TMP, TMP, AF.Exp, accum_out=PC))
            rpart_reduce(PC, 3, ALU.add)
            rop("scalar", lambda e: e.activation(
                RSC[0:1, 3:4], RSC[0:1, 3:4], AF.Ln))
            rop("vector", lambda e: e.tensor_tensor(
                RSC[0:1, 9:10], RSC[0:1, 3:4], RSC[0:1, 1:2], ALU.subtract))
            rop("vector", lambda e: e.tensor_scalar(
                RSC[0:1, 0:1], RSC[0:1, 9:10], -LOGK, None, ALU.add))
        for it in range(NITER_CD - 1):
            steps.append(lambda it=it: cd_iter(it))

        def v_init():
            rop("vector", lambda e: e.tensor_scalar(
                RSC[0:1, 9:10], RSC[0:1, 0:1], -1.0, None, ALU.mult))
            rbcast(9, 1, RT[:, 32:33])
            rop("vector", lambda e: e.tensor_scalar(
                VV, S128[:], BC1, 0.0, ALU.add, ALU.min))
            rop("vector", lambda e: e.tensor_reduce(
                PC, VV, axis=AX.X, op=ALU.min))
            rpart_reduce(PC, 9, ALU.min)
            rop("vector", lambda e: e.tensor_scalar(
                RSC[0:1, 5:6], RSC[0:1, 9:10], -1.0, None, ALU.add))
            rop("vector", lambda e: e.memset(RSC[0:1, 6:7], 1.0))
            rop("vector", lambda e: e.memset(RSC[0:1, 12:13], 1024.0))
        steps.append(v_init)

        def count_vv(thr_col, cmp_op, out_col):
            rbcast(thr_col, 1, RT[:, 32:33])
            rop("vector", lambda e, cmp_op=cmp_op: e.tensor_scalar(
                TMP, VV, BC1, None, cmp_op))
            rop("vector", lambda e: e.reduce_sum(PC, TMP, axis=AX.X))
            rpart_reduce(PC, out_col, ALU.add)

        def count_ix():
            rbcast(15, 1, RT[:, 32:33])
            rop("vector", lambda e: e.tensor_scalar(
                TMP, IOTA[:], BC1, None, ALU.is_le))
            rop("vector", lambda e: e.tensor_mul(TMP, TMP, EQM))
            rop("vector", lambda e: e.reduce_sum(PC, TMP, axis=AX.X))
            rpart_reduce(PC, 4, ALU.add)

        def bis_iter(locol, hicol, midcol, cntfn, invert):
            rop("vector", lambda e: e.tensor_tensor(
                RSC[0:1, midcol:midcol + 1], RSC[0:1, locol:locol + 1],
                RSC[0:1, hicol:hicol + 1], ALU.add))
            rop("vector", lambda e: e.tensor_scalar(
                RSC[0:1, midcol:midcol + 1], RSC[0:1, midcol:midcol + 1],
                0.5, None, ALU.mult))
            cntfn()
            cmp_op = ALU.is_lt if invert else ALU.is_ge
            rop("vector", lambda e, cmp_op=cmp_op: e.tensor_tensor(
                RSC[0:1, 8:9], RSC[0:1, 4:5], RSC[0:1, 12:13], cmp_op))
            rop("vector", lambda e: e.tensor_tensor(
                RSC[0:1, 9:10], RSC[0:1, midcol:midcol + 1],
                RSC[0:1, locol:locol + 1], ALU.subtract))
            rop("vector", lambda e: e.tensor_mul(
                RSC[0:1, 9:10], RSC[0:1, 9:10], RSC[0:1, 8:9]))
            rop("vector", lambda e: e.tensor_add(
                RSC[0:1, locol:locol + 1], RSC[0:1, locol:locol + 1],
                RSC[0:1, 9:10]))
            rop("vector", lambda e: e.tensor_scalar(
                RSC[0:1, 8:9], RSC[0:1, 8:9], -1.0, 1.0, ALU.mult, ALU.add))
            rop("vector", lambda e: e.tensor_tensor(
                RSC[0:1, 9:10], RSC[0:1, midcol:midcol + 1],
                RSC[0:1, hicol:hicol + 1], ALU.subtract))
            rop("vector", lambda e: e.tensor_mul(
                RSC[0:1, 9:10], RSC[0:1, 9:10], RSC[0:1, 8:9]))
            rop("vector", lambda e: e.tensor_add(
                RSC[0:1, hicol:hicol + 1], RSC[0:1, hicol:hicol + 1],
                RSC[0:1, 9:10]))

        for it in range(NITER_TH):
            steps.append(lambda it=it: bis_iter(
                5, 6, 7, lambda: count_vv(7, ALU.is_gt, 4), False))

        def theta_final():
            rbcast(6, 1, RT[:, 32:33])
            rop("vector", lambda e: e.tensor_scalar(
                TMP, VV, BC1, None, ALU.is_le))
            rop("vector", lambda e: e.tensor_scalar(
                EQM, TMP, -1.0, 1.0, ALU.mult, ALU.add))
            rop("vector", lambda e: e.tensor_scalar(
                EQM, EQM, -1e30, None, ALU.mult))
            rop("vector", lambda e: e.tensor_mul(TMP, TMP, VV))
            rop("vector", lambda e: e.tensor_add(TMP, TMP, EQM))
            rop("vector", lambda e: e.reduce_max(PC, TMP, axis=AX.X))
            rpart_reduce(PC, 10, ALU.max)
            count_vv(10, ALU.is_gt, 11)
            rop("vector", lambda e: e.tensor_scalar(
                RSC[0:1, 12:13], RSC[0:1, 11:12], -1.0, 1024.0, ALU.mult,
                ALU.add))
            rop("vector", lambda e: e.tensor_scalar(
                EQM, VV, BC1, None, ALU.is_equal))
            rop("vector", lambda e: e.memset(RSC[0:1, 13:14], -1.0))
            rop("vector", lambda e: e.memset(RSC[0:1, 14:15], 2047.0))
        steps.append(theta_final)

        for it in range(NITER_IX):
            steps.append(lambda it=it: bis_iter(13, 14, 15, count_ix, True))

        def mask_final():
            rbcast(10, 1, RT[:, 32:33])
            rop("vector", lambda e: e.tensor_scalar(
                TMP, VV, BC1, None, ALU.is_gt))
            rbcast(14, 1, RT[:, 33:34])
            rop("vector", lambda e: e.tensor_scalar(
                MASKC[:], IOTA[:], BC2, None, ALU.is_le))
            rop("vector", lambda e: e.tensor_mul(MASKC[:], MASKC[:], EQM))
            rop("vector", lambda e: e.tensor_add(MASKC[:], MASKC[:], TMP))
            if tap("maskc_b%d" % bi, (128, NT)) is not None:
                rop("sync", lambda e, bi=bi: e.dma_start(
                    out=tap_handles["maskc_b%d" % bi], in_=MASKC[:]))
            if tap("vsel_b%d" % bi, (128, NT + 2)) is not None:
                th_ = tap_handles["vsel_b%d" % bi]
                rop("sync", lambda e, th_=th_: e.dma_start(
                    out=th_[:, 0:NT], in_=VV))
                rop("sync", lambda e, th_=th_: e.dma_start(
                    out=th_[0:1, NT:NT + 2], in_=RSC[0:1, 10:12]))
        steps.append(mask_final)
        return steps

    WQL = arb(0 * KB, ND * 512)
    WKVL = arb(8 * KB, ND * 1024)
    WOL = arb(24 * KB, 4 * 1024)
    OT = arb(32 * KB, 4 * T)
    QTW = arb(48 * KB, 512)
    KTW = arb(49 * KB, 512)
    VW = arb(50 * KB, 512)
    EW = arb(51 * KB, 128)
    ETW = arb(52 * KB, 128)
    OLW = arb(53 * KB, 512)
    ZL8 = arf(54 * KB, 8)
    RZ8 = arf(55 * KB, 8)
    WSTG = arf(56 * KB, 1024)

    def load_wcast(dst, src_ap, cols, scale_col=None, wait=(), tag=0):
        tl = b.op("sync", lambda e, src_ap=src_ap, cols=cols: e.dma_start(
            out=WSTG[:, 0:cols], in_=src_ap), wait=list(wait), inc=[(s_dma, 16)])
        if scale_col is None:
            tc = b.op("scalar", lambda e, dst=dst, cols=cols: e.copy(
                dst, WSTG[:, 0:cols]), wait=[tl[0]], inc=[(s_se, 1)])
        else:
            tc = b.op("scalar", lambda e, dst=dst, cols=cols,
                      scale_col=scale_col: e.activation(
                dst, WSTG[:, 0:cols], AF.Identity, scale=scale_col),
                wait=[tl[0]], inc=[(s_se, 1)])
        return tc[0]

    def stage_light(bi, rsteps):
        xload = []
        for t in range(NT):
            tk = b.op("sync", lambda e, t=t: e.dma_start(
                out=bcol(X, t, D), in_=hbm_tiles(scratch_x1[bi])[t]),
                inc=[(s_dma, 16)])
            xload.append(tk[0])
        b.xload_light = xload
        wtoks = []
        prev = None
        for dt in range(ND):
            prev = load_wcast(bcol(WQL, dt, 512),
                              g["L0_ffn_wq_light"][dt * 128:(dt + 1) * 128, :],
                              512, scale_col=bcol(GLC, dt),
                              wait=[prev] if prev else [])
            wtoks.append(prev)
        for dt in range(ND):
            prev = load_wcast(bcol(WKVL, dt, 1024),
                              g["L0_ffn_wkv_light"][dt * 128:(dt + 1) * 128, :],
                              1024, scale_col=bcol(GLC, dt), wait=[prev])
            wtoks.append(prev)
        for ht in range(4):
            prev = load_wcast(bcol(WOL, ht, 1024),
                              g["L0_ffn_wo_light"][ht * 128:(ht + 1) * 128, :],
                              1024, wait=[prev])
            wtoks.append(prev)
        wq_ready = wtoks[ND - 1]
        wkv_ready = wtoks[2 * ND - 1]
        wo_ready = wtoks[-1]

        rs_i = 0
        prev_otev = None
        for t in range(NT):
            n_per = (len(rsteps) + NT - 1) // NT
            for _ in range(n_per):
                if rs_i < len(rsteps):
                    rsteps[rs_i]()
                    rs_i += 1
            evq = []
            for mj in range(4):
                bank = mj % 2
                for dt in range(ND):
                    tmm = b.op("tensor", lambda e, mj=mj, dt=dt, t=t, bank=bank:
                               e.matmul(psf(bank, 128),
                                        WQL[:, dt * 512 + mj * 128:
                                            dt * 512 + (mj + 1) * 128],
                                        BASET[:, dt * T + t * 128:
                                              dt * T + (t + 1) * 128],
                                        start=(dt == 0), stop=(dt == ND - 1),
                                        skip_group_check=True),
                               wait=([wq_ready] if dt == 0 and t == 0 and mj < 2
                                     else ([evq[mj - 2]] if dt == 0 and mj >= 2
                                           else ([prev_otev] if dt == 0 else []))),
                               inc=[(s_te, 1)])
                tev = b.op("scalar", lambda e, mj=mj, bank=bank: e.activation(
                    bcol(QTW, mj, 128), psf(bank, 128), AF.Copy, scale=0.125),
                    wait=[tmm[0]], inc=[(s_se, 1)])
                evq.append(tev[0])
            evk = []
            for mj in range(4):
                bank = 2 + mj % 2
                for dt in range(ND):
                    tmm = b.op("tensor", lambda e, mj=mj, dt=dt, t=t, bank=bank:
                               e.matmul(psf(bank, 128),
                                        WKVL[:, dt * 1024 + mj * 128:
                                            dt * 1024 + (mj + 1) * 128],
                                        BASET[:, dt * T + t * 128:
                                              dt * T + (t + 1) * 128],
                                        start=(dt == 0), stop=(dt == ND - 1),
                                        skip_group_check=True),
                               wait=([wkv_ready] if dt == 0 and t == 0 and mj < 2
                                     else ([evk[mj - 2]] if dt == 0 and mj >= 2
                                           else [])),
                               inc=[(s_te, 1)])
                tev = b.op("scalar", lambda e, mj=mj, bank=bank: e.copy(
                    bcol(KTW, mj, 128), psf(bank, 128)),
                    wait=[tmm[0]], inc=[(s_se, 1)])
                evk.append(tev[0])
            for dt in range(ND):
                tmm = b.op("tensor", lambda e, dt=dt, t=t: e.matmul(
                    psf(4, 512),
                    BASET[:, dt * T + t * 128: dt * T + (t + 1) * 128],
                    WKVL[:, dt * 1024 + 512: dt * 1024 + 1024],
                    start=(dt == 0), stop=(dt == ND - 1),
                    skip_group_check=True),
                    wait=([evk[-1]] if dt == 0 and t == 0 else []),
                    inc=[(s_te, 1)])
            tvev = b.op("scalar", lambda e: e.copy(VW[:], psf(4, 512)),
                        wait=[tmm[0]], inc=[(s_se, 1)])
            b.wait("tensor", [tvev[0]])

            zevs = []
            oev = None
            for h in range(8):
                mj = h // 2
                pb_ = (h % 2) * 64
                tsc_ = b.op("tensor", lambda e, mj=mj, pb_=pb_: e.matmul(
                    psf(2, 128),
                    QTW[pb_:pb_ + 64, mj * 128:(mj + 1) * 128],
                    KTW[pb_:pb_ + 64, mj * 128:(mj + 1) * 128],
                    start=True, stop=True),
                    wait=[evq[mj], evk[mj]] + ([zevs[-1]] if zevs else []),
                    inc=[(s_te, 1)])
                tnm_ = b.op("vector", lambda e: e.reduce_max(
                    TCOL[:, 8:9], psf(2, 128), axis=AX.X, negate=True),
                    wait=[tsc_[0]], inc=[(s_ve, 1)])
                tex_ = b.op("scalar", lambda e, h=h: e.activation(
                    EW[:], psf(2, 128), AF.Exp, bias=TCOL[:, 8:9],
                    accum_out=bcol(ZL8, h)), wait=[tnm_[0]], inc=[(s_se, 1)])
                ttr_ = b.op("tensor", lambda e: e.transpose(
                    psb(0, 128), EW[:], IDB[:]),
                    wait=[tex_[0]], inc=[(s_te, 1)])
                tet_ = b.op("scalar", lambda e: e.copy(ETW[:], psb(0, 128)),
                            wait=[ttr_[0]], inc=[(s_se, 1)])
                tav_ = b.op("tensor", lambda e, h=h: e.matmul(
                    psf(3, 512)[:, h * 64:(h + 1) * 64], ETW[:],
                    VW[:, h * 64:(h + 1) * 64],
                    start=(h == 0), stop=(h == 7), skip_group_check=True),
                    wait=[tet_[0]] + ([oev] if h == 0 and oev else []),
                    inc=[(s_te, 1)])
                zevs.append(tex_[0])
            tlz = b.op("scalar", lambda e: e.activation(
                RZ8[:, 0:8], ZL8[:, 0:8], AF.Ln), wait=zevs, inc=[(s_se, 1)])
            trz = b.op("scalar", lambda e: e.activation(
                RZ8[:, 0:8], RZ8[:, 0:8], AF.Exp, scale=-1.0), inc=[(s_se, 1)])
            prev_t = None
            for h in range(8):
                tevo = b.op("scalar", lambda e, h=h: e.activation(
                    bcol(OLW, h, 64), psf(3, 512)[:, h * 64:(h + 1) * 64],
                    AF.Identity, scale=bcol(RZ8, h)),
                    wait=[tav_[0]] if h == 0 else [], inc=[(s_se, 1)])
                tto = b.op("tensor", lambda e, h=h: e.transpose(
                    psb(1, 128)[0:64, :], bcol(OLW, h, 64), IDB[:]),
                    wait=[tevo[0]] + ([prev_t] if prev_t else []),
                    inc=[(s_te, 1)])
                tote = b.op("scalar", lambda e, h=h, t=t: e.copy(
                    OT[(h % 2) * 64:(h % 2) * 64 + 64,
                       (h // 2) * T + t * 128:(h // 2) * T + (t + 1) * 128],
                    psb(1, 128)[0:64, 0:128]),
                    wait=[tto[0]], inc=[(s_se, 1)])
                prev_t = tote[0]
            oev = prev_t
            prev_otev = prev_t

            for dc in range(2):
                bank = dc
                for ht in range(4):
                    tmm = b.op("tensor", lambda e, ht=ht, dc=dc, t=t, bank=bank:
                               e.matmul(psf(bank, 512),
                                        OT[:, ht * T + t * 128:
                                           ht * T + (t + 1) * 128],
                                        WOL[:, ht * 1024 + dc * 512:
                                            ht * 1024 + (dc + 1) * 512],
                                        start=(ht == 0), stop=(ht == 3),
                                        skip_group_check=True),
                               wait=([prev_t, wo_ready] if ht == 0 else []),
                               inc=[(s_te, 1)])
                tadd = b.op("vector", lambda e, t=t, dc=dc, bank=bank:
                            e.tensor_add(
                                X[:, t * D + dc * 512: t * D + (dc + 1) * 512],
                                X[:, t * D + dc * 512: t * D + (dc + 1) * 512],
                                psf(bank, 512)),
                            wait=[tmm[0], xload[t]], inc=[(s_ve, 1)])
                b.wait("tensor", [tadd[0]])
        while rs_i < len(rsteps):
            rsteps[rs_i]()
            rs_i += 1
        b.barrier()

    WQH = arb(0 * KB, ND * 512)
    WKVTH = arb(8 * KB, ND * 1024)
    ETH = arb(24 * KB, NT * 256)
    OTH = arb(32 * KB, 4 * T)
    WSTGH = arf(48 * KB, 1024)
    QTM0 = arb(52 * KB, T)
    QTM1 = arb(56 * KB, T)
    KTO = arb(60 * KB, T)
    VH = arb(48 * KB, NT * 66)
    OHS = arb(50 * KB + 512, 64)
    MRW = arf(51 * KB, 128)

    def stage_heavy(bi):
        def load_wcast_h(dst, src_ap, cols, scale_col=None, wait=()):
            tl = b.op("sync", lambda e, src_ap=src_ap, cols=cols: e.dma_start(
                out=WSTGH[:, 0:cols], in_=src_ap), wait=list(wait),
                inc=[(s_dma, 16)])
            if scale_col is None:
                tc = b.op("scalar", lambda e, dst=dst, cols=cols: e.copy(
                    dst, WSTGH[:, 0:cols]), wait=[tl[0]], inc=[(s_se, 1)])
            else:
                tc = b.op("scalar", lambda e, dst=dst, cols=cols,
                          scale_col=scale_col: e.activation(
                    dst, WSTGH[:, 0:cols], AF.Identity, scale=scale_col),
                    wait=[tl[0]], inc=[(s_se, 1)])
            return tc[0]

        prev = None
        for dt in range(ND):
            prev = load_wcast_h(bcol(WQH, dt, 512),
                                g["L0_ffn_wq_heavy"][dt * 128:(dt + 1) * 128, :],
                                512, scale_col=bcol(GHC, dt),
                                wait=[prev] if prev else [])
        for rt_ in range(ND):
            tl = b.op("sync", lambda e, rt_=rt_: e.dma_start(
                out=WSTGH[:, 0:1024],
                in_=g["L0_ffn_wkv_heavy"][rt_ * 128:(rt_ + 1) * 128, :]),
                wait=[prev], inc=[(s_dma, 16)])
            for dt in range(ND):
                fb = 4 + (dt % 2)
                tt = b.op("tensor", lambda e, dt=dt, fb=fb: e.transpose(
                    psf(fb, 128), WSTGH[:, dt * 128:(dt + 1) * 128], IDF[:]),
                    wait=[tl[0]] + ([prev] if prev else []), inc=[(s_te, 1)])
                tev = b.op("scalar", lambda e, dt=dt, rt_=rt_, fb=fb:
                           e.activation(
                    WKVTH[:, dt * 1024 + rt_ * 128: dt * 1024 + (rt_ + 1) * 128],
                    psf(fb, 128), AF.Identity, scale=bcol(GHC, dt)),
                    wait=[tt[0]], inc=[(s_se, 1)])
                prev = tev[0]
            b.wait("tensor", [prev])
        wall = prev

        tko = b.op("gpsimd", lambda e: e.memset(KTO[:], 1.0),
                   wait=[wall], inc=[(s_gp, 1)])
        tvo0 = b.op("gpsimd", lambda e: e.memset(VH[:], 1.0),
                    wait=[tko[0]], inc=[(s_gp, 1)])

        QTMs = [QTM0, QTM1]
        qtm_free = [None, None]
        kto_free = None
        vh_free = None
        last_ot = None
        NQC = 8
        for j in range(4):
            pevs = []
            for tc in range(4):
                bank = tc % 2
                for dt in range(ND):
                    tmm = b.op("tensor", lambda e, j=j, dt=dt, tc=tc, bank=bank:
                               e.matmul(psf(bank, 512),
                                        WQH[:, dt * 512 + j * 128:
                                            dt * 512 + (j + 1) * 128],
                                        BASET[:, dt * T + tc * 512:
                                              dt * T + (tc + 1) * 512],
                                        start=(dt == 0), stop=(dt == ND - 1),
                                        skip_group_check=True),
                               wait=(([wall] + [x for x in qtm_free if x] +
                                      (pevs[-2:] if pevs else []))
                                     if dt == 0 else []),
                               inc=[(s_te, 1)])
                ev2 = []
                for hh in range(2):
                    tev = b.op("scalar",
                               lambda e, tc=tc, hh=hh, bank=bank: e.activation(
                        QTMs[hh][hh * 0 + 0:64, tc * 512:(tc + 1) * 512]
                        if False else
                        QTMs[hh].rearrange("p f -> p f")[0:64,
                                                        tc * 512:(tc + 1) * 512],
                        psf(bank, 512)[hh * 64:hh * 64 + 64, :],
                        AF.Copy, scale=0.125),
                        wait=[tmm[0]], inc=[(s_se, 1)])
                    ev2.append(tev[0])
                pevs.append(ev2[-1])
            b.wait("tensor", [pevs[-1]])
            qproj_done = pevs[-1]

            for hh in range(2):
                h = 2 * j + hh
                QTM = QTMs[hh]
                kevs = []
                for tc in range(4):
                    for dt in range(ND):
                        tmm = b.op("tensor",
                                   lambda e, h=h, dt=dt, tc=tc: e.matmul(
                            psf(2, 512)[0:64, :],
                            WKVTH[:, dt * 1024 + h * 128:
                                  dt * 1024 + h * 128 + 64],
                            BASET[:, dt * T + tc * 512:
                                  dt * T + (tc + 1) * 512],
                            start=(dt == 0), stop=(dt == ND - 1),
                            skip_group_check=True),
                            wait=(([kevs[-1]] if kevs else
                                   [tko[0]] + ([kto_free] if kto_free else []))
                                  if dt == 0 else []),
                            inc=[(s_te, 1)])
                    tev = b.op("scalar", lambda e, tc=tc: e.copy(
                        KTO[0:64, tc * 512:(tc + 1) * 512],
                        psf(2, 512)[0:64, :]), wait=[tmm[0]], inc=[(s_se, 1)])
                    kevs.append(tev[0])
                vevs = []
                for kt in range(NT):
                    for dt in range(ND):
                        tmm = b.op("tensor",
                                   lambda e, h=h, dt=dt, kt=kt: e.matmul(
                            psf(3, 64),
                            BASET[:, dt * T + kt * 128: dt * T + (kt + 1) * 128],
                            WKVTH[:, dt * 1024 + h * 128 + 64:
                                  dt * 1024 + h * 128 + 128],
                            start=(dt == 0), stop=(dt == ND - 1),
                            skip_group_check=True),
                            wait=(([vevs[-1]] if vevs else
                                   [tvo0[0]] + ([vh_free] if vh_free else []))
                                  if dt == 0 else []),
                            inc=[(s_te, 1)])
                    tev = b.op("scalar", lambda e, kt=kt: e.copy(
                        VH[:, kt * 66: kt * 66 + 64], psf(3, 64)),
                        wait=[tmm[0]], inc=[(s_se, 1)])
                    vevs.append(tev[0])

                M128 = RT[:, 35:51]
                p1prev = None
                for t in range(NT):
                    for q4 in range(4):
                        tmm1 = b.op("tensor",
                                    lambda e, t=t, q4=q4, QTM=QTM: e.matmul(
                            psf(2 + q4, 512),
                            QTM[0:64, t * 128:(t + 1) * 128],
                            KTO[0:64, q4 * 512:(q4 + 1) * 512],
                            start=True, stop=True),
                            wait=(([qproj_done, kevs[-1], vevs[-1]] +
                                   ([p1prev] if p1prev else []))
                                  if q4 == 0 else []),
                            inc=[(s_te, 1)])
                    trm = b.op("vector", lambda e, t=t, M128=M128: e.reduce_max(
                        bcol(M128, t), PSF[:, 2 * 512: 6 * 512],
                        axis=AX.X, negate=True),
                        wait=[tmm1[0]], inc=[(s_ve, 1)])
                    p1prev = trm[0]
                ttm = b.op("tensor", lambda e, M128=M128: e.transpose(
                    psf(2, 128)[0:16, :], M128, IDF[:]),
                    wait=[p1prev], inc=[(s_te, 1)])
                tme = b.op("scalar", lambda e: e.copy(
                    MRW[0:16, 0:128], psf(2, 128)[0:16, :]),
                    wait=[ttm[0]], inc=[(s_se, 1)])
                b.wait("tensor", [tme[0]])
                tmd = b.op("gpsimd", lambda e, QTM=QTM: e.dma_start(
                    out=QTM[64:65, 0:T], in_=MRW[0:16, 0:128]),
                    wait=[tme[0]], inc=[(s_gdma, 16)])

                et_last2 = [tmd[0], tmd[0]]
                avprev = [None, None]
                for qc in range(NQC):
                    etevs = []
                    for kt in range(NT):
                        bank = kt % 2
                        tmm = b.op("tensor",
                                   lambda e, kt=kt, qc=qc, bank=bank, QTM=QTM:
                                   e.matmul(
                            psf(bank, 256),
                            KTO[0:65, kt * 128:(kt + 1) * 128],
                            QTM[0:65, qc * 256:(qc + 1) * 256],
                            start=True, stop=True),
                            wait=([et_last2[kt % 2]] if kt < 2 else
                                  [etevs[kt - 2]]),
                            inc=[(s_te, 1)])
                        tex = b.op("scalar", lambda e, kt=kt, bank=bank:
                                   e.activation(
                            ETH[:, kt * 256:(kt + 1) * 256], psf(bank, 256),
                            AF.Exp), wait=[tmm[0]], inc=[(s_se, 1)])
                        etevs.append(tex[0])
                    et_last2 = [etevs[-2], etevs[-1]]
                    for ms in range(2):
                        bank = 2 + ms
                        for kt in range(NT):
                            tmm = b.op("tensor",
                                       lambda e, kt=kt, ms=ms, bank=bank:
                                       e.matmul(
                                PSF[:, bank * 512: bank * 512 + 66],
                                ETH[:, kt * 256 + ms * 128:
                                    kt * 256 + ms * 128 + 128],
                                VH[:, kt * 66:(kt + 1) * 66],
                                start=(kt == 0), stop=(kt == NT - 1),
                                skip_group_check=True),
                                wait=(([etevs[-1]] +
                                       ([avprev[ms]] if avprev[ms] else []))
                                      if kt == 0 else []),
                                inc=[(s_te, 1)])
                        tlz = b.op("scalar", lambda e, bank=bank: e.activation(
                            TCOL[:, 10:11],
                            PSF[:, bank * 512 + 64: bank * 512 + 65], AF.Ln),
                            wait=[tmm[0]], inc=[(s_se, 1)])
                        trz = b.op("scalar", lambda e: e.activation(
                            TCOL[:, 11:12], TCOL[:, 10:11], AF.Exp, scale=-1.0),
                            inc=[(s_se, 1)])
                        tevo = b.op("scalar", lambda e, bank=bank: e.activation(
                            OHS[:, 0:64], PSF[:, bank * 512: bank * 512 + 64],
                            AF.Identity, scale=TCOL[:, 11:12]),
                            inc=[(s_se, 1)])
                        tto = b.op("tensor", lambda e: e.transpose(
                            psb(1, 128)[0:64, :], OHS[:, 0:64], IDB[:]),
                            wait=[tevo[0]], inc=[(s_te, 1)])
                        tote = b.op("scalar", lambda e, h=h, qc=qc, ms=ms:
                                    e.copy(
                            OTH[(h % 2) * 64:(h % 2) * 64 + 64,
                                (h // 2) * T + qc * 256 + ms * 128:
                                (h // 2) * T + qc * 256 + (ms + 1) * 128],
                            psb(1, 128)[0:64, 0:128]),
                            wait=[tto[0]], inc=[(s_se, 1)])
                        b.wait("tensor", [tote[0]])
                        avprev[ms] = tote[0]
                        last_ot = tote[0]
                qtm_free[hh] = et_last2[1]
                kto_free = et_last2[1]
                vh_free = last_ot

        WOH2 = arb(0 * KB, 4 * 1024)
        prevw = None
        for ht in range(4):
            prevw = load_wcast_h(bcol(WOH2, ht, 1024),
                                 g["L0_ffn_wo_heavy"][ht * 128:(ht + 1) * 128, :],
                                 1024,
                                 wait=[last_ot] if ht == 0 else [prevw])
        woadd_prev = None
        for t in range(NT):
            for dc in range(2):
                bank = dc
                for ht in range(4):
                    tmm = b.op("tensor", lambda e, ht=ht, dc=dc, t=t, bank=bank:
                               e.matmul(psf(bank, 512),
                                        OTH[:, ht * T + t * 128:
                                            ht * T + (t + 1) * 128],
                                        WOH2[:, ht * 1024 + dc * 512:
                                             ht * 1024 + (dc + 1) * 512],
                                        start=(ht == 0), stop=(ht == 3),
                                        skip_group_check=True),
                               wait=(([prevw, last_ot] +
                                      ([woadd_prev] if woadd_prev else []))
                                     if ht == 0 else []),
                               inc=[(s_te, 1)])
                tadd = b.op("vector", lambda e, t=t, dc=dc, bank=bank:
                            e.scalar_tensor_tensor(
                                X[:, t * D + dc * 512: t * D + (dc + 1) * 512],
                                psf(bank, 512), bcol(MASKC, t),
                                X[:, t * D + dc * 512: t * D + (dc + 1) * 512],
                                ALU.mult, ALU.add),
                            wait=[tmm[0]], inc=[(s_ve, 1)])
                woadd_prev = tadd[0]
        b.barrier()

    WQ1 = arb(0 * KB, ND * 1024)
    WK1 = arb(16 * KB, ND * 64)
    WV1 = arb(17 * KB, ND * 64)
    ETH1 = arb(18 * KB, NT * 256)
    OT1H = arb(64 * KB, 8 * 1024)
    QTM1s = [arb(80 * KB, 1024), arb(82 * KB, 1024)]
    KTO1 = arb(84 * KB, T)
    V1 = arb(88 * KB, NT * 66)
    OHS1 = arb(91 * KB, 64)
    MRW1 = arf(92 * KB, 128)

    def stage_l1attn(bi):
        tk_ = b.op("gpsimd", lambda e: e.dma_start(
            out=WK1.rearrange("p (a c) -> p a c", a=ND),
            in_=g["L1_attn_Wk"].rearrange("(a p) c -> p a c", p=128)),
            inc=[(s_gdma, 16)])
        tv_ = b.op("gpsimd", lambda e: e.dma_start(
            out=WV1.rearrange("p (a c) -> p a c", a=ND),
            in_=g["L1_attn_Wv"].rearrange("(a p) c -> p a c", p=128)),
            inc=[(s_gdma, 16)])

        tko = b.op("gpsimd", lambda e: e.memset(KTO1[:], 1.0), inc=[(s_gp, 1)])
        kevs = []
        for tc in range(4):
            for dt in range(ND):
                tmm = b.op("tensor", lambda e, dt=dt, tc=tc: e.matmul(
                    psf(2, 512)[0:64, :],
                    WK1[:, dt * 64:(dt + 1) * 64],
                    XNT[:, dt * T + tc * 512: dt * T + (tc + 1) * 512],
                    start=(dt == 0), stop=(dt == ND - 1),
                    skip_group_check=True),
                    wait=(([tk_[0], tko[0]] + ([kevs[-1]] if kevs else []))
                          if dt == 0 else []),
                    inc=[(s_te, 1)])
            tev = b.op("scalar", lambda e, tc=tc: e.activation(
                KTO1[0:64, tc * 512:(tc + 1) * 512], psf(2, 512)[0:64, :],
                AF.Identity, bias=BKC[0:64, 0:1]),
                wait=[tmm[0]], inc=[(s_se, 1)])
            kevs.append(tev[0])
        tvo = b.op("gpsimd", lambda e: e.memset(V1[:], 1.0),
                   wait=[tko[0]], inc=[(s_gp, 1)])
        vevs = []
        for kt in range(NT):
            for dt in range(ND):
                tmm = b.op("tensor", lambda e, dt=dt, kt=kt: e.matmul(
                    psf(3, 64),
                    XNT[:, dt * T + kt * 128: dt * T + (kt + 1) * 128],
                    WV1[:, dt * 64:(dt + 1) * 64],
                    start=(dt == 0), stop=False, skip_group_check=True),
                    wait=(([tv_[0], tvo[0]] + ([vevs[-1]] if vevs else []))
                          if dt == 0 else []),
                    inc=[(s_te, 1)])
            tbb = b.op("tensor", lambda e: e.matmul(
                psf(3, 64), ONESB[:], BVROW[:], start=False, stop=True,
                skip_group_check=True), inc=[(s_te, 1)])
            tev = b.op("scalar", lambda e, kt=kt: e.copy(
                V1[:, kt * 66: kt * 66 + 64], psf(3, 64)),
                wait=[tbb[0]], inc=[(s_se, 1)])
            vevs.append(tev[0])

        woadd_prev = None
        wo_used = None
        for hf in range(2):
            hoff = hf * 1024
            tq = b.op("gpsimd", lambda e: e.dma_start(
                out=WQ1.rearrange("p (a c) -> p a c", a=ND),
                in_=g["L1_attn_Wq"].rearrange("(a p) c -> p a c", p=128)),
                wait=[wo_used] if wo_used else [], inc=[(s_gdma, 16)])
            qtm_free = [None, None]
            last_ot = None
            for j in range(8):
                pevs = []
                for tc in range(2):
                    bank = tc % 2
                    for dt in range(ND):
                        tmm = b.op("tensor",
                                   lambda e, j=j, dt=dt, tc=tc, bank=bank,
                                   hoff=hoff: e.matmul(
                            psf(bank, 512),
                            WQ1[:, dt * 1024 + j * 128: dt * 1024 + (j + 1) * 128],
                            XNT[:, dt * T + hoff + tc * 512:
                                dt * T + hoff + (tc + 1) * 512],
                            start=(dt == 0), stop=(dt == ND - 1),
                            skip_group_check=True),
                            wait=(([tq[0]] + [x for x in qtm_free if x] +
                                   (pevs[-2:] if pevs else []))
                                  if dt == 0 else []),
                            inc=[(s_te, 1)])
                    ev2 = []
                    for hh in range(2):
                        tev = b.op("scalar",
                                   lambda e, j=j, tc=tc, hh=hh, bank=bank:
                                   e.activation(
                            QTM1s[hh][0:64, tc * 512:(tc + 1) * 512],
                            psf(bank, 512)[hh * 64:hh * 64 + 64, :],
                            AF.Identity, scale=0.125,
                            bias=BQC[hh * 64:hh * 64 + 64, j:j + 1]),
                            wait=[tmm[0]], inc=[(s_se, 1)])
                        ev2.append(tev[0])
                    pevs.append(ev2[-1])
                b.wait("tensor", [pevs[-1]])
                qproj_done = pevs[-1]

                for hh in range(2):
                    h = 2 * j + hh
                    QTM = QTM1s[hh]
                    M128 = RT[:, 35:51]
                    p1prev = None
                    for t8 in range(8):
                        for q4 in range(4):
                            tmm1 = b.op("tensor",
                                        lambda e, t8=t8, q4=q4, QTM=QTM:
                                        e.matmul(
                                psf(2 + q4, 512),
                                QTM[0:64, t8 * 128:(t8 + 1) * 128],
                                KTO1[0:64, q4 * 512:(q4 + 1) * 512],
                                start=True, stop=True),
                                wait=(([qproj_done, kevs[-1], vevs[-1]] +
                                       ([p1prev] if p1prev else []))
                                      if q4 == 0 else []),
                                inc=[(s_te, 1)])
                        trm = b.op("vector",
                                   lambda e, t8=t8, M128=M128: e.reduce_max(
                            bcol(M128, t8), PSF[:, 2 * 512: 6 * 512],
                            axis=AX.X, negate=True),
                            wait=[tmm1[0]], inc=[(s_ve, 1)])
                        p1prev = trm[0]
                    ttm = b.op("tensor", lambda e, M128=M128: e.transpose(
                        psf(2, 128)[0:16, :], M128[:, 0:16], IDF[:]),
                        wait=[p1prev], inc=[(s_te, 1)])
                    tme = b.op("scalar", lambda e: e.copy(
                        MRW1[0:8, 0:128], psf(2, 128)[0:8, :]),
                        wait=[ttm[0]], inc=[(s_se, 1)])
                    b.wait("tensor", [tme[0]])
                    tmd = b.op("gpsimd", lambda e, QTM=QTM: e.dma_start(
                        out=QTM[64:65, 0:1024], in_=MRW1[0:8, 0:128]),
                        wait=[tme[0]], inc=[(s_gdma, 16)])

                    et_last2 = [tmd[0], tmd[0]]
                    avprev = [None, None]
                    for qc in range(4):
                        etevs = []
                        for kt in range(NT):
                            bank = kt % 2
                            tmm = b.op("tensor",
                                       lambda e, kt=kt, qc=qc, bank=bank,
                                       QTM=QTM: e.matmul(
                                psf(bank, 256),
                                KTO1[0:65, kt * 128:(kt + 1) * 128],
                                QTM[0:65, qc * 256:(qc + 1) * 256],
                                start=True, stop=True),
                                wait=([et_last2[kt % 2]] if kt < 2 else
                                      [etevs[kt - 2]]),
                                inc=[(s_te, 1)])
                            tex = b.op("scalar", lambda e, kt=kt, bank=bank:
                                       e.activation(
                                ETH1[:, kt * 256:(kt + 1) * 256],
                                psf(bank, 256), AF.Exp),
                                wait=[tmm[0]], inc=[(s_se, 1)])
                            etevs.append(tex[0])
                        et_last2 = [etevs[-2], etevs[-1]]
                        for ms in range(2):
                            bank = 2 + ms
                            for kt in range(NT):
                                tmm = b.op("tensor",
                                           lambda e, kt=kt, ms=ms, bank=bank:
                                           e.matmul(
                                    PSF[:, bank * 512: bank * 512 + 66],
                                    ETH1[:, kt * 256 + ms * 128:
                                         kt * 256 + ms * 128 + 128],
                                    V1[:, kt * 66:(kt + 1) * 66],
                                    start=(kt == 0), stop=(kt == NT - 1),
                                    skip_group_check=True),
                                    wait=(([etevs[-1]] +
                                           ([avprev[ms]] if avprev[ms] else []))
                                          if kt == 0 else []),
                                    inc=[(s_te, 1)])
                            tlz = b.op("scalar", lambda e, bank=bank:
                                       e.activation(
                                TCOL[:, 10:11],
                                PSF[:, bank * 512 + 64: bank * 512 + 65],
                                AF.Ln), wait=[tmm[0]], inc=[(s_se, 1)])
                            trz = b.op("scalar", lambda e: e.activation(
                                TCOL[:, 11:12], TCOL[:, 10:11], AF.Exp,
                                scale=-1.0), inc=[(s_se, 1)])
                            tevo = b.op("scalar", lambda e, bank=bank:
                                        e.activation(
                                OHS1[:, 0:64],
                                PSF[:, bank * 512: bank * 512 + 64],
                                AF.Identity, scale=TCOL[:, 11:12]),
                                inc=[(s_se, 1)])
                            tto = b.op("tensor", lambda e: e.transpose(
                                psb(1, 128)[0:64, :], OHS1[:, 0:64], IDB[:]),
                                wait=[tevo[0]], inc=[(s_te, 1)])
                            tote = b.op("scalar",
                                        lambda e, h=h, qc=qc, ms=ms: e.copy(
                                OT1H[(h % 2) * 64:(h % 2) * 64 + 64,
                                     (h // 2) * 1024 + qc * 256 + ms * 128:
                                     (h // 2) * 1024 + qc * 256 + (ms + 1) * 128],
                                psb(1, 128)[0:64, 0:128]),
                                wait=[tto[0]], inc=[(s_se, 1)])
                            b.wait("tensor", [tote[0]])
                            avprev[ms] = tote[0]
                            last_ot = tote[0]
                    qtm_free[hh] = et_last2[1]

            two = b.op("gpsimd", lambda e: e.dma_start(
                out=WQ1.rearrange("p (a c) -> p a c", a=ND),
                in_=g["L1_attn_Wo"].rearrange("(a p) c -> p a c", p=128)),
                wait=[qproj_done], inc=[(s_gdma, 16)])
            for t8 in range(8):
                t = hf * 8 + t8
                for dc in range(2):
                    bank = dc
                    for ht in range(8):
                        tmm = b.op("tensor",
                                   lambda e, ht=ht, dc=dc, t8=t8, bank=bank:
                                   e.matmul(
                            psf(bank, 512),
                            OT1H[:, ht * 1024 + t8 * 128:
                                 ht * 1024 + (t8 + 1) * 128],
                            WQ1[:, ht * 1024 + dc * 512:
                                ht * 1024 + (dc + 1) * 512],
                            start=(ht == 0), stop=False,
                            skip_group_check=True),
                            wait=(([last_ot, two[0]] +
                                   ([woadd_prev] if woadd_prev else []))
                                  if ht == 0 else []),
                            inc=[(s_te, 1)])
                    tbb = b.op("tensor", lambda e, dc=dc, bank=bank: e.matmul(
                        psf(bank, 512), ONESB[:],
                        ROWSB[0:1, D + dc * 512: D + (dc + 1) * 512],
                        start=False, stop=True, skip_group_check=True),
                        inc=[(s_te, 1)])
                    tadd = b.op("vector", lambda e, t=t, dc=dc, bank=bank:
                                e.tensor_add(
                        X[:, t * D + dc * 512: t * D + (dc + 1) * 512],
                        X[:, t * D + dc * 512: t * D + (dc + 1) * 512],
                        psf(bank, 512)),
                        wait=[tbb[0]], inc=[(s_ve, 1)])
                    woadd_prev = tadd[0]
            wo_used = tmm[0]
        b.barrier()

    OUTACC = arb(64 * KB, 8 * 1024)
    SILU = arf(80 * KB, 1024)
    XSTG2 = [arf(84 * KB, D), arf(88 * KB, D)]

    def w1buf(p, ag):
        return arb((p * 16 + ag * 8) * KB, ND * 512)

    def w2buf(p):
        return arb(p * 16 * KB, 32 * 256)

    def t3t(kt):
        return arb(96 * KB + kt * 2 * KB, 1024)

    def stage_moe(bi):
        acc_free = None
        wreg_free = None
        t3_free = None
        out_done = []
        for c in range(2):
            tms = b.op("gpsimd", lambda e: e.memset(OUTACC[:], 0.0),
                       wait=[acc_free] if acc_free else [], inc=[(s_gp, 1)])
            acc_prev = tms[0]
            for e_ in range(NEXP):
                last_t3 = None
                pair_ev = {}
                dma_toks = {}
                for grp in range(8):
                    p = grp % 2
                    wa = b.op("gpsimd", lambda e, e_=e_, grp=grp, p=p:
                              e.dma_start(
                        out=w1buf(p, 0).rearrange("q (a c) -> q a c", a=ND),
                        in_=g["L1_ffn_w1"][e_].rearrange(
                            "(a q) f -> q a f", q=128)[:, :,
                            grp * 512:(grp + 1) * 512]),
                        wait=([wreg_free] if wreg_free and grp < 2 else []) +
                             ([dma_toks[grp - 2][2]] if grp >= 2 else []),
                        inc=[(s_gdma, 16)])
                    wg = b.op("gpsimd", lambda e, e_=e_, grp=grp, p=p:
                              e.dma_start(
                        out=w1buf(p, 1).rearrange("q (a c) -> q a c", a=ND),
                        in_=g["L1_ffn_w1"][e_].rearrange(
                            "(a q) f -> q a f", q=128)[:, :,
                            FFN + grp * 512: FFN + (grp + 1) * 512]),
                        inc=[(s_gdma, 16)])
                    grp_last = None
                    for i in range(4):
                        pp = i % 2
                        ab = 4 * pp
                        gb = 4 * pp + 2
                        ja = grp * 4 + i
                        for nch in range(2):
                            for dt in range(ND):
                                tma = b.op("tensor",
                                           lambda e, i=i, dt=dt, nch=nch,
                                           c=c, p=p, ab=ab: e.matmul(
                                    psf(ab + nch, 512),
                                    w1buf(p, 0)[:, dt * 512 + i * 128:
                                                dt * 512 + (i + 1) * 128],
                                    XNT[:, dt * T + c * 1024 + nch * 512:
                                        dt * T + c * 1024 + (nch + 1) * 512],
                                    start=(dt == 0), stop=(dt == ND - 1),
                                    skip_group_check=True),
                                    wait=(([wa[0]] +
                                           ([pair_ev[pp][0]] if pp in pair_ev
                                            else []) +
                                           ([t3_free] if t3_free and grp == 0
                                            and i == 0 else []))
                                          if dt == 0 and nch == 0 else []),
                                    inc=[(s_te, 1)])
                            for dt in range(ND):
                                tmg = b.op("tensor",
                                           lambda e, i=i, dt=dt, nch=nch,
                                           c=c, p=p, gb=gb: e.matmul(
                                    psf(gb + nch, 512),
                                    w1buf(p, 1)[:, dt * 512 + i * 128:
                                                dt * 512 + (i + 1) * 128],
                                    XNT[:, dt * T + c * 1024 + nch * 512:
                                        dt * T + c * 1024 + (nch + 1) * 512],
                                    start=(dt == 0), stop=(dt == ND - 1),
                                    skip_group_check=True),
                                    wait=(([wg[0]] +
                                           ([pair_ev[pp][1]] if pp in pair_ev
                                            else []))
                                          if dt == 0 and nch == 0 else []),
                                    inc=[(s_te, 1)])
                        tsi = b.op("scalar", lambda e, ab=ab: e.activation(
                            SILU[:], PSF[:, ab * 512:(ab + 2) * 512],
                            AF.Silu), wait=[tma[0]], inc=[(s_se, 1)])
                        tt3 = b.op("vector", lambda e, ja=ja, gb=gb: e.tensor_mul(
                            t3t(ja), SILU[:], PSF[:, gb * 512:(gb + 2) * 512]),
                            wait=[tsi[0], tmg[0]], inc=[(s_ve, 1)])
                        pair_ev[pp] = (tt3[0], tt3[0])
                        grp_last = tt3[0]
                        last_t3 = tt3[0]
                    dma_toks[grp] = (wa[0], wg[0], grp_last)
                w2toks = {}
                y2prev = {}
                for nq in range(4):
                    p = nq % 2
                    wts = []
                    for piece in range(4):
                        tw2 = b.op("gpsimd",
                                   lambda e, e_=e_, nq=nq, p=p, piece=piece:
                                   e.dma_start(
                            out=w2buf(p).rearrange(
                                "q (a c) -> q a c", a=32)[:, piece * 8:
                                                          (piece + 1) * 8],
                            in_=g["L1_ffn_w2"][e_].rearrange(
                                "(a q) f -> q a f", q=128)[:, piece * 8:
                                                           (piece + 1) * 8,
                                nq * 256:(nq + 1) * 256]),
                            wait=([last_t3] if nq < 2 and piece == 0 else []) +
                                 ([w2toks[nq - 2]] if nq >= 2 and piece == 0
                                  else []),
                            inc=[(s_gdma, 16)])
                        wts.append(tw2[0])
                    for mt in range(8):
                        bank = mt % 2
                        for kt in range(32):
                            tmy = b.op("tensor",
                                       lambda e, kt=kt, mt=mt, nq=nq, p=p,
                                       bank=bank: e.matmul(
                                psf(bank, 256),
                                t3t(kt)[:, mt * 128:(mt + 1) * 128],
                                w2buf(p)[:, kt * 256:(kt + 1) * 256],
                                start=(kt == 0), stop=(kt == 31),
                                skip_group_check=True),
                                wait=((wts + [last_t3] +
                                       ([y2prev[bank]] if bank in y2prev
                                        else []))
                                      if kt == 0 else []),
                                inc=[(s_te, 1)])
                        tya = b.op("vector",
                                   lambda e, mt=mt, nq=nq, c=c, e_=e_,
                                   bank=bank: e.scalar_tensor_tensor(
                            OUTACC[:, mt * 1024 + nq * 256:
                                   mt * 1024 + (nq + 1) * 256],
                            psf(bank, 256),
                            SELC[:, (8 * c + mt) * NEXP + e_:
                                 (8 * c + mt) * NEXP + e_ + 1],
                            OUTACC[:, mt * 1024 + nq * 256:
                                   mt * 1024 + (nq + 1) * 256],
                            ALU.mult, ALU.add),
                            wait=[tmy[0], acc_prev], inc=[(s_ve, 1)])
                        y2prev[bank] = tya[0]
                    w2toks[nq] = tmy[0]
                acc_prev = tya[0]
                wreg_free = tya[0]
                t3_free = tya[0]
            slot_busy = [None, None]
            for mt in range(8):
                tg = 8 * c + mt
                sl = mt % 2
                tl = b.op("sync", lambda e, tg=tg, sl=sl: e.dma_start(
                    out=XSTG2[sl][:], in_=hbm_tiles(scratch_x1[bi])[tg]),
                    wait=[slot_busy[sl]] if slot_busy[sl] else [],
                    inc=[(s_dma, 16)])
                ta_ = b.op("vector", lambda e, mt=mt, sl=sl: e.tensor_add(
                    XSTG2[sl][:], XSTG2[sl][:],
                    OUTACC[:, mt * 1024:(mt + 1) * 1024]),
                    wait=[tl[0], acc_prev], inc=[(s_ve, 1)])
                tw = b.op("sync", lambda e, tg=tg, sl=sl, bi=bi: e.dma_start(
                    out=hbm_tiles(out[bi])[tg], in_=XSTG2[sl][:]),
                    wait=[ta_[0]], inc=[(s_out, 16)])
                slot_busy[sl] = tw[0]
                out_done.append(tw[0])
            acc_free = ta_[0]
        b.barrier()

    setup_constants()
    for bi in range(NB):
        stage_ln(bi, 0)
        if stop_after == "ln0":
            for t in range(NT):
                b.op("sync", lambda e, t=t, bi=bi: e.dma_start(
                    out=hbm_tiles(out[bi])[t], in_=bcol(X, t, D)),
                    inc=[(s_out, 16)])
            continue
        offload_X(bi)
        stage_ssm(bi)
        stage_yadd(bi)
        rsteps = routing_steps(bi)
        if stop_after == "ssm":
            for r in rsteps:
                r()
            b.barrier()
            b.op("sync", lambda e, bi=bi: e.dma_start(
                out=out[bi], in_=scratch_x1[bi]), inc=[(s_out, 16)])
            continue
        stage_light(bi, rsteps)
        stage_heavy(bi)
        if stop_after == "block0":
            for t in range(NT):
                b.op("sync", lambda e, t=t, bi=bi: e.dma_start(
                    out=hbm_tiles(out[bi])[t], in_=bcol(X, t, D)),
                    inc=[(s_out, 16)])
            continue
        stage_ln(bi, 1)
        stage_l1attn(bi)
        if stop_after == "l1attn":
            for t in range(NT):
                b.op("sync", lambda e, t=t, bi=bi: e.dma_start(
                    out=hbm_tiles(out[bi])[t], in_=bcol(X, t, D)),
                    inc=[(s_out, 16)])
            continue
        offload_X(bi)
        stage_moe(bi)

    b.wait("sync", [(s_out, s_out.n)])
    b.barrier()
    b.finalize()
    es.close()
    return nc, tap_handles


def kernel(**inputs):
    nc, _ = build_program(stop_after=None)
    xs = np.ascontiguousarray(inputs["L0_attn_x"], np.float32)
    common = {k: np.ascontiguousarray(np.asarray(v), np.float32)
              for k, v in inputs.items() if k != "L0_attn_x"}
    in_maps = []
    for c in range(8):
        m = dict(common)
        m["x"] = xs[2 * c:2 * c + 2]
        in_maps.append(m)
    res = run_bass_kernel_spmd(nc, in_maps, core_ids=list(range(8)))
    return np.concatenate([r["out"] for r in res.results], axis=0)


# version 02
# speedup vs baseline: 1.0300x; 1.0300x over previous
import numpy as np
import ml_dtypes
from contextlib import ExitStack

import concourse.bass as bass
import concourse.mybir as mybir
from concourse.bass_utils import run_bass_kernel_spmd

F32 = mybir.dt.float32
BF16 = mybir.dt.bfloat16
AF = mybir.ActivationFunctionType
ALU = mybir.AluOpType
AX = mybir.AxisListType

T = 2048
NT = 16
D = 1024
ND = 8
FFN = 4096
NEXP = 8
NB = 2
NITER_CD = 50
NITER_TH = 48
NITER_IX = 14
LOGK = float(np.log(np.float32(1152.0)))


class Sem:
    def __init__(self, h):
        self.h = h
        self.n = 0


class Builder:
    ENGINES = ("tensor", "vector", "scalar", "gpsimd", "sync")

    def __init__(self, nc, es):
        self.nc = nc
        self.es = es
        self.q = {e: [] for e in self.ENGINES}
        self._sems = {}
        self.tapnames = set()

    def sem(self, name):
        if name not in self._sems:
            self._sems[name] = Sem(self.es.enter_context(self.nc.semaphore(name)))
        return self._sems[name]

    def sb(self, name, shape, dt=F32, **kw):
        return self.es.enter_context(self.nc.sbuf_tensor(name, shape, dt, **kw))

    def ps(self, name, shape, dt=F32):
        return self.es.enter_context(self.nc.psum_tensor(name, shape, dt))

    def dma_pool(self, tag):
        if not hasattr(self, "_dmapools"):
            self._dmapools = {}
            self._dmarr = {}
        if tag not in self._dmapools:
            self._dmapools[tag] = [self.sem("dmas_%s%d" % (tag, i))
                                   for i in range(8)]
            self._dmarr[tag] = -1
        return self._dmapools[tag]

    def op(self, eng, fn, wait=(), inc=()):
        toks = []
        incs = []
        extra_wait = []
        for s, k in inc:
            if getattr(s, "isdma", False):
                tag = s.tag
                pool = self.dma_pool(tag)
                self._dmarr[tag] = (self._dmarr[tag] + 1) % len(pool)
                slot = pool[self._dmarr[tag]]
                extra_wait.append((slot, slot.n))
                slot.n += 16
                incs.append((slot.h, 16))
                toks.append((slot, slot.n))
            else:
                s.n += k
                incs.append((s.h, k))
                toks.append((s, s.n))
        waits = tuple((s.h, v) for (s, v) in tuple(wait) + tuple(extra_wait)
                      if s is not None)

        def run(engine, fn=fn, waits=waits, incs=incs):
            for h, v in waits:
                engine.wait_ge(h, v)
            r = fn(engine)
            for h, k in incs:
                r = r.then_inc(h, k)

        self.q[eng].append(run)
        return toks

    def wait(self, eng, tokens):
        waits = tuple((s.h, v) for (s, v) in tokens if s is not None)

        def run(engine, waits=waits):
            for h, v in waits:
                engine.wait_ge(h, v)

        self.q[eng].append(run)

    def barrier(self, engines=None):
        engines = engines or self.ENGINES
        s = self.sem("bar")
        for e in engines:
            self.op(e, lambda eng: eng.nop(), inc=[(s, 1)])
        target = s.n
        for e in engines:
            self.wait(e, [(s, target)])
        return (s, target)

    def finalize(self):
        with self.nc.Block() as blk:
            for eng in self.ENGINES:
                def body(engine, eng=eng):
                    for f in self.q[eng]:
                        f(engine)
                getattr(blk, eng)(body)


def bcol(ap, j, w=1):
    return ap[:, j * w:(j + 1) * w]


def build_program(taps=(), stop_after=None):
    nc = bass.Bass("TRN2", target_bir_lowering=False, detect_race_conditions=False)
    es = ExitStack()
    b = Builder(nc, es)
    b.tapnames = set(taps)

    g = {}

    def din(name, shape):
        g[name] = nc.dram_tensor(name, list(shape), F32, kind="ExternalInput").ap()

    din("x", (NB, T, D))
    din("L0_attn_norm_w", (D,))
    din("L0_attn_A_w", (D, D)); din("L0_attn_A_b", (D,))
    din("L0_attn_B_w", (D, D)); din("L0_attn_C_w", (D, D))
    din("L0_attn_D_w", (D, D))
    din("L0_attn_out_w", (D, D)); din("L0_attn_out_b", (D,))
    din("L0_ffn_gamma_light", (D,)); din("L0_ffn_wq_light", (D, 512))
    din("L0_ffn_wkv_light", (D, 1024)); din("L0_ffn_wo_light", (512, D))
    din("L0_ffn_q_routing_token", (D,)); din("L0_ffn_kv_routing_token", (D,))
    din("L0_ffn_gamma_heavy", (D,)); din("L0_ffn_wq_heavy", (D, 512))
    din("L0_ffn_wkv_heavy", (1024, D)); din("L0_ffn_wo_heavy", (512, D))
    din("L1_attn_Wq", (D, 1024)); din("L1_attn_bq", (1024,))
    din("L1_attn_Wk", (D, 64)); din("L1_attn_bk", (64,))
    din("L1_attn_Wv", (D, 64)); din("L1_attn_bv", (64,))
    din("L1_attn_Wo", (1024, D)); din("L1_attn_bo", (D,))
    din("L1_ffn_router_w", (D, NEXP)); din("L1_ffn_router_b", (NEXP,))
    din("L1_ffn_w1", (NEXP, D, 2 * FFN)); din("L1_ffn_w2", (NEXP, FFN, D))

    out = nc.dram_tensor("out", [NB, T, D], F32, kind="ExternalOutput").ap()
    scratch_x1 = nc.dram_tensor("scratch_x1", [NB, T, D], F32, kind="Internal").ap()

    tap_handles = {}

    def tap(name, shape, dt=F32):
        if name in b.tapnames and name not in tap_handles:
            tap_handles[name] = nc.dram_tensor(
                "tap_" + name, list(shape), dt, kind="ExternalOutput").ap()
        return tap_handles.get(name)

    def hbm_tiles(ap2d):
        return ap2d.rearrange("(t p) d -> t p d", p=128)

    S128 = b.sb("S128", [128, NT])
    MASKC = b.sb("MASKC", [128, NT])
    BFC = b.sb("BFC", [128, NT])
    RNC = b.sb("RNC", [128, NT])
    IOTA = b.sb("IOTA", [128, NT])
    SELC = b.sb("SELC", [128, NT * NEXP])
    NWC = b.sb("NWC", [128, ND])
    GLC = b.sb("GLC", [128, ND])
    GHC = b.sb("GHC", [128, ND])
    ABC = b.sb("ABC", [128, ND])
    BQC = b.sb("BQC", [128, ND])
    RB = b.sb("RB", [128, D])
    ONES = b.sb("ONES", [1, 128])
    ONESB = b.sb("ONESB", [1, 128], BF16)
    ROWSB = b.sb("ROWSB", [1, 2 * D], BF16)
    BVROW = b.sb("BVROW", [1, 64], BF16)
    RWF = b.sb("RWF", [128, ND * NEXP])
    RBROW = b.sb("RBROW", [1, NEXP])
    BKC = b.sb("BKC", [64, 1])
    IDB = b.sb("IDB", [128, 128], BF16)
    IDF = b.sb("IDF", [128, 128])
    XC = b.sb("XC", [128, D])
    XN = b.sb("XN", [128, D])
    TBF1 = b.sb("TBF1", [128, D], BF16)
    TBF2 = b.sb("TBF2", [128, D], BF16)
    TCOL = b.sb("TCOL", [128, 16])
    TCOL2 = b.sb("TCOL2", [128, 16])
    RT = b.sb("RT", [128, 64])
    RSC = b.sb("RSC", [1, 32])
    ROW128 = b.sb("ROW128", [1, 128])

    AR_BYTES = 160 * 1024
    ARENA = b.sb("ARENA", [128, AR_BYTES // 4], F32)
    ARB = ARENA.bitcast(BF16)

    def arf(off, cols):
        assert off % 4 == 0 and off + cols * 4 <= AR_BYTES, (off, cols)
        return ARENA[:, off // 4: off // 4 + cols]

    def arb(off, cols):
        assert off % 2 == 0 and off + cols * 2 <= AR_BYTES, (off, cols)
        return ARB[:, off // 2: off // 2 + cols]

    KB = 1024
    A_X = 96 * KB
    X = arf(A_X, NT * D)
    BASET = arb(64 * KB, ND * T)
    XNT = arb(32 * KB, ND * T)

    PSF = b.ps("PSF", [128, 4096], F32)
    PSFB = PSF.bitcast(BF16)

    def psf(bank, w=512, off=0):
        return PSF[:, bank * 512 + off: bank * 512 + off + w]

    def psb(bank, w=1024, off=0):
        return PSFB[:, (6 + bank) * 1024 + off: (6 + bank) * 1024 + off + w]

    s_dma = b.sem("s_dma")
    s_dma.isdma = True
    s_dma.tag = "sy"
    s_gdma = b.sem("s_gdma")
    s_gdma.isdma = True
    s_gdma.tag = "gp"
    s_te = b.sem("s_te")
    s_ve = b.sem("s_ve")
    s_se = b.sem("s_se")
    s_gp = b.sem("s_gp")
    s_out = b.sem("s_out")

    def setup_constants():
        p0 = b.op("gpsimd", lambda e: e.memset(IDB[:], 0.0), inc=[(s_gp, 1)])
        p0 = b.op("gpsimd", lambda e: e.affine_select(
            out=IDB[:], in_=IDB[:], compare_op=ALU.not_equal, fill=1.0, base=0,
            pattern=[[-1, 128]], channel_multiplier=1),
            wait=[p0[0]], inc=[(s_gp, 1)])
        p0 = b.op("gpsimd", lambda e: e.memset(IDF[:], 0.0),
                  wait=[p0[0]], inc=[(s_gp, 1)])
        p0 = b.op("gpsimd", lambda e: e.affine_select(
            out=IDF[:], in_=IDF[:], compare_op=ALU.not_equal, fill=1.0, base=0,
            pattern=[[-1, 128]], channel_multiplier=1),
            wait=[p0[0]], inc=[(s_gp, 1)])
        p0 = b.op("gpsimd", lambda e: e.iota(
            IOTA[:], pattern=[[128, NT]], base=0, channel_multiplier=1,
            allow_small_or_imprecise_dtypes=True), wait=[p0[0]], inc=[(s_gp, 1)])
        p0 = b.op("gpsimd", lambda e: e.memset(ONES[:], 1.0),
                  wait=[p0[0]], inc=[(s_gp, 1)])
        pgp = b.op("gpsimd", lambda e: e.memset(ONESB[:], 1.0),
                   wait=[p0[0]], inc=[(s_gp, 1)])
        b.wait("tensor", [pgp[0]])
        b.wait("vector", [pgp[0]])
        b.wait("scalar", [pgp[0]])

        t1 = b.op("sync", lambda e: e.dma_start(
            out=XC[0:1, 0:D], in_=g["L0_attn_out_b"][None, :]), inc=[(s_dma, 16)])
        tv1 = b.op("vector", lambda e: e.tensor_copy(ROWSB[:, 0:D], XC[0:1, 0:D]),
                   wait=[t1[0]], inc=[(s_ve, 1)])
        t2 = b.op("sync", lambda e: e.dma_start(
            out=XC[0:1, 0:D], in_=g["L1_attn_bo"][None, :]),
            wait=[tv1[0]], inc=[(s_dma, 16)])
        tv2 = b.op("vector", lambda e: e.tensor_copy(ROWSB[:, D:2 * D], XC[0:1, 0:D]),
                   wait=[t2[0]], inc=[(s_ve, 1)])
        b.op("sync", lambda e: e.dma_start(
            out=RBROW[:], in_=g["L1_ffn_router_b"][None, :]), inc=[(s_dma, 16)])
        b.op("gpsimd", lambda e: e.dma_start(
            out=BVROW[:], in_=g["L1_attn_bv"][None, :]), inc=[(s_gdma, 16)])

        def chan_cols(name, dest, scale=1.0, prev=None):
            tl = b.op("sync", lambda e, name=name: e.dma_start(
                out=XN[0:8, 0:128],
                in_=g[name].rearrange("(a p) -> a p", p=128)),
                wait=[prev] if prev else [tv2[0]], inc=[(s_dma, 16)])
            tt = b.op("tensor", lambda e: e.transpose(
                psf(0, 8), XN[0:8, 0:128], IDF[0:8, 0:8]),
                wait=[tl[0]], inc=[(s_te, 1)])
            te_ = b.op("scalar", lambda e, dest=dest, scale=scale: e.activation(
                dest[:, 0:8], psf(0, 8), AF.Copy, scale=scale),
                wait=[tt[0]], inc=[(s_se, 1)])
            b.wait("tensor", [te_[0]])
            return te_[0]

        p = None
        p = chan_cols("L0_attn_norm_w", NWC, prev=p)
        p = chan_cols("L0_ffn_gamma_light", GLC, prev=p)
        p = chan_cols("L0_ffn_gamma_heavy", GHC, prev=p)
        p = chan_cols("L0_attn_A_b", ABC, prev=p)
        p = chan_cols("L1_attn_bq", BQC, scale=0.125, prev=p)

        tk = b.op("sync", lambda e: e.dma_start(
            out=XN[0:1, 0:64], in_=g["L1_attn_bk"][None, :]),
            wait=[p], inc=[(s_dma, 16)])
        tkt = b.op("tensor", lambda e: e.transpose(
            psf(0, 1)[0:64, :], XN[0:1, 0:64], IDF[0:1, 0:1]),
            wait=[tk[0]], inc=[(s_te, 1)])
        tke = b.op("scalar", lambda e: e.copy(BKC[:], psf(0, 1)[0:64, 0:1]),
                   wait=[tkt[0]], inc=[(s_se, 1)])
        b.wait("tensor", [tke[0]])

        tr = b.op("sync", lambda e: e.dma_start(
            out=XN[0:1, 0:D], in_=g["L0_ffn_q_routing_token"][None, :]),
            wait=[tke[0]], inc=[(s_dma, 16)])
        prev_ev = None
        for dt in range(ND):
            bank = dt % 2
            tm = b.op("tensor", lambda e, dt=dt, bank=bank: e.matmul(
                psf(bank, 128), ONES[:], XN[0:1, dt * 128:(dt + 1) * 128],
                start=True, stop=True),
                wait=[tr[0]] + ([prev_ev] if prev_ev else []), inc=[(s_te, 1)])
            tev = b.op("scalar", lambda e, dt=dt, bank=bank: e.copy(
                RB[:, dt * 128:(dt + 1) * 128], psf(bank, 128)),
                wait=[tm[0]], inc=[(s_se, 1)])
            prev_ev = tev[0]
        b.wait("tensor", [prev_ev])

        b.op("sync", lambda e: e.dma_start(
            out=RWF.rearrange("p (a c) -> p a c", a=ND),
            in_=g["L1_ffn_router_w"].rearrange("(a p) c -> p a c", p=128)),
            wait=[prev_ev], inc=[(s_dma, 16)])
        b.barrier()

    def part_reduce(src_col, dst_col, alu, extra_wait=()):
        tt = b.op("tensor", lambda e, src_col=src_col: e.transpose(
            psf(5, 128)[0:1, :], src_col, IDF[:]),
            wait=list(extra_wait), inc=[(s_te, 1)])
        te_ = b.op("scalar", lambda e: e.copy(ROW128[:], psf(5, 128)[0:1, :]),
                   wait=[tt[0]], inc=[(s_se, 1)])
        b.wait("tensor", [te_[0]])
        tr_ = b.op("vector", lambda e, dst_col=dst_col, alu=alu: e.tensor_reduce(
            RSC[0:1, dst_col:dst_col + 1], ROW128[:], axis=AX.X, op=alu),
            wait=[te_[0]], inc=[(s_ve, 1)])
        return tr_[0]

    def bcast_cols(src_row_off, n, dst, extra_wait=()):
        tm = b.op("tensor", lambda e, src_row_off=src_row_off, n=n: e.matmul(
            psf(5, n), ONES[:], RSC[0:1, src_row_off:src_row_off + n],
            start=True, stop=True), wait=list(extra_wait), inc=[(s_te, 1)])
        te_ = b.op("scalar", lambda e, dst=dst, n=n: e.copy(dst, psf(5, n)),
                   wait=[tm[0]], inc=[(s_se, 1)])
        b.wait("tensor", [te_[0]])
        return te_[0]

    def stage_ln(bi, which):
        first = (which == 0)
        b.ln_bankfree = {0: None, 1: None}
        tap_xn = tap("xn%d_b%d" % (which, bi), (T, D))
        for t in range(NT):
            if first:
                tl = b.op("sync", lambda e, t=t: e.dma_start(
                    out=bcol(X, t, D), in_=hbm_tiles(g["x"][bi])[t]),
                    inc=[(s_dma, 16)])
                b.wait("vector", [tl[0]])
            xt = bcol(X, t, D)
            tm = b.op("vector", lambda e, xt=xt: e.reduce_sum(
                TCOL[:, 0:1], xt, axis=AX.X), inc=[(s_ve, 1)])
            tnm = b.op("vector", lambda e: e.tensor_scalar(
                TCOL[:, 1:2], TCOL[:, 0:1], -1.0 / D, None, ALU.mult),
                wait=[tm[0]], inc=[(s_ve, 1)])
            txc = b.op("scalar", lambda e, xt=xt: e.activation(
                XC[:], xt, AF.Identity, bias=TCOL[:, 1:2]),
                wait=[tnm[0]], inc=[(s_se, 1)])
            tsq = b.op("scalar", lambda e: e.activation(
                XN[:], XC[:], AF.Square, accum_out=TCOL[:, 2:3]),
                wait=[txc[0]], inc=[(s_se, 1)])
            tv = b.op("vector", lambda e: e.tensor_scalar(
                TCOL[:, 3:4], TCOL[:, 2:3], 1.0 / D, 1e-5, ALU.mult, ALU.add),
                wait=[tsq[0]], inc=[(s_ve, 1)])
            tln = b.op("scalar", lambda e: e.activation(
                TCOL[:, 4:5], TCOL[:, 3:4], AF.Ln), wait=[tv[0]], inc=[(s_se, 1)])
            trs = b.op("scalar", lambda e: e.activation(
                TCOL[:, 5:6], TCOL[:, 4:5], AF.Exp, scale=-0.5), inc=[(s_se, 1)])
            txn = b.op("scalar", lambda e: e.activation(
                XN[:], XC[:], AF.Identity, scale=TCOL[:, 5:6]), inc=[(s_se, 1)])
            if tap_xn is not None:
                b.op("sync", lambda e, t=t, tap_xn=tap_xn: e.dma_start(
                    out=hbm_tiles(tap_xn)[t], in_=XN[:]),
                    wait=[txn[0]], inc=[(s_out, 16)])

            if first:
                tsd0 = b.op("vector", lambda e: e.tensor_mul(
                    XC[:], XN[:], RB[:]), wait=[txn[0]], inc=[(s_ve, 1)])
                tsdot = b.op("vector", lambda e, t=t: e.reduce_sum(
                    bcol(S128, t), XC[:], axis=AX.X), inc=[(s_ve, 1)])
                tl2 = b.op("scalar", lambda e: e.activation(
                    XC[:], XN[:], AF.Square, accum_out=TCOL[:, 6:7]),
                    wait=[tsdot[0]], inc=[(s_se, 1)])
                tv2 = b.op("vector", lambda e: e.tensor_scalar(
                    TCOL[:, 7:8], TCOL[:, 6:7], 1.0 / D, 1e-5, ALU.mult, ALU.add),
                    wait=[tl2[0]], inc=[(s_ve, 1)])
                tln2 = b.op("scalar", lambda e: e.activation(
                    TCOL2[:, 0:1], TCOL[:, 7:8], AF.Ln), wait=[tv2[0]],
                    inc=[(s_se, 1)])
                trn = b.op("scalar", lambda e, t=t: e.activation(
                    bcol(RNC, t), TCOL2[:, 0:1], AF.Exp, scale=-0.5),
                    inc=[(s_se, 1)])
                tmx = b.op("vector", lambda e: e.tensor_scalar(
                    TCOL2[:, 3:4], TCOL[:, 6:7], 1e-24, None, ALU.max),
                    wait=[tl2[0]], inc=[(s_ve, 1)])
                tsl = b.op("scalar", lambda e: e.activation(
                    TCOL2[:, 1:2], TCOL2[:, 3:4], AF.Ln, scale=1.0 / 1024.0),
                    wait=[tmx[0]], inc=[(s_se, 1)])
                tbf = b.op("scalar", lambda e, t=t: e.activation(
                    bcol(BFC, t), TCOL2[:, 1:2], AF.Exp, scale=-0.5),
                    inc=[(s_se, 1)])
                tbase = b.op("scalar", lambda e, t=t: e.activation(
                    TBF1[:], XN[:], AF.Identity, scale=bcol(BFC, t)),
                    inc=[(s_se, 1)])
                txnr = b.op("scalar", lambda e, t=t: e.activation(
                    TBF2[:], XN[:], AF.Identity, scale=bcol(RNC, t)),
                    wait=[trn[0]], inc=[(s_se, 1)])
                b.op("vector", lambda e, xt=xt: e.tensor_add(xt, xt, XN[:]),
                     wait=[txn[0]], inc=[(s_ve, 1)])
                srcdst = [(TBF1, BASET, tbase, 0), (TBF2, XNT, txnr, 1)]
            else:
                tcast = b.op("scalar", lambda e: e.copy(TBF2[:], XN[:]),
                             inc=[(s_se, 1)])
                srcdst = [(TBF2, XNT, tcast, 1)]

            for src, dst, tok, bank in srcdst:
                scaled = first and (dst is XNT)
                for dt in range(ND):
                    w0 = ([tok[0]] + ([b.ln_bankfree[bank]]
                                      if b.ln_bankfree[bank] else []))
                    tt = b.op("tensor", lambda e, src=src, dt=dt, bank=bank:
                              e.transpose(psb(bank, 128, off=dt * 128),
                                          src[:, dt * 128:(dt + 1) * 128], IDB[:]),
                              wait=(w0 if dt == 0 else []),
                              inc=[(s_te, 1)])
                if scaled:
                    for dt in range(ND):
                        tev = b.op("scalar",
                                   lambda e, dst=dst, dt=dt, t=t, bank=bank:
                                   e.activation(
                            dst[:, dt * T + t * 128: dt * T + (t + 1) * 128],
                            psb(bank, 128, off=dt * 128), AF.Identity,
                            scale=bcol(NWC, dt)),
                            wait=[tt[0]] if dt == 0 else [], inc=[(s_se, 1)])
                else:
                    tev = b.op("scalar", lambda e, dst=dst, t=t, bank=bank:
                               e.copy(
                        dst.rearrange("p (a q) -> p a q", a=ND)[
                            :, :, t * 128:(t + 1) * 128],
                        psb(bank, 1024).rearrange("p (a q) -> p a q", a=ND)),
                        wait=[tt[0]], inc=[(s_se, 1)])
                b.ln_bankfree[bank] = tev[0]

            if not first:
                prevt = None
                for dt in range(ND):
                    fb = 2 + (dt % 2)
                    tt3 = b.op("tensor", lambda e, dt=dt, fb=fb: e.transpose(
                        psf(fb, 128), XN[:, dt * 128:(dt + 1) * 128], IDF[:]),
                        wait=[txn[0]] + ([prevt] if prevt else []),
                        inc=[(s_te, 1)])
                    tev3 = b.op("scalar", lambda e, dt=dt, fb=fb: e.copy(
                        XC[:, dt * 128:(dt + 1) * 128], psf(fb, 128)),
                        wait=[tt3[0]], inc=[(s_se, 1)])
                    prevt = tev3[0]
                lg = psf(4, NEXP)
                for dt in range(ND):
                    b.op("tensor", lambda e, dt=dt, lg=lg: e.matmul(
                        lg, XC[:, dt * 128:(dt + 1) * 128],
                        RWF.rearrange("p (a c) -> p a c", a=ND)[:, dt],
                        start=(dt == 0), stop=False, skip_group_check=True),
                        wait=[prevt] if dt == 0 else [], inc=[(s_te, 1)])
                tbias = b.op("tensor", lambda e, lg=lg: e.matmul(
                    lg, ONES[:], RBROW[:], start=False, stop=True,
                    skip_group_check=True), inc=[(s_te, 1)])
                tnm2 = b.op("vector", lambda e, lg=lg: e.reduce_max(
                    TCOL2[:, 4:5], lg, axis=AX.X, negate=True),
                    wait=[tbias[0]], inc=[(s_ve, 1)])
                tz = b.op("scalar", lambda e, lg=lg: e.activation(
                    TCOL2[:, 8:16], lg, AF.Exp, bias=TCOL2[:, 4:5],
                    accum_out=TCOL2[:, 5:6]),
                    wait=[tnm2[0]], inc=[(s_se, 1)])
                tlz = b.op("scalar", lambda e: e.activation(
                    TCOL2[:, 6:7], TCOL2[:, 5:6], AF.Ln),
                    wait=[tz[0]], inc=[(s_se, 1)])
                tmp_ = b.op("scalar", lambda e: e.activation(
                    TCOL2[:, 7:8], TCOL2[:, 6:7], AF.Exp, scale=-1.0),
                    inc=[(s_se, 1)])
                pv = b.op("vector", lambda e, lg=lg: e.tensor_scalar(
                    TCOL[:, 0:8], lg, TCOL2[:, 4:5], 0.0, ALU.add, ALU.is_equal),
                    wait=[tz[0]], inc=[(s_ve, 1)])
                b.wait("tensor", [pv[0]])
                pv = b.op("vector", lambda e: e.memset(TCOL[:, 8:9], 0.0),
                          wait=[pv[0]], inc=[(s_ve, 1)])
                for j in range(1, NEXP):
                    pv = b.op("vector", lambda e, j=j: e.tensor_add(
                        TCOL[:, 8 + j:9 + j], TCOL[:, 7 + j:8 + j],
                        TCOL[:, j - 1:j]), wait=[pv[0]], inc=[(s_ve, 1)])
                pv = b.op("vector", lambda e: e.tensor_scalar(
                    TCOL2[:, 8:16], TCOL[:, 8:16], 0.0, None, ALU.is_equal),
                    wait=[pv[0]], inc=[(s_ve, 1)])
                pv = b.op("vector", lambda e: e.tensor_mul(
                    TCOL[:, 8:16], TCOL2[:, 8:16], TCOL[:, 0:8]),
                    wait=[pv[0]], inc=[(s_ve, 1)])
                b.op("vector", lambda e, t=t: e.tensor_scalar(
                    SELC[:, t * NEXP:(t + 1) * NEXP], TCOL[:, 8:16],
                    TCOL2[:, 7:8], None, ALU.mult),
                    wait=[tmp_[0], pv[0]], inc=[(s_ve, 1)])
        b.barrier()

        nm = "baset_b%d" % bi
        if first and tap(nm, (128, ND * T), BF16) is not None:
            b.op("sync", lambda e: e.dma_start(out=tap_handles[nm], in_=BASET[:]),
                 inc=[(s_out, 16)])
        nm2 = "xnt%d_b%d" % (which, bi)
        if tap(nm2, (128, ND * T), BF16) is not None:
            b.op("sync", lambda e: e.dma_start(out=tap_handles[nm2], in_=XNT[:]),
                 inc=[(s_out, 16)])
        if first and tap("s128_b%d" % bi, (128, NT)) is not None:
            b.op("sync", lambda e, bi=bi: e.dma_start(
                out=tap_handles["s128_b%d" % bi], in_=S128[:]), inc=[(s_out, 16)])
        if not first and tap("selc_b%d" % bi, (128, NT * NEXP)) is not None:
            b.op("sync", lambda e, bi=bi: e.dma_start(
                out=tap_handles["selc_b%d" % bi], in_=SELC[:]), inc=[(s_out, 16)])

    def offload_X(bi, dst=None):
        dst = scratch_x1 if dst is None else dst
        toks = []
        for t in range(NT):
            tk = b.op("sync", lambda e, t=t, dst=dst: e.dma_start(
                out=hbm_tiles(dst[bi])[t], in_=bcol(X, t, D)),
                inc=[(s_dma, 16)])
            toks.append(tk[0])
        return toks

    HT = arb(96 * KB, ND * T)
    OWT = arb(128 * KB, ND * 1024)
    SA = arf(144 * KB, T)
    SU = arf(152 * KB, T)
    SH = arf(0 * KB, T)
    SCc = arf(8 * KB, T)
    SD = arf(16 * KB, T)
    WROW = arf(24 * KB, 2048)
    WT4 = arb(128 * KB, 4 * ND * 128)

    def wt(m, dt):
        return WT4[:, (m * ND + dt) * 128:(m * ND + dt + 1) * 128]

    def stage_ssm(bi):
        mats = ["L0_attn_A_w", "L0_attn_B_w", "L0_attn_C_w", "L0_attn_D_w"]
        last_wrow_consumer = None
        evs = []
        mevs = {}
        scan_tok = None
        for nt in range(ND):
            prev_evs = list(evs)
            tl1 = b.op("sync", lambda e, nt=nt: e.dma_start(
                out=WROW[:, 0:1024],
                in_=g[mats[0]][nt * 128:(nt + 1) * 128, :]),
                wait=[last_wrow_consumer] if last_wrow_consumer else [],
                inc=[(s_dma, 16)])
            tl2 = b.op("sync", lambda e, nt=nt: e.dma_start(
                out=WROW[:, 1024:2048],
                in_=g[mats[1]][nt * 128:(nt + 1) * 128, :]), inc=[(s_dma, 16)])
            trans_done = []
            for m in range(4):
                if m == 2:
                    tl1 = b.op("sync", lambda e, nt=nt: e.dma_start(
                        out=WROW[:, 0:1024],
                        in_=g[mats[2]][nt * 128:(nt + 1) * 128, :]),
                        wait=trans_done[-2:], inc=[(s_dma, 16)])
                    tl2 = b.op("sync", lambda e, nt=nt: e.dma_start(
                        out=WROW[:, 1024:2048],
                        in_=g[mats[3]][nt * 128:(nt + 1) * 128, :]),
                        inc=[(s_dma, 16)])
                src_off = (m % 2) * 1024
                dtok = tl1 if m % 2 == 0 else tl2
                tb0 = 2 if m % 2 == 0 else 6
                w0 = ([dtok[0]] + mevs.get(("bank", tb0), []) +
                      mevs.get(("bank", tb0 + 1), []) + prev_evs)
                for dt in range(ND):
                    bank = tb0 + dt // 4
                    off = (dt % 4) * 128
                    tt = b.op("tensor",
                              lambda e, src_off=src_off, dt=dt, bank=bank,
                              off=off: e.transpose(
                                  psf(bank, 128, off=off),
                                  WROW[:, src_off + dt * 128:
                                       src_off + (dt + 1) * 128], IDF[:]),
                              wait=(w0 if dt == 0 else []),
                              inc=[(s_te, 1)])
                tev = b.op("scalar", lambda e, m=m, tb0=tb0: e.copy(
                    WT4.rearrange("p (a q) -> p a q", a=4 * ND)[
                        :, m * ND:(m + 1) * ND],
                    PSF[:, tb0 * 512:(tb0 + 2) * 512].rearrange(
                        "p (a q) -> p a q", a=ND)),
                    wait=[tt[0]], inc=[(s_se, 1)])
                mevs[("bank", tb0)] = [tev[0]]
                mevs[("bank", tb0 + 1)] = [tev[0]]
                trans_done.append(tev[0])
                b.wait("tensor", [tev[0]])
            last_wrow_consumer = trans_done[-1]
            wt_ready = trans_done[-1]

            order = [0, 2, 3, 1]
            quad = {0: 0, 2: 4, 3: 0, 1: 4}
            gstop = {}
            ev_quad = {0: evs, 4: evs}
            sa_ev = sc_ev = sd_ev = None
            u_evs = []
            for m in order:
                qb = quad[m]
                for dt in range(ND):
                    for tc in range(4):
                        tmm = b.op("tensor",
                                   lambda e, m=m, dt=dt, tc=tc, qb=qb: e.matmul(
                                       psf(qb + tc, 512), wt(m, dt),
                                       XNT[:, dt * T + tc * 512:
                                           dt * T + (tc + 1) * 512],
                                       start=(dt == 0), stop=(dt == ND - 1),
                                       skip_group_check=True),
                                   wait=(([wt_ready] + list(ev_quad[qb]))
                                         if dt == 0 and tc == 0 else []),
                                   inc=[(s_te, 1)])
                gstop[m] = tmm[0]
                if m == 0:
                    toks = []
                    for tc in range(4):
                        ta_ = b.op("scalar", lambda e, tc=tc, nt=nt, qb=qb:
                                   e.activation(
                            SA[:, tc * 512:(tc + 1) * 512], psf(qb + tc, 512),
                            AF.Sigmoid, bias=bcol(ABC, nt)),
                            wait=[tmm[0]] if tc == 0 else [], inc=[(s_se, 1)])
                        toks.append(ta_[0])
                    sa_ev = toks
                    ev_quad[qb] = toks
                elif m == 2:
                    toks = []
                    for tc in range(4):
                        tc_ = b.op("scalar", lambda e, tc=tc, qb=qb:
                                   e.activation(
                            SCc[:, tc * 512:(tc + 1) * 512], psf(qb + tc, 512),
                            AF.Sigmoid),
                            wait=[tmm[0]] if tc == 0 else [], inc=[(s_se, 1)])
                        toks.append(tc_[0])
                    sc_ev = toks
                    ev_quad[qb] = toks
                elif m == 3:
                    toks = []
                    for tc in range(4):
                        td_ = b.op("scalar", lambda e, tc=tc, qb=qb:
                                   e.activation(
                            SD[:, tc * 512:(tc + 1) * 512], psf(qb + tc, 512),
                            AF.Sigmoid),
                            wait=[tmm[0]] if tc == 0 else [], inc=[(s_se, 1)])
                        toks.append(td_[0])
                    sd_ev = toks
                    ev_quad[qb] = toks
                else:
                    toks = []
                    for tc in range(4):
                        tu_ = b.op("vector", lambda e, tc=tc, qb=qb:
                                   e.tensor_mul(
                            SU[:, tc * 512:(tc + 1) * 512],
                            SD[:, tc * 512:(tc + 1) * 512], psf(qb + tc, 512)),
                            wait=([tmm[0], sd_ev[-1]] if tc == 0 else []),
                            inc=[(s_ve, 1)])
                        toks.append(tu_[0])
                    u_evs = toks
                    ev_quad[qb] = toks
            tsc = b.op("vector", lambda e: e.tensor_tensor_scan(
                SH[:], SA[:], SU[:], 0.0, ALU.mult, ALU.add),
                wait=[sa_ev[-1], u_evs[-1]], inc=[(s_ve, 1)])
            thc = b.op("vector", lambda e, nt=nt: e.tensor_mul(
                arb(96 * KB + nt * 2 * T, T), SCc[:], SH[:]),
                wait=[tsc[0], sc_ev[-1]], inc=[(s_ve, 1)])
            evs = [sa_ev[-1], sc_ev[-1], sd_ev[-1], u_evs[-1], tsc[0], thc[0]]

        if tap("ht_b%d" % bi, (128, ND * T), BF16) is not None:
            b.barrier()
            b.op("sync", lambda e, bi=bi: e.dma_start(
                out=tap_handles["ht_b%d" % bi], in_=HT), inc=[(s_out, 16)])
        if tap("sascu_b%d" % bi, (128, 4 * T)) is not None:
            b.barrier()
            th_ = tap_handles["sascu_b%d" % bi]
            b.op("sync", lambda e, th_=th_: e.dma_start(out=th_[:, 0:T], in_=SA),
                 inc=[(s_out, 16)])
            b.op("sync", lambda e, th_=th_: e.dma_start(out=th_[:, T:2 * T], in_=SU),
                 inc=[(s_out, 16)])
            b.op("sync", lambda e, th_=th_: e.dma_start(out=th_[:, 2 * T:3 * T], in_=SH),
                 inc=[(s_out, 16)])
            b.op("sync", lambda e, th_=th_: e.dma_start(out=th_[:, 3 * T:4 * T], in_=SCc),
                 inc=[(s_out, 16)])

        prev = last_wrow_consumer
        first_extra = list(evs)
        for dtt in range(ND):
            tl = b.op("sync", lambda e, dtt=dtt: e.dma_start(
                out=WROW[:, 0:1024],
                in_=g["L0_attn_out_w"][dtt * 128:(dtt + 1) * 128, :]),
                wait=([prev] if prev else []) + (first_extra if dtt == 0 else []),
                inc=[(s_dma, 16)])
            first_extra = []
            for ntt in range(ND):
                bank = 2 + ntt // 4
                off = (ntt % 4) * 128
                tt = b.op("tensor", lambda e, ntt=ntt, bank=bank, off=off:
                          e.transpose(psf(bank, 128, off=off),
                                      WROW[:, ntt * 128:(ntt + 1) * 128],
                                      IDF[:]),
                          wait=([tl[0]] + ([prev] if prev else []))
                          if ntt == 0 else [],
                          inc=[(s_te, 1)])
            tev = b.op("scalar", lambda e, dtt=dtt: e.copy(
                OWT.rearrange("p (a q) -> p a q", a=ND)[:, :,
                                                        dtt * 128:(dtt + 1) * 128],
                PSF[:, 2 * 512: 4 * 512].rearrange("p (a q) -> p a q", a=ND)),
                wait=[tt[0]], inc=[(s_se, 1)])
            prev = tev[0]
            b.wait("tensor", [prev])
        b.barrier()

    def stage_yadd(bi):
        XSTG = [arf(0 * KB, D), arf(4 * KB, D)]
        slot_free = [None, None]
        prev_add = {}
        for t in range(NT):
            sl = t % 2
            tk = b.op("sync", lambda e, t=t, sl=sl: e.dma_start(
                out=XSTG[sl][:], in_=hbm_tiles(scratch_x1[bi])[t]),
                wait=[slot_free[sl]] if slot_free[sl] else [],
                inc=[(s_dma, 16)])
            adds = []
            for dc in range(2):
                bank = (t * 2 + dc) % 4
                for ntt in range(ND):
                    b.op("tensor", lambda e, t=t, dc=dc, ntt=ntt, bank=bank:
                         e.matmul(psf(bank, 512),
                                  HT[:, ntt * T + t * 128: ntt * T + (t + 1) * 128],
                                  OWT[:, ntt * 1024 + dc * 512:
                                      ntt * 1024 + (dc + 1) * 512],
                                  start=(ntt == 0), stop=False,
                                  skip_group_check=True),
                         wait=([prev_add[bank]] if ntt == 0 and bank in prev_add
                               else []),
                         inc=[(s_te, 1)])
                tb_ = b.op("tensor", lambda e, dc=dc, bank=bank: e.matmul(
                    psf(bank, 512), ONESB[:],
                    ROWSB[0:1, dc * 512:(dc + 1) * 512],
                    start=False, stop=True, skip_group_check=True),
                    inc=[(s_te, 1)])
                tadd = b.op("vector", lambda e, sl=sl, dc=dc, bank=bank:
                            e.tensor_add(
                                XSTG[sl][:, dc * 512:(dc + 1) * 512],
                                XSTG[sl][:, dc * 512:(dc + 1) * 512],
                                psf(bank, 512)),
                            wait=[tb_[0], tk[0]], inc=[(s_ve, 1)])
                prev_add[bank] = tadd[0]
                adds.append(tadd[0])
            tw = b.op("sync", lambda e, t=t, sl=sl: e.dma_start(
                out=hbm_tiles(scratch_x1[bi])[t], in_=XSTG[sl][:]),
                wait=adds, inc=[(s_dma, 16)])
            slot_free[sl] = tw[0]
        b.barrier()

    def routing_steps(bi):
        steps = []
        VV = RT[:, 0:NT]
        TMP = RT[:, 16:32]
        BC1 = RT[:, 32:33]
        BC2 = RT[:, 33:34]
        PC = RT[:, 34:35]
        EQM = RT[:, 35:51]
        ch = {"t": None}
        SEMS = {"vector": s_ve, "scalar": s_se, "tensor": s_te, "sync": s_out}

        def rop(eng, fn, extra=()):
            tok = b.op(eng, fn,
                       wait=([ch["t"]] if ch["t"] else []) + list(extra),
                       inc=[(SEMS[eng], 1)] if eng != "sync" else [(s_out, 16)])
            ch["t"] = tok[0]
            return tok[0]

        def rpart_reduce(src_col, dst_col, alu):
            rop("tensor", lambda e, src_col=src_col: e.transpose(
                psf(5, 128)[0:1, :], src_col, IDF[:]))
            rop("scalar", lambda e: e.copy(ROW128[:], psf(5, 128)[0:1, :]))
            b.wait("tensor", [ch["t"]])
            rop("vector", lambda e, dst_col=dst_col, alu=alu: e.tensor_reduce(
                RSC[0:1, dst_col:dst_col + 1], ROW128[:], axis=AX.X, op=alu))

        def rbcast(src_off, n, dst):
            rop("tensor", lambda e, src_off=src_off, n=n: e.matmul(
                psf(5, n), ONES[:], RSC[0:1, src_off:src_off + n],
                start=True, stop=True))
            rop("scalar", lambda e, dst=dst, n=n: e.copy(dst, psf(5, n)))
            b.wait("tensor", [ch["t"]])

        def s_init():
            rop("vector", lambda e: e.reduce_max(PC, S128[:], axis=AX.X))
            rpart_reduce(PC, 2, ALU.max)
            rop("vector", lambda e: e.memset(RSC[0:1, 0:1], 0.5753641128540039))
        steps.append(s_init)

        def cd_iter(_it):
            rop("vector", lambda e: e.tensor_tensor(
                RSC[0:1, 9:10], RSC[0:1, 2:3], RSC[0:1, 0:1], ALU.min))
            rop("vector", lambda e: e.tensor_scalar(
                RSC[0:1, 1:2], RSC[0:1, 9:10], -1.0, None, ALU.mult))
            rbcast(0, 2, RT[:, 32:34])
            rop("vector", lambda e: e.tensor_scalar(
                TMP, S128[:], BC1, BC2, ALU.min, ALU.add))
            rop("scalar", lambda e: e.activation(
                TMP, TMP, AF.Exp, accum_out=PC))
            rpart_reduce(PC, 3, ALU.add)
            rop("scalar", lambda e: e.activation(
                RSC[0:1, 3:4], RSC[0:1, 3:4], AF.Ln))
            rop("vector", lambda e: e.tensor_tensor(
                RSC[0:1, 9:10], RSC[0:1, 3:4], RSC[0:1, 1:2], ALU.subtract))
            rop("vector", lambda e: e.tensor_scalar(
                RSC[0:1, 0:1], RSC[0:1, 9:10], -LOGK, None, ALU.add))
        for it in range(NITER_CD - 1):
            steps.append(lambda it=it: cd_iter(it))

        def v_init():
            rop("vector", lambda e: e.tensor_scalar(
                RSC[0:1, 9:10], RSC[0:1, 0:1], -1.0, None, ALU.mult))
            rbcast(9, 1, RT[:, 32:33])
            rop("vector", lambda e: e.tensor_scalar(
                VV, S128[:], BC1, 0.0, ALU.add, ALU.min))
            rop("vector", lambda e: e.tensor_reduce(
                PC, VV, axis=AX.X, op=ALU.min))
            rpart_reduce(PC, 9, ALU.min)
            rop("vector", lambda e: e.tensor_scalar(
                RSC[0:1, 5:6], RSC[0:1, 9:10], -1.0, None, ALU.add))
            rop("vector", lambda e: e.memset(RSC[0:1, 6:7], 1.0))
            rop("vector", lambda e: e.memset(RSC[0:1, 12:13], 1024.0))
        steps.append(v_init)

        def count_vv(thr_col, cmp_op, out_col):
            rbcast(thr_col, 1, RT[:, 32:33])
            rop("vector", lambda e, cmp_op=cmp_op: e.tensor_scalar(
                TMP, VV, BC1, None, cmp_op))
            rop("vector", lambda e: e.reduce_sum(PC, TMP, axis=AX.X))
            rpart_reduce(PC, out_col, ALU.add)

        def count_ix():
            rbcast(15, 1, RT[:, 32:33])
            rop("vector", lambda e: e.tensor_scalar(
                TMP, IOTA[:], BC1, None, ALU.is_le))
            rop("vector", lambda e: e.tensor_mul(TMP, TMP, EQM))
            rop("vector", lambda e: e.reduce_sum(PC, TMP, axis=AX.X))
            rpart_reduce(PC, 4, ALU.add)

        def bis_iter(locol, hicol, midcol, cntfn, invert):
            rop("vector", lambda e: e.tensor_tensor(
                RSC[0:1, midcol:midcol + 1], RSC[0:1, locol:locol + 1],
                RSC[0:1, hicol:hicol + 1], ALU.add))
            rop("vector", lambda e: e.tensor_scalar(
                RSC[0:1, midcol:midcol + 1], RSC[0:1, midcol:midcol + 1],
                0.5, None, ALU.mult))
            cntfn()
            cmp_op = ALU.is_lt if invert else ALU.is_ge
            rop("vector", lambda e, cmp_op=cmp_op: e.tensor_tensor(
                RSC[0:1, 8:9], RSC[0:1, 4:5], RSC[0:1, 12:13], cmp_op))
            rop("vector", lambda e: e.tensor_tensor(
                RSC[0:1, 9:10], RSC[0:1, midcol:midcol + 1],
                RSC[0:1, locol:locol + 1], ALU.subtract))
            rop("vector", lambda e: e.tensor_mul(
                RSC[0:1, 9:10], RSC[0:1, 9:10], RSC[0:1, 8:9]))
            rop("vector", lambda e: e.tensor_add(
                RSC[0:1, locol:locol + 1], RSC[0:1, locol:locol + 1],
                RSC[0:1, 9:10]))
            rop("vector", lambda e: e.tensor_scalar(
                RSC[0:1, 8:9], RSC[0:1, 8:9], -1.0, 1.0, ALU.mult, ALU.add))
            rop("vector", lambda e: e.tensor_tensor(
                RSC[0:1, 9:10], RSC[0:1, midcol:midcol + 1],
                RSC[0:1, hicol:hicol + 1], ALU.subtract))
            rop("vector", lambda e: e.tensor_mul(
                RSC[0:1, 9:10], RSC[0:1, 9:10], RSC[0:1, 8:9]))
            rop("vector", lambda e: e.tensor_add(
                RSC[0:1, hicol:hicol + 1], RSC[0:1, hicol:hicol + 1],
                RSC[0:1, 9:10]))

        for it in range(NITER_TH):
            steps.append(lambda it=it: bis_iter(
                5, 6, 7, lambda: count_vv(7, ALU.is_gt, 4), False))

        def theta_final():
            rbcast(6, 1, RT[:, 32:33])
            rop("vector", lambda e: e.tensor_scalar(
                TMP, VV, BC1, None, ALU.is_le))
            rop("vector", lambda e: e.tensor_scalar(
                EQM, TMP, -1.0, 1.0, ALU.mult, ALU.add))
            rop("vector", lambda e: e.tensor_scalar(
                EQM, EQM, -1e30, None, ALU.mult))
            rop("vector", lambda e: e.tensor_mul(TMP, TMP, VV))
            rop("vector", lambda e: e.tensor_add(TMP, TMP, EQM))
            rop("vector", lambda e: e.reduce_max(PC, TMP, axis=AX.X))
            rpart_reduce(PC, 10, ALU.max)
            count_vv(10, ALU.is_gt, 11)
            rop("vector", lambda e: e.tensor_scalar(
                RSC[0:1, 12:13], RSC[0:1, 11:12], -1.0, 1024.0, ALU.mult,
                ALU.add))
            rop("vector", lambda e: e.tensor_scalar(
                EQM, VV, BC1, None, ALU.is_equal))
            rop("vector", lambda e: e.memset(RSC[0:1, 13:14], -1.0))
            rop("vector", lambda e: e.memset(RSC[0:1, 14:15], 2047.0))
        steps.append(theta_final)

        for it in range(NITER_IX):
            steps.append(lambda it=it: bis_iter(13, 14, 15, count_ix, True))

        def mask_final():
            rbcast(10, 1, RT[:, 32:33])
            rop("vector", lambda e: e.tensor_scalar(
                TMP, VV, BC1, None, ALU.is_gt))
            rbcast(14, 1, RT[:, 33:34])
            rop("vector", lambda e: e.tensor_scalar(
                MASKC[:], IOTA[:], BC2, None, ALU.is_le))
            rop("vector", lambda e: e.tensor_mul(MASKC[:], MASKC[:], EQM))
            rop("vector", lambda e: e.tensor_add(MASKC[:], MASKC[:], TMP))
            if tap("maskc_b%d" % bi, (128, NT)) is not None:
                rop("sync", lambda e, bi=bi: e.dma_start(
                    out=tap_handles["maskc_b%d" % bi], in_=MASKC[:]))
            if tap("vsel_b%d" % bi, (128, NT + 2)) is not None:
                th_ = tap_handles["vsel_b%d" % bi]
                rop("sync", lambda e, th_=th_: e.dma_start(
                    out=th_[:, 0:NT], in_=VV))
                rop("sync", lambda e, th_=th_: e.dma_start(
                    out=th_[0:1, NT:NT + 2], in_=RSC[0:1, 10:12]))
        steps.append(mask_final)
        return steps

    WQL = arb(0 * KB, ND * 512)
    WKVL = arb(8 * KB, ND * 1024)
    WOL = arb(24 * KB, 4 * 1024)
    OT = arb(32 * KB, 4 * T)
    QTW = arb(48 * KB, 512)
    KTW = arb(49 * KB, 512)
    VW = arb(50 * KB, 512)
    EW = arb(51 * KB, 128)
    ETW = arb(52 * KB, 128)
    OLW = arb(53 * KB, 512)
    ZL8 = arf(54 * KB, 8)
    RZ8 = arf(55 * KB, 8)
    WSTG = arf(56 * KB, 1024)

    def load_wcast(dst, src_ap, cols, scale_col=None, wait=(), tag=0):
        tl = b.op("sync", lambda e, src_ap=src_ap, cols=cols: e.dma_start(
            out=WSTG[:, 0:cols], in_=src_ap), wait=list(wait), inc=[(s_dma, 16)])
        if scale_col is None:
            tc = b.op("scalar", lambda e, dst=dst, cols=cols: e.copy(
                dst, WSTG[:, 0:cols]), wait=[tl[0]], inc=[(s_se, 1)])
        else:
            tc = b.op("scalar", lambda e, dst=dst, cols=cols,
                      scale_col=scale_col: e.activation(
                dst, WSTG[:, 0:cols], AF.Identity, scale=scale_col),
                wait=[tl[0]], inc=[(s_se, 1)])
        return tc[0]

    def stage_light(bi, rsteps):
        xload = []
        for t in range(NT):
            tk = b.op("sync", lambda e, t=t: e.dma_start(
                out=bcol(X, t, D), in_=hbm_tiles(scratch_x1[bi])[t]),
                inc=[(s_dma, 16)])
            xload.append(tk[0])
        b.xload_light = xload
        wtoks = []
        prev = None
        for dt in range(ND):
            prev = load_wcast(bcol(WQL, dt, 512),
                              g["L0_ffn_wq_light"][dt * 128:(dt + 1) * 128, :],
                              512, scale_col=bcol(GLC, dt),
                              wait=[prev] if prev else [])
            wtoks.append(prev)
        for dt in range(ND):
            prev = load_wcast(bcol(WKVL, dt, 1024),
                              g["L0_ffn_wkv_light"][dt * 128:(dt + 1) * 128, :],
                              1024, scale_col=bcol(GLC, dt), wait=[prev])
            wtoks.append(prev)
        for ht in range(4):
            prev = load_wcast(bcol(WOL, ht, 1024),
                              g["L0_ffn_wo_light"][ht * 128:(ht + 1) * 128, :],
                              1024, wait=[prev])
            wtoks.append(prev)
        wq_ready = wtoks[ND - 1]
        wkv_ready = wtoks[2 * ND - 1]
        wo_ready = wtoks[-1]

        rs_i = 0
        prev_otev = None
        for t in range(NT):
            n_per = (len(rsteps) + NT - 1) // NT
            for _ in range(n_per):
                if rs_i < len(rsteps):
                    rsteps[rs_i]()
                    rs_i += 1
            evq = []
            for mj in range(4):
                bank = mj % 2
                for dt in range(ND):
                    tmm = b.op("tensor", lambda e, mj=mj, dt=dt, t=t, bank=bank:
                               e.matmul(psf(bank, 128),
                                        WQL[:, dt * 512 + mj * 128:
                                            dt * 512 + (mj + 1) * 128],
                                        BASET[:, dt * T + t * 128:
                                              dt * T + (t + 1) * 128],
                                        start=(dt == 0), stop=(dt == ND - 1),
                                        skip_group_check=True),
                               wait=([wq_ready] if dt == 0 and t == 0 and mj < 2
                                     else ([evq[mj - 2]] if dt == 0 and mj >= 2
                                           else ([prev_otev] if dt == 0 else []))),
                               inc=[(s_te, 1)])
                tev = b.op("scalar", lambda e, mj=mj, bank=bank: e.activation(
                    bcol(QTW, mj, 128), psf(bank, 128), AF.Copy, scale=0.125),
                    wait=[tmm[0]], inc=[(s_se, 1)])
                evq.append(tev[0])
            evk = []
            for mj in range(4):
                bank = 2 + mj % 2
                for dt in range(ND):
                    tmm = b.op("tensor", lambda e, mj=mj, dt=dt, t=t, bank=bank:
                               e.matmul(psf(bank, 128),
                                        WKVL[:, dt * 1024 + mj * 128:
                                            dt * 1024 + (mj + 1) * 128],
                                        BASET[:, dt * T + t * 128:
                                              dt * T + (t + 1) * 128],
                                        start=(dt == 0), stop=(dt == ND - 1),
                                        skip_group_check=True),
                               wait=([wkv_ready] if dt == 0 and t == 0 and mj < 2
                                     else ([evk[mj - 2]] if dt == 0 and mj >= 2
                                           else [])),
                               inc=[(s_te, 1)])
                tev = b.op("scalar", lambda e, mj=mj, bank=bank: e.copy(
                    bcol(KTW, mj, 128), psf(bank, 128)),
                    wait=[tmm[0]], inc=[(s_se, 1)])
                evk.append(tev[0])
            for dt in range(ND):
                tmm = b.op("tensor", lambda e, dt=dt, t=t: e.matmul(
                    psf(4, 512),
                    BASET[:, dt * T + t * 128: dt * T + (t + 1) * 128],
                    WKVL[:, dt * 1024 + 512: dt * 1024 + 1024],
                    start=(dt == 0), stop=(dt == ND - 1),
                    skip_group_check=True),
                    wait=([evk[-1]] if dt == 0 and t == 0 else []),
                    inc=[(s_te, 1)])
            tvev = b.op("scalar", lambda e: e.copy(VW[:], psf(4, 512)),
                        wait=[tmm[0]], inc=[(s_se, 1)])
            b.wait("tensor", [tvev[0]])

            zevs = []
            oev = None
            for h in range(8):
                mj = h // 2
                pb_ = (h % 2) * 64
                tsc_ = b.op("tensor", lambda e, mj=mj, pb_=pb_: e.matmul(
                    psf(2, 128),
                    QTW[pb_:pb_ + 64, mj * 128:(mj + 1) * 128],
                    KTW[pb_:pb_ + 64, mj * 128:(mj + 1) * 128],
                    start=True, stop=True),
                    wait=[evq[mj], evk[mj]] + ([zevs[-1]] if zevs else []),
                    inc=[(s_te, 1)])
                tnm_ = b.op("vector", lambda e: e.reduce_max(
                    TCOL[:, 8:9], psf(2, 128), axis=AX.X, negate=True),
                    wait=[tsc_[0]], inc=[(s_ve, 1)])
                tex_ = b.op("scalar", lambda e, h=h: e.activation(
                    EW[:], psf(2, 128), AF.Exp, bias=TCOL[:, 8:9],
                    accum_out=bcol(ZL8, h)), wait=[tnm_[0]], inc=[(s_se, 1)])
                ttr_ = b.op("tensor", lambda e: e.transpose(
                    psb(0, 128), EW[:], IDB[:]),
                    wait=[tex_[0]], inc=[(s_te, 1)])
                tet_ = b.op("scalar", lambda e: e.copy(ETW[:], psb(0, 128)),
                            wait=[ttr_[0]], inc=[(s_se, 1)])
                tav_ = b.op("tensor", lambda e, h=h: e.matmul(
                    psf(3, 512)[:, h * 64:(h + 1) * 64], ETW[:],
                    VW[:, h * 64:(h + 1) * 64],
                    start=(h == 0), stop=(h == 7), skip_group_check=True),
                    wait=[tet_[0]] + ([oev] if h == 0 and oev else []),
                    inc=[(s_te, 1)])
                zevs.append(tex_[0])
            tlz = b.op("scalar", lambda e: e.activation(
                RZ8[:, 0:8], ZL8[:, 0:8], AF.Ln), wait=zevs, inc=[(s_se, 1)])
            trz = b.op("scalar", lambda e: e.activation(
                RZ8[:, 0:8], RZ8[:, 0:8], AF.Exp, scale=-1.0), inc=[(s_se, 1)])
            prev_t = None
            for h in range(8):
                tevo = b.op("scalar", lambda e, h=h: e.activation(
                    bcol(OLW, h, 64), psf(3, 512)[:, h * 64:(h + 1) * 64],
                    AF.Identity, scale=bcol(RZ8, h)),
                    wait=[tav_[0]] if h == 0 else [], inc=[(s_se, 1)])
                tto = b.op("tensor", lambda e, h=h: e.transpose(
                    psb(1, 128)[0:64, :], bcol(OLW, h, 64), IDB[:]),
                    wait=[tevo[0]] + ([prev_t] if prev_t else []),
                    inc=[(s_te, 1)])
                tote = b.op("scalar", lambda e, h=h, t=t: e.copy(
                    OT[(h % 2) * 64:(h % 2) * 64 + 64,
                       (h // 2) * T + t * 128:(h // 2) * T + (t + 1) * 128],
                    psb(1, 128)[0:64, 0:128]),
                    wait=[tto[0]], inc=[(s_se, 1)])
                prev_t = tote[0]
            oev = prev_t
            prev_otev = prev_t

            for dc in range(2):
                bank = dc
                for ht in range(4):
                    tmm = b.op("tensor", lambda e, ht=ht, dc=dc, t=t, bank=bank:
                               e.matmul(psf(bank, 512),
                                        OT[:, ht * T + t * 128:
                                           ht * T + (t + 1) * 128],
                                        WOL[:, ht * 1024 + dc * 512:
                                            ht * 1024 + (dc + 1) * 512],
                                        start=(ht == 0), stop=(ht == 3),
                                        skip_group_check=True),
                               wait=([prev_t, wo_ready] if ht == 0 else []),
                               inc=[(s_te, 1)])
                tadd = b.op("vector", lambda e, t=t, dc=dc, bank=bank:
                            e.tensor_add(
                                X[:, t * D + dc * 512: t * D + (dc + 1) * 512],
                                X[:, t * D + dc * 512: t * D + (dc + 1) * 512],
                                psf(bank, 512)),
                            wait=[tmm[0], xload[t]], inc=[(s_ve, 1)])
                b.wait("tensor", [tadd[0]])
        while rs_i < len(rsteps):
            rsteps[rs_i]()
            rs_i += 1
        b.barrier()

    WQH = arb(0 * KB, ND * 512)
    WKVTH = arb(8 * KB, ND * 1024)
    ETH = arb(24 * KB, NT * 256)
    OTH = arb(32 * KB, 4 * T)
    WSTGH = arf(48 * KB, 1024)
    QTM0 = arb(52 * KB, T)
    QTM1 = arb(56 * KB, T)
    KTO = arb(60 * KB, T)
    VH = arb(48 * KB, NT * 66)
    OHS = arb(50 * KB + 512, 64)
    MRW = arf(51 * KB, 128)

    def stage_heavy(bi):
        def load_wcast_h(dst, src_ap, cols, scale_col=None, wait=()):
            tl = b.op("sync", lambda e, src_ap=src_ap, cols=cols: e.dma_start(
                out=WSTGH[:, 0:cols], in_=src_ap), wait=list(wait),
                inc=[(s_dma, 16)])
            if scale_col is None:
                tc = b.op("scalar", lambda e, dst=dst, cols=cols: e.copy(
                    dst, WSTGH[:, 0:cols]), wait=[tl[0]], inc=[(s_se, 1)])
            else:
                tc = b.op("scalar", lambda e, dst=dst, cols=cols,
                          scale_col=scale_col: e.activation(
                    dst, WSTGH[:, 0:cols], AF.Identity, scale=scale_col),
                    wait=[tl[0]], inc=[(s_se, 1)])
            return tc[0]

        prev = None
        for dt in range(ND):
            prev = load_wcast_h(bcol(WQH, dt, 512),
                                g["L0_ffn_wq_heavy"][dt * 128:(dt + 1) * 128, :],
                                512, scale_col=bcol(GHC, dt),
                                wait=[prev] if prev else [])
        for rt_ in range(ND):
            tl = b.op("sync", lambda e, rt_=rt_: e.dma_start(
                out=WSTGH[:, 0:1024],
                in_=g["L0_ffn_wkv_heavy"][rt_ * 128:(rt_ + 1) * 128, :]),
                wait=[prev], inc=[(s_dma, 16)])
            for dt in range(ND):
                fb = 4 + (dt % 2)
                tt = b.op("tensor", lambda e, dt=dt, fb=fb: e.transpose(
                    psf(fb, 128), WSTGH[:, dt * 128:(dt + 1) * 128], IDF[:]),
                    wait=[tl[0]] + ([prev] if prev else []), inc=[(s_te, 1)])
                tev = b.op("scalar", lambda e, dt=dt, rt_=rt_, fb=fb:
                           e.activation(
                    WKVTH[:, dt * 1024 + rt_ * 128: dt * 1024 + (rt_ + 1) * 128],
                    psf(fb, 128), AF.Identity, scale=bcol(GHC, dt)),
                    wait=[tt[0]], inc=[(s_se, 1)])
                prev = tev[0]
            b.wait("tensor", [prev])
        wall = prev

        tko = b.op("gpsimd", lambda e: e.memset(KTO[:], 1.0),
                   wait=[wall], inc=[(s_gp, 1)])
        tvo0 = b.op("gpsimd", lambda e: e.memset(VH[:], 1.0),
                    wait=[tko[0]], inc=[(s_gp, 1)])

        QTMs = [QTM0, QTM1]
        qtm_free = [None, None]
        kto_free = None
        vh_free = None
        last_ot = None
        NQC = 8
        for j in range(4):
            pevs = []
            for tc in range(4):
                bank = tc % 2
                for dt in range(ND):
                    tmm = b.op("tensor", lambda e, j=j, dt=dt, tc=tc, bank=bank:
                               e.matmul(psf(bank, 512),
                                        WQH[:, dt * 512 + j * 128:
                                            dt * 512 + (j + 1) * 128],
                                        BASET[:, dt * T + tc * 512:
                                              dt * T + (tc + 1) * 512],
                                        start=(dt == 0), stop=(dt == ND - 1),
                                        skip_group_check=True),
                               wait=(([wall] + [x for x in qtm_free if x] +
                                      (pevs[-2:] if pevs else []))
                                     if dt == 0 else []),
                               inc=[(s_te, 1)])
                ev2 = []
                for hh in range(2):
                    tev = b.op("scalar",
                               lambda e, tc=tc, hh=hh, bank=bank: e.activation(
                        QTMs[hh][hh * 0 + 0:64, tc * 512:(tc + 1) * 512]
                        if False else
                        QTMs[hh].rearrange("p f -> p f")[0:64,
                                                        tc * 512:(tc + 1) * 512],
                        psf(bank, 512)[hh * 64:hh * 64 + 64, :],
                        AF.Copy, scale=0.125),
                        wait=[tmm[0]], inc=[(s_se, 1)])
                    ev2.append(tev[0])
                pevs.append(ev2[-1])
            b.wait("tensor", [pevs[-1]])
            qproj_done = pevs[-1]

            for hh in range(2):
                h = 2 * j + hh
                QTM = QTMs[hh]
                kevs = []
                for tc in range(4):
                    for dt in range(ND):
                        tmm = b.op("tensor",
                                   lambda e, h=h, dt=dt, tc=tc: e.matmul(
                            psf(2, 512)[0:64, :],
                            WKVTH[:, dt * 1024 + h * 128:
                                  dt * 1024 + h * 128 + 64],
                            BASET[:, dt * T + tc * 512:
                                  dt * T + (tc + 1) * 512],
                            start=(dt == 0), stop=(dt == ND - 1),
                            skip_group_check=True),
                            wait=(([kevs[-1]] if kevs else
                                   [tko[0]] + ([kto_free] if kto_free else []))
                                  if dt == 0 else []),
                            inc=[(s_te, 1)])
                    tev = b.op("scalar", lambda e, tc=tc: e.copy(
                        KTO[0:64, tc * 512:(tc + 1) * 512],
                        psf(2, 512)[0:64, :]), wait=[tmm[0]], inc=[(s_se, 1)])
                    kevs.append(tev[0])
                vevs = []
                for kt in range(NT):
                    for dt in range(ND):
                        tmm = b.op("tensor",
                                   lambda e, h=h, dt=dt, kt=kt: e.matmul(
                            psf(3, 64),
                            BASET[:, dt * T + kt * 128: dt * T + (kt + 1) * 128],
                            WKVTH[:, dt * 1024 + h * 128 + 64:
                                  dt * 1024 + h * 128 + 128],
                            start=(dt == 0), stop=(dt == ND - 1),
                            skip_group_check=True),
                            wait=(([vevs[-1]] if vevs else
                                   [tvo0[0]] + ([vh_free] if vh_free else []))
                                  if dt == 0 else []),
                            inc=[(s_te, 1)])
                    tev = b.op("scalar", lambda e, kt=kt: e.copy(
                        VH[:, kt * 66: kt * 66 + 64], psf(3, 64)),
                        wait=[tmm[0]], inc=[(s_se, 1)])
                    vevs.append(tev[0])

                M128 = RT[:, 35:51]
                p1prev = None
                for t in range(NT):
                    for q4 in range(4):
                        tmm1 = b.op("tensor",
                                    lambda e, t=t, q4=q4, QTM=QTM: e.matmul(
                            psf(2 + q4, 512),
                            QTM[0:64, t * 128:(t + 1) * 128],
                            KTO[0:64, q4 * 512:(q4 + 1) * 512],
                            start=True, stop=True),
                            wait=(([qproj_done, kevs[-1], vevs[-1]] +
                                   ([p1prev] if p1prev else []))
                                  if q4 == 0 else []),
                            inc=[(s_te, 1)])
                    trm = b.op("vector", lambda e, t=t, M128=M128: e.reduce_max(
                        bcol(M128, t), PSF[:, 2 * 512: 6 * 512],
                        axis=AX.X, negate=True),
                        wait=[tmm1[0]], inc=[(s_ve, 1)])
                    p1prev = trm[0]
                ttm = b.op("tensor", lambda e, M128=M128: e.transpose(
                    psf(2, 128)[0:16, :], M128, IDF[:]),
                    wait=[p1prev], inc=[(s_te, 1)])
                tme = b.op("scalar", lambda e: e.copy(
                    MRW[0:16, 0:128], psf(2, 128)[0:16, :]),
                    wait=[ttm[0]], inc=[(s_se, 1)])
                b.wait("tensor", [tme[0]])
                tmd = b.op("gpsimd", lambda e, QTM=QTM: e.dma_start(
                    out=QTM[64:65, 0:T], in_=MRW[0:16, 0:128]),
                    wait=[tme[0]], inc=[(s_gdma, 16)])

                et_last2 = [tmd[0], tmd[0]]
                avprev = [None, None]
                for qc in range(NQC):
                    etevs = []
                    for kt in range(NT):
                        bank = kt % 2
                        tmm = b.op("tensor",
                                   lambda e, kt=kt, qc=qc, bank=bank, QTM=QTM:
                                   e.matmul(
                            psf(bank, 256),
                            KTO[0:65, kt * 128:(kt + 1) * 128],
                            QTM[0:65, qc * 256:(qc + 1) * 256],
                            start=True, stop=True),
                            wait=([et_last2[kt % 2]] if kt < 2 else
                                  [etevs[kt - 2]]),
                            inc=[(s_te, 1)])
                        tex = b.op("scalar", lambda e, kt=kt, bank=bank:
                                   e.activation(
                            ETH[:, kt * 256:(kt + 1) * 256], psf(bank, 256),
                            AF.Exp), wait=[tmm[0]], inc=[(s_se, 1)])
                        etevs.append(tex[0])
                    et_last2 = [etevs[-2], etevs[-1]]
                    for ms in range(2):
                        bank = 2 + ms
                        for kt in range(NT):
                            tmm = b.op("tensor",
                                       lambda e, kt=kt, ms=ms, bank=bank:
                                       e.matmul(
                                PSF[:, bank * 512: bank * 512 + 66],
                                ETH[:, kt * 256 + ms * 128:
                                    kt * 256 + ms * 128 + 128],
                                VH[:, kt * 66:(kt + 1) * 66],
                                start=(kt == 0), stop=(kt == NT - 1),
                                skip_group_check=True),
                                wait=(([etevs[-1]] +
                                       ([avprev[ms]] if avprev[ms] else []))
                                      if kt == 0 else []),
                                inc=[(s_te, 1)])
                        tlz = b.op("scalar", lambda e, bank=bank: e.activation(
                            TCOL[:, 10:11],
                            PSF[:, bank * 512 + 64: bank * 512 + 65], AF.Ln),
                            wait=[tmm[0]], inc=[(s_se, 1)])
                        trz = b.op("scalar", lambda e: e.activation(
                            TCOL[:, 11:12], TCOL[:, 10:11], AF.Exp, scale=-1.0),
                            inc=[(s_se, 1)])
                        tevo = b.op("scalar", lambda e, bank=bank: e.activation(
                            OHS[:, 0:64], PSF[:, bank * 512: bank * 512 + 64],
                            AF.Identity, scale=TCOL[:, 11:12]),
                            inc=[(s_se, 1)])
                        tto = b.op("tensor", lambda e: e.transpose(
                            psb(1, 128)[0:64, :], OHS[:, 0:64], IDB[:]),
                            wait=[tevo[0]], inc=[(s_te, 1)])
                        tote = b.op("scalar", lambda e, h=h, qc=qc, ms=ms:
                                    e.copy(
                            OTH[(h % 2) * 64:(h % 2) * 64 + 64,
                                (h // 2) * T + qc * 256 + ms * 128:
                                (h // 2) * T + qc * 256 + (ms + 1) * 128],
                            psb(1, 128)[0:64, 0:128]),
                            wait=[tto[0]], inc=[(s_se, 1)])
                        b.wait("tensor", [tote[0]])
                        avprev[ms] = tote[0]
                        last_ot = tote[0]
                qtm_free[hh] = et_last2[1]
                kto_free = et_last2[1]
                vh_free = last_ot

        WOH2 = arb(0 * KB, 4 * 1024)
        prevw = None
        for ht in range(4):
            prevw = load_wcast_h(bcol(WOH2, ht, 1024),
                                 g["L0_ffn_wo_heavy"][ht * 128:(ht + 1) * 128, :],
                                 1024,
                                 wait=[last_ot] if ht == 0 else [prevw])
        woadd_prev = None
        for t in range(NT):
            for dc in range(2):
                bank = dc
                for ht in range(4):
                    tmm = b.op("tensor", lambda e, ht=ht, dc=dc, t=t, bank=bank:
                               e.matmul(psf(bank, 512),
                                        OTH[:, ht * T + t * 128:
                                            ht * T + (t + 1) * 128],
                                        WOH2[:, ht * 1024 + dc * 512:
                                             ht * 1024 + (dc + 1) * 512],
                                        start=(ht == 0), stop=(ht == 3),
                                        skip_group_check=True),
                               wait=(([prevw, last_ot] +
                                      ([woadd_prev] if woadd_prev else []))
                                     if ht == 0 else []),
                               inc=[(s_te, 1)])
                tadd = b.op("vector", lambda e, t=t, dc=dc, bank=bank:
                            e.scalar_tensor_tensor(
                                X[:, t * D + dc * 512: t * D + (dc + 1) * 512],
                                psf(bank, 512), bcol(MASKC, t),
                                X[:, t * D + dc * 512: t * D + (dc + 1) * 512],
                                ALU.mult, ALU.add),
                            wait=[tmm[0]], inc=[(s_ve, 1)])
                woadd_prev = tadd[0]
        b.barrier()

    WQ1 = arb(0 * KB, ND * 1024)
    WK1 = arb(16 * KB, ND * 64)
    WV1 = arb(17 * KB, ND * 64)
    ETH1 = arb(18 * KB, NT * 256)
    OT1H = arb(64 * KB, 8 * 1024)
    QTM1s = [arb(80 * KB, 1024), arb(82 * KB, 1024)]
    KTO1 = arb(84 * KB, T)
    V1 = arb(88 * KB, NT * 66)
    OHS1 = arb(91 * KB, 64)
    MRW1 = arf(92 * KB, 128)

    def stage_l1attn(bi):
        tk_ = b.op("gpsimd", lambda e: e.dma_start(
            out=WK1.rearrange("p (a c) -> p a c", a=ND),
            in_=g["L1_attn_Wk"].rearrange("(a p) c -> p a c", p=128)),
            inc=[(s_gdma, 16)])
        tv_ = b.op("gpsimd", lambda e: e.dma_start(
            out=WV1.rearrange("p (a c) -> p a c", a=ND),
            in_=g["L1_attn_Wv"].rearrange("(a p) c -> p a c", p=128)),
            inc=[(s_gdma, 16)])

        tko = b.op("gpsimd", lambda e: e.memset(KTO1[:], 1.0), inc=[(s_gp, 1)])
        kevs = []
        for tc in range(4):
            for dt in range(ND):
                tmm = b.op("tensor", lambda e, dt=dt, tc=tc: e.matmul(
                    psf(2, 512)[0:64, :],
                    WK1[:, dt * 64:(dt + 1) * 64],
                    XNT[:, dt * T + tc * 512: dt * T + (tc + 1) * 512],
                    start=(dt == 0), stop=(dt == ND - 1),
                    skip_group_check=True),
                    wait=(([tk_[0], tko[0]] + ([kevs[-1]] if kevs else []))
                          if dt == 0 else []),
                    inc=[(s_te, 1)])
            tev = b.op("scalar", lambda e, tc=tc: e.activation(
                KTO1[0:64, tc * 512:(tc + 1) * 512], psf(2, 512)[0:64, :],
                AF.Identity, bias=BKC[0:64, 0:1]),
                wait=[tmm[0]], inc=[(s_se, 1)])
            kevs.append(tev[0])
        tvo = b.op("gpsimd", lambda e: e.memset(V1[:], 1.0),
                   wait=[tko[0]], inc=[(s_gp, 1)])
        vevs = []
        for kt in range(NT):
            for dt in range(ND):
                tmm = b.op("tensor", lambda e, dt=dt, kt=kt: e.matmul(
                    psf(3, 64),
                    XNT[:, dt * T + kt * 128: dt * T + (kt + 1) * 128],
                    WV1[:, dt * 64:(dt + 1) * 64],
                    start=(dt == 0), stop=False, skip_group_check=True),
                    wait=(([tv_[0], tvo[0]] + ([vevs[-1]] if vevs else []))
                          if dt == 0 else []),
                    inc=[(s_te, 1)])
            tbb = b.op("tensor", lambda e: e.matmul(
                psf(3, 64), ONESB[:], BVROW[:], start=False, stop=True,
                skip_group_check=True), inc=[(s_te, 1)])
            tev = b.op("scalar", lambda e, kt=kt: e.copy(
                V1[:, kt * 66: kt * 66 + 64], psf(3, 64)),
                wait=[tbb[0]], inc=[(s_se, 1)])
            vevs.append(tev[0])

        woadd_prev = None
        wo_used = None
        for hf in range(2):
            hoff = hf * 1024
            tq = b.op("gpsimd", lambda e: e.dma_start(
                out=WQ1.rearrange("p (a c) -> p a c", a=ND),
                in_=g["L1_attn_Wq"].rearrange("(a p) c -> p a c", p=128)),
                wait=[wo_used] if wo_used else [], inc=[(s_gdma, 16)])
            qtm_free = [None, None]
            last_ot = None
            for j in range(8):
                pevs = []
                for tc in range(2):
                    bank = tc % 2
                    for dt in range(ND):
                        tmm = b.op("tensor",
                                   lambda e, j=j, dt=dt, tc=tc, bank=bank,
                                   hoff=hoff: e.matmul(
                            psf(bank, 512),
                            WQ1[:, dt * 1024 + j * 128: dt * 1024 + (j + 1) * 128],
                            XNT[:, dt * T + hoff + tc * 512:
                                dt * T + hoff + (tc + 1) * 512],
                            start=(dt == 0), stop=(dt == ND - 1),
                            skip_group_check=True),
                            wait=(([tq[0]] + [x for x in qtm_free if x] +
                                   (pevs[-2:] if pevs else []))
                                  if dt == 0 else []),
                            inc=[(s_te, 1)])
                    ev2 = []
                    for hh in range(2):
                        tev = b.op("scalar",
                                   lambda e, j=j, tc=tc, hh=hh, bank=bank:
                                   e.activation(
                            QTM1s[hh][0:64, tc * 512:(tc + 1) * 512],
                            psf(bank, 512)[hh * 64:hh * 64 + 64, :],
                            AF.Identity, scale=0.125,
                            bias=BQC[hh * 64:hh * 64 + 64, j:j + 1]),
                            wait=[tmm[0]], inc=[(s_se, 1)])
                        ev2.append(tev[0])
                    pevs.append(ev2[-1])
                b.wait("tensor", [pevs[-1]])
                qproj_done = pevs[-1]

                for hh in range(2):
                    h = 2 * j + hh
                    QTM = QTM1s[hh]
                    M128 = RT[:, 35:51]
                    p1prev = None
                    for t8 in range(8):
                        for q4 in range(4):
                            tmm1 = b.op("tensor",
                                        lambda e, t8=t8, q4=q4, QTM=QTM:
                                        e.matmul(
                                psf(2 + q4, 512),
                                QTM[0:64, t8 * 128:(t8 + 1) * 128],
                                KTO1[0:64, q4 * 512:(q4 + 1) * 512],
                                start=True, stop=True),
                                wait=(([qproj_done, kevs[-1], vevs[-1]] +
                                       ([p1prev] if p1prev else []))
                                      if q4 == 0 else []),
                                inc=[(s_te, 1)])
                        trm = b.op("vector",
                                   lambda e, t8=t8, M128=M128: e.reduce_max(
                            bcol(M128, t8), PSF[:, 2 * 512: 6 * 512],
                            axis=AX.X, negate=True),
                            wait=[tmm1[0]], inc=[(s_ve, 1)])
                        p1prev = trm[0]
                    ttm = b.op("tensor", lambda e, M128=M128: e.transpose(
                        psf(2, 128)[0:16, :], M128[:, 0:16], IDF[:]),
                        wait=[p1prev], inc=[(s_te, 1)])
                    tme = b.op("scalar", lambda e: e.copy(
                        MRW1[0:8, 0:128], psf(2, 128)[0:8, :]),
                        wait=[ttm[0]], inc=[(s_se, 1)])
                    b.wait("tensor", [tme[0]])
                    tmd = b.op("gpsimd", lambda e, QTM=QTM: e.dma_start(
                        out=QTM[64:65, 0:1024], in_=MRW1[0:8, 0:128]),
                        wait=[tme[0]], inc=[(s_gdma, 16)])

                    et_last2 = [tmd[0], tmd[0]]
                    avprev = [None, None]
                    for qc in range(4):
                        etevs = []
                        for kt in range(NT):
                            bank = kt % 2
                            tmm = b.op("tensor",
                                       lambda e, kt=kt, qc=qc, bank=bank,
                                       QTM=QTM: e.matmul(
                                psf(bank, 256),
                                KTO1[0:65, kt * 128:(kt + 1) * 128],
                                QTM[0:65, qc * 256:(qc + 1) * 256],
                                start=True, stop=True),
                                wait=([et_last2[kt % 2]] if kt < 2 else
                                      [etevs[kt - 2]]),
                                inc=[(s_te, 1)])
                            tex = b.op("scalar", lambda e, kt=kt, bank=bank:
                                       e.activation(
                                ETH1[:, kt * 256:(kt + 1) * 256],
                                psf(bank, 256), AF.Exp),
                                wait=[tmm[0]], inc=[(s_se, 1)])
                            etevs.append(tex[0])
                        et_last2 = [etevs[-2], etevs[-1]]
                        for ms in range(2):
                            bank = 2 + ms
                            for kt in range(NT):
                                tmm = b.op("tensor",
                                           lambda e, kt=kt, ms=ms, bank=bank:
                                           e.matmul(
                                    PSF[:, bank * 512: bank * 512 + 66],
                                    ETH1[:, kt * 256 + ms * 128:
                                         kt * 256 + ms * 128 + 128],
                                    V1[:, kt * 66:(kt + 1) * 66],
                                    start=(kt == 0), stop=(kt == NT - 1),
                                    skip_group_check=True),
                                    wait=(([etevs[-1]] +
                                           ([avprev[ms]] if avprev[ms] else []))
                                          if kt == 0 else []),
                                    inc=[(s_te, 1)])
                            tlz = b.op("scalar", lambda e, bank=bank:
                                       e.activation(
                                TCOL[:, 10:11],
                                PSF[:, bank * 512 + 64: bank * 512 + 65],
                                AF.Ln), wait=[tmm[0]], inc=[(s_se, 1)])
                            trz = b.op("scalar", lambda e: e.activation(
                                TCOL[:, 11:12], TCOL[:, 10:11], AF.Exp,
                                scale=-1.0), inc=[(s_se, 1)])
                            tevo = b.op("scalar", lambda e, bank=bank:
                                        e.activation(
                                OHS1[:, 0:64],
                                PSF[:, bank * 512: bank * 512 + 64],
                                AF.Identity, scale=TCOL[:, 11:12]),
                                inc=[(s_se, 1)])
                            tto = b.op("tensor", lambda e: e.transpose(
                                psb(1, 128)[0:64, :], OHS1[:, 0:64], IDB[:]),
                                wait=[tevo[0]], inc=[(s_te, 1)])
                            tote = b.op("scalar",
                                        lambda e, h=h, qc=qc, ms=ms: e.copy(
                                OT1H[(h % 2) * 64:(h % 2) * 64 + 64,
                                     (h // 2) * 1024 + qc * 256 + ms * 128:
                                     (h // 2) * 1024 + qc * 256 + (ms + 1) * 128],
                                psb(1, 128)[0:64, 0:128]),
                                wait=[tto[0]], inc=[(s_se, 1)])
                            b.wait("tensor", [tote[0]])
                            avprev[ms] = tote[0]
                            last_ot = tote[0]
                    qtm_free[hh] = et_last2[1]

            two = b.op("gpsimd", lambda e: e.dma_start(
                out=WQ1.rearrange("p (a c) -> p a c", a=ND),
                in_=g["L1_attn_Wo"].rearrange("(a p) c -> p a c", p=128)),
                wait=[qproj_done], inc=[(s_gdma, 16)])
            for t8 in range(8):
                t = hf * 8 + t8
                for dc in range(2):
                    bank = dc
                    for ht in range(8):
                        tmm = b.op("tensor",
                                   lambda e, ht=ht, dc=dc, t8=t8, bank=bank:
                                   e.matmul(
                            psf(bank, 512),
                            OT1H[:, ht * 1024 + t8 * 128:
                                 ht * 1024 + (t8 + 1) * 128],
                            WQ1[:, ht * 1024 + dc * 512:
                                ht * 1024 + (dc + 1) * 512],
                            start=(ht == 0), stop=False,
                            skip_group_check=True),
                            wait=(([last_ot, two[0]] +
                                   ([woadd_prev] if woadd_prev else []))
                                  if ht == 0 else []),
                            inc=[(s_te, 1)])
                    tbb = b.op("tensor", lambda e, dc=dc, bank=bank: e.matmul(
                        psf(bank, 512), ONESB[:],
                        ROWSB[0:1, D + dc * 512: D + (dc + 1) * 512],
                        start=False, stop=True, skip_group_check=True),
                        inc=[(s_te, 1)])
                    tadd = b.op("vector", lambda e, t=t, dc=dc, bank=bank:
                                e.tensor_add(
                        X[:, t * D + dc * 512: t * D + (dc + 1) * 512],
                        X[:, t * D + dc * 512: t * D + (dc + 1) * 512],
                        psf(bank, 512)),
                        wait=[tbb[0]], inc=[(s_ve, 1)])
                    woadd_prev = tadd[0]
            wo_used = tmm[0]
        b.barrier()

    OUTACC = arb(64 * KB, 8 * 1024)
    SILU = arf(80 * KB, 1024)
    XSTG2 = [arf(84 * KB, D), arf(88 * KB, D)]

    def w1buf(p, ag):
        return arb((p * 16 + ag * 8) * KB, ND * 512)

    def w2buf(p):
        return arb(p * 16 * KB, 32 * 256)

    def t3t(kt):
        return arb(96 * KB + kt * 2 * KB, 1024)

    def stage_moe(bi):
        acc_free = None
        wreg_free = None
        t3_free = None
        out_done = []
        for c in range(2):
            tms = b.op("gpsimd", lambda e: e.memset(OUTACC[:], 0.0),
                       wait=[acc_free] if acc_free else [], inc=[(s_gp, 1)])
            acc_prev = tms[0]
            for e_ in range(NEXP):
                last_t3 = None
                pair_ev = {}
                dma_toks = {}
                for grp in range(8):
                    p = grp % 2
                    wa = b.op("gpsimd", lambda e, e_=e_, grp=grp, p=p:
                              e.dma_start(
                        out=w1buf(p, 0).rearrange("q (a c) -> q a c", a=ND),
                        in_=g["L1_ffn_w1"][e_].rearrange(
                            "(a q) f -> q a f", q=128)[:, :,
                            grp * 512:(grp + 1) * 512]),
                        wait=([wreg_free] if wreg_free and grp < 2 else []) +
                             ([dma_toks[grp - 2][2]] if grp >= 2 else []),
                        inc=[(s_gdma, 16)])
                    wg = b.op("gpsimd", lambda e, e_=e_, grp=grp, p=p:
                              e.dma_start(
                        out=w1buf(p, 1).rearrange("q (a c) -> q a c", a=ND),
                        in_=g["L1_ffn_w1"][e_].rearrange(
                            "(a q) f -> q a f", q=128)[:, :,
                            FFN + grp * 512: FFN + (grp + 1) * 512]),
                        inc=[(s_gdma, 16)])
                    grp_last = None
                    for i in range(4):
                        pp = i % 2
                        ab = 4 * pp
                        gb = 4 * pp + 2
                        ja = grp * 4 + i
                        for nch in range(2):
                            for dt in range(ND):
                                tma = b.op("tensor",
                                           lambda e, i=i, dt=dt, nch=nch,
                                           c=c, p=p, ab=ab: e.matmul(
                                    psf(ab + nch, 512),
                                    w1buf(p, 0)[:, dt * 512 + i * 128:
                                                dt * 512 + (i + 1) * 128],
                                    XNT[:, dt * T + c * 1024 + nch * 512:
                                        dt * T + c * 1024 + (nch + 1) * 512],
                                    start=(dt == 0), stop=(dt == ND - 1),
                                    skip_group_check=True),
                                    wait=(([wa[0]] +
                                           ([pair_ev[pp][0]] if pp in pair_ev
                                            else []) +
                                           ([t3_free] if t3_free and grp == 0
                                            and i == 0 else []))
                                          if dt == 0 and nch == 0 else []),
                                    inc=[(s_te, 1)])
                            for dt in range(ND):
                                tmg = b.op("tensor",
                                           lambda e, i=i, dt=dt, nch=nch,
                                           c=c, p=p, gb=gb: e.matmul(
                                    psf(gb + nch, 512),
                                    w1buf(p, 1)[:, dt * 512 + i * 128:
                                                dt * 512 + (i + 1) * 128],
                                    XNT[:, dt * T + c * 1024 + nch * 512:
                                        dt * T + c * 1024 + (nch + 1) * 512],
                                    start=(dt == 0), stop=(dt == ND - 1),
                                    skip_group_check=True),
                                    wait=(([wg[0]] +
                                           ([pair_ev[pp][1]] if pp in pair_ev
                                            else []))
                                          if dt == 0 and nch == 0 else []),
                                    inc=[(s_te, 1)])
                        tsi = b.op("scalar", lambda e, ab=ab: e.activation(
                            SILU[:], PSF[:, ab * 512:(ab + 2) * 512],
                            AF.Silu), wait=[tma[0]], inc=[(s_se, 1)])
                        tt3 = b.op("vector", lambda e, ja=ja, gb=gb: e.tensor_mul(
                            t3t(ja), SILU[:], PSF[:, gb * 512:(gb + 2) * 512]),
                            wait=[tsi[0], tmg[0]], inc=[(s_ve, 1)])
                        pair_ev[pp] = (tt3[0], tt3[0])
                        grp_last = tt3[0]
                        last_t3 = tt3[0]
                    dma_toks[grp] = (wa[0], wg[0], grp_last)
                w2toks = {}
                y2prev = {}
                for nq in range(4):
                    p = nq % 2
                    wts = []
                    for piece in range(4):
                        tw2 = b.op("gpsimd",
                                   lambda e, e_=e_, nq=nq, p=p, piece=piece:
                                   e.dma_start(
                            out=w2buf(p).rearrange(
                                "q (a c) -> q a c", a=32)[:, piece * 8:
                                                          (piece + 1) * 8],
                            in_=g["L1_ffn_w2"][e_].rearrange(
                                "(a q) f -> q a f", q=128)[:, piece * 8:
                                                           (piece + 1) * 8,
                                nq * 256:(nq + 1) * 256]),
                            wait=([last_t3] if nq < 2 and piece == 0 else []) +
                                 ([w2toks[nq - 2]] if nq >= 2 and piece == 0
                                  else []),
                            inc=[(s_gdma, 16)])
                        wts.append(tw2[0])
                    for mt in range(8):
                        bank = mt % 2
                        for kt in range(32):
                            tmy = b.op("tensor",
                                       lambda e, kt=kt, mt=mt, nq=nq, p=p,
                                       bank=bank: e.matmul(
                                psf(bank, 256),
                                t3t(kt)[:, mt * 128:(mt + 1) * 128],
                                w2buf(p)[:, kt * 256:(kt + 1) * 256],
                                start=(kt == 0), stop=(kt == 31),
                                skip_group_check=True),
                                wait=((wts + [last_t3] +
                                       ([y2prev[bank]] if bank in y2prev
                                        else []))
                                      if kt == 0 else []),
                                inc=[(s_te, 1)])
                        tya = b.op("vector",
                                   lambda e, mt=mt, nq=nq, c=c, e_=e_,
                                   bank=bank: e.scalar_tensor_tensor(
                            OUTACC[:, mt * 1024 + nq * 256:
                                   mt * 1024 + (nq + 1) * 256],
                            psf(bank, 256),
                            SELC[:, (8 * c + mt) * NEXP + e_:
                                 (8 * c + mt) * NEXP + e_ + 1],
                            OUTACC[:, mt * 1024 + nq * 256:
                                   mt * 1024 + (nq + 1) * 256],
                            ALU.mult, ALU.add),
                            wait=[tmy[0], acc_prev], inc=[(s_ve, 1)])
                        y2prev[bank] = tya[0]
                    w2toks[nq] = tmy[0]
                acc_prev = tya[0]
                wreg_free = tya[0]
                t3_free = tya[0]
            slot_busy = [None, None]
            for mt in range(8):
                tg = 8 * c + mt
                sl = mt % 2
                tl = b.op("sync", lambda e, tg=tg, sl=sl: e.dma_start(
                    out=XSTG2[sl][:], in_=hbm_tiles(scratch_x1[bi])[tg]),
                    wait=[slot_busy[sl]] if slot_busy[sl] else [],
                    inc=[(s_dma, 16)])
                ta_ = b.op("vector", lambda e, mt=mt, sl=sl: e.tensor_add(
                    XSTG2[sl][:], XSTG2[sl][:],
                    OUTACC[:, mt * 1024:(mt + 1) * 1024]),
                    wait=[tl[0], acc_prev], inc=[(s_ve, 1)])
                tw = b.op("sync", lambda e, tg=tg, sl=sl, bi=bi: e.dma_start(
                    out=hbm_tiles(out[bi])[tg], in_=XSTG2[sl][:]),
                    wait=[ta_[0]], inc=[(s_out, 16)])
                slot_busy[sl] = tw[0]
                out_done.append(tw[0])
            acc_free = ta_[0]
        b.barrier()

    setup_constants()
    for bi in range(NB):
        stage_ln(bi, 0)
        if stop_after == "ln0":
            for t in range(NT):
                b.op("sync", lambda e, t=t, bi=bi: e.dma_start(
                    out=hbm_tiles(out[bi])[t], in_=bcol(X, t, D)),
                    inc=[(s_out, 16)])
            continue
        offload_X(bi)
        stage_ssm(bi)
        stage_yadd(bi)
        rsteps = routing_steps(bi)
        if stop_after == "ssm":
            for r in rsteps:
                r()
            b.barrier()
            b.op("sync", lambda e, bi=bi: e.dma_start(
                out=out[bi], in_=scratch_x1[bi]), inc=[(s_out, 16)])
            continue
        stage_light(bi, rsteps)
        stage_heavy(bi)
        if stop_after == "block0":
            for t in range(NT):
                b.op("sync", lambda e, t=t, bi=bi: e.dma_start(
                    out=hbm_tiles(out[bi])[t], in_=bcol(X, t, D)),
                    inc=[(s_out, 16)])
            continue
        stage_ln(bi, 1)
        stage_l1attn(bi)
        if stop_after == "l1attn":
            for t in range(NT):
                b.op("sync", lambda e, t=t, bi=bi: e.dma_start(
                    out=hbm_tiles(out[bi])[t], in_=bcol(X, t, D)),
                    inc=[(s_out, 16)])
            continue
        offload_X(bi)
        stage_moe(bi)

    b.wait("sync", [(s_out, s_out.n)])
    b.barrier()
    b.finalize()
    es.close()
    return nc, tap_handles


def kernel(**inputs):
    nc, _ = build_program(stop_after=None)
    xs = np.ascontiguousarray(inputs["L0_attn_x"], np.float32)
    common = {k: np.ascontiguousarray(np.asarray(v), np.float32)
              for k, v in inputs.items() if k != "L0_attn_x"}
    in_maps = []
    for c in range(8):
        m = dict(common)
        m["x"] = xs[2 * c:2 * c + 2]
        in_maps.append(m)
    res = run_bass_kernel_spmd(nc, in_maps, core_ids=list(range(8)))
    return np.concatenate([r["out"] for r in res.results], axis=0)


# version 03
# speedup vs baseline: 1.2183x; 1.1828x over previous
import numpy as np
import ml_dtypes
from contextlib import ExitStack

import concourse.bass as bass
import concourse.mybir as mybir
from concourse.bass_utils import run_bass_kernel_spmd

F32 = mybir.dt.float32
BF16 = mybir.dt.bfloat16
AF = mybir.ActivationFunctionType
ALU = mybir.AluOpType
AX = mybir.AxisListType

T = 2048
NT = 16
D = 1024
ND = 8
FFN = 4096
NEXP = 8
NB = 2
NITER_CD = 50
NITER_TH = 48
NITER_IX = 14
LOGK = float(np.log(np.float32(1152.0)))


class Sem:
    def __init__(self, h):
        self.h = h
        self.n = 0


class Builder:
    ENGINES = ("tensor", "vector", "scalar", "gpsimd", "sync")

    def __init__(self, nc, es):
        self.nc = nc
        self.es = es
        self.q = {e: [] for e in self.ENGINES}
        self._sems = {}
        self.tapnames = set()

    def sem(self, name):
        if name not in self._sems:
            self._sems[name] = Sem(self.es.enter_context(self.nc.semaphore(name)))
        return self._sems[name]

    def sb(self, name, shape, dt=F32, **kw):
        return self.es.enter_context(self.nc.sbuf_tensor(name, shape, dt, **kw))

    def ps(self, name, shape, dt=F32):
        return self.es.enter_context(self.nc.psum_tensor(name, shape, dt))

    def dma_pool(self, tag):
        if not hasattr(self, "_dmapools"):
            self._dmapools = {}
            self._dmarr = {}
        if tag not in self._dmapools:
            self._dmapools[tag] = [self.sem("dmas_%s%d" % (tag, i))
                                   for i in range(8)]
            self._dmarr[tag] = -1
        return self._dmapools[tag]

    def op(self, eng, fn, wait=(), inc=()):
        toks = []
        incs = []
        extra_wait = []
        for s, k in inc:
            if getattr(s, "isdma", False):
                tag = s.tag
                pool = self.dma_pool(tag)
                self._dmarr[tag] = (self._dmarr[tag] + 1) % len(pool)
                slot = pool[self._dmarr[tag]]
                extra_wait.append((slot, slot.n))
                slot.n += 16
                incs.append((slot.h, 16))
                toks.append((slot, slot.n))
            else:
                s.n += k
                incs.append((s.h, k))
                toks.append((s, s.n))
        waits = tuple((s.h, v) for (s, v) in tuple(wait) + tuple(extra_wait)
                      if s is not None)

        def run(engine, fn=fn, waits=waits, incs=incs):
            for h, v in waits:
                engine.wait_ge(h, v)
            r = fn(engine)
            for h, k in incs:
                r = r.then_inc(h, k)

        self.q[eng].append(run)
        return toks

    def wait(self, eng, tokens):
        waits = tuple((s.h, v) for (s, v) in tokens if s is not None)

        def run(engine, waits=waits):
            for h, v in waits:
                engine.wait_ge(h, v)

        self.q[eng].append(run)

    def barrier(self, engines=None):
        engines = engines or self.ENGINES
        s = self.sem("bar")
        for e in engines:
            self.op(e, lambda eng: eng.nop(), inc=[(s, 1)])
        target = s.n
        for e in engines:
            self.wait(e, [(s, target)])
        return (s, target)

    def finalize(self):
        with self.nc.Block() as blk:
            for eng in self.ENGINES:
                def body(engine, eng=eng):
                    for f in self.q[eng]:
                        f(engine)
                getattr(blk, eng)(body)


def bcol(ap, j, w=1):
    return ap[:, j * w:(j + 1) * w]


def build_program(taps=(), stop_after=None):
    nc = bass.Bass("TRN2", target_bir_lowering=False, detect_race_conditions=False)
    es = ExitStack()
    b = Builder(nc, es)
    b.tapnames = set(taps)

    g = {}

    def din(name, shape):
        g[name] = nc.dram_tensor(name, list(shape), F32, kind="ExternalInput").ap()

    din("x", (NB, T, D))
    din("L0_attn_norm_w", (D,))
    din("L0_attn_A_w", (D, D)); din("L0_attn_A_b", (D,))
    din("L0_attn_B_w", (D, D)); din("L0_attn_C_w", (D, D))
    din("L0_attn_D_w", (D, D))
    din("L0_attn_out_w", (D, D)); din("L0_attn_out_b", (D,))
    din("L0_ffn_gamma_light", (D,)); din("L0_ffn_wq_light", (D, 512))
    din("L0_ffn_wkv_light", (D, 1024)); din("L0_ffn_wo_light", (512, D))
    din("L0_ffn_q_routing_token", (D,)); din("L0_ffn_kv_routing_token", (D,))
    din("L0_ffn_gamma_heavy", (D,)); din("L0_ffn_wq_heavy", (D, 512))
    din("L0_ffn_wkv_heavy", (1024, D)); din("L0_ffn_wo_heavy", (512, D))
    din("L1_attn_Wq", (D, 1024)); din("L1_attn_bq", (1024,))
    din("L1_attn_Wk", (D, 64)); din("L1_attn_bk", (64,))
    din("L1_attn_Wv", (D, 64)); din("L1_attn_bv", (64,))
    din("L1_attn_Wo", (1024, D)); din("L1_attn_bo", (D,))
    din("L1_ffn_router_w", (D, NEXP)); din("L1_ffn_router_b", (NEXP,))
    din("L1_ffn_w1", (NEXP, D, 2 * FFN)); din("L1_ffn_w2", (NEXP, FFN, D))

    out = nc.dram_tensor("out", [NB, T, D], F32, kind="ExternalOutput").ap()
    scratch_x1 = nc.dram_tensor("scratch_x1", [NB, T, D], F32, kind="Internal").ap()

    tap_handles = {}

    def tap(name, shape, dt=F32):
        if name in b.tapnames and name not in tap_handles:
            tap_handles[name] = nc.dram_tensor(
                "tap_" + name, list(shape), dt, kind="ExternalOutput").ap()
        return tap_handles.get(name)

    def hbm_tiles(ap2d):
        return ap2d.rearrange("(t p) d -> t p d", p=128)

    S128 = b.sb("S128", [128, NT])
    MASKC = b.sb("MASKC", [128, NT])
    BFC = b.sb("BFC", [128, NT])
    RNC = b.sb("RNC", [128, NT])
    IOTA = b.sb("IOTA", [128, NT])
    SELC = b.sb("SELC", [128, NT * NEXP])
    NWC = b.sb("NWC", [128, ND])
    GLC = b.sb("GLC", [128, ND])
    GHC = b.sb("GHC", [128, ND])
    ABC = b.sb("ABC", [128, ND])
    BQC = b.sb("BQC", [128, ND])
    RB = b.sb("RB", [128, D])
    ONES = b.sb("ONES", [1, 128])
    ONESB = b.sb("ONESB", [1, 128], BF16)
    ROWSB = b.sb("ROWSB", [1, 2 * D], BF16)
    BVROW = b.sb("BVROW", [1, 64], BF16)
    RWF = b.sb("RWF", [128, ND * NEXP])
    RBROW = b.sb("RBROW", [1, NEXP])
    BKC = b.sb("BKC", [64, 1])
    IDB = b.sb("IDB", [128, 128], BF16)
    IDF = b.sb("IDF", [128, 128])
    XC = b.sb("XC", [128, D])
    XN = b.sb("XN", [128, D])
    TBF1 = b.sb("TBF1", [128, D], BF16)
    TBF2 = b.sb("TBF2", [128, D], BF16)
    TCOL = b.sb("TCOL", [128, 16])
    TCOL2 = b.sb("TCOL2", [128, 16])
    RT = b.sb("RT", [128, 64])
    RSC = b.sb("RSC", [1, 32])
    ROW128 = b.sb("ROW128", [1, 128])

    AR_BYTES = 160 * 1024
    ARENA = b.sb("ARENA", [128, AR_BYTES // 4], F32)
    ARB = ARENA.bitcast(BF16)

    def arf(off, cols):
        assert off % 4 == 0 and off + cols * 4 <= AR_BYTES, (off, cols)
        return ARENA[:, off // 4: off // 4 + cols]

    def arb(off, cols):
        assert off % 2 == 0 and off + cols * 2 <= AR_BYTES, (off, cols)
        return ARB[:, off // 2: off // 2 + cols]

    KB = 1024
    A_X = 96 * KB
    X = arf(A_X, NT * D)
    BASET = arb(64 * KB, ND * T)
    XNT = arb(32 * KB, ND * T)

    PSF = b.ps("PSF", [128, 4096], F32)
    PSFB = PSF.bitcast(BF16)

    def psf(bank, w=512, off=0):
        return PSF[:, bank * 512 + off: bank * 512 + off + w]

    def psb(bank, w=1024, off=0):
        return PSFB[:, (6 + bank) * 1024 + off: (6 + bank) * 1024 + off + w]

    s_dma = b.sem("s_dma")
    s_dma.isdma = True
    s_dma.tag = "sy"
    s_gdma = b.sem("s_gdma")
    s_gdma.isdma = True
    s_gdma.tag = "gp"
    s_te = b.sem("s_te")
    s_ve = b.sem("s_ve")
    s_se = b.sem("s_se")
    s_gp = b.sem("s_gp")
    s_out = b.sem("s_out")

    def setup_constants():
        p0 = b.op("gpsimd", lambda e: e.memset(IDB[:], 0.0), inc=[(s_gp, 1)])
        p0 = b.op("gpsimd", lambda e: e.affine_select(
            out=IDB[:], in_=IDB[:], compare_op=ALU.not_equal, fill=1.0, base=0,
            pattern=[[-1, 128]], channel_multiplier=1),
            wait=[p0[0]], inc=[(s_gp, 1)])
        p0 = b.op("gpsimd", lambda e: e.memset(IDF[:], 0.0),
                  wait=[p0[0]], inc=[(s_gp, 1)])
        p0 = b.op("gpsimd", lambda e: e.affine_select(
            out=IDF[:], in_=IDF[:], compare_op=ALU.not_equal, fill=1.0, base=0,
            pattern=[[-1, 128]], channel_multiplier=1),
            wait=[p0[0]], inc=[(s_gp, 1)])
        p0 = b.op("gpsimd", lambda e: e.iota(
            IOTA[:], pattern=[[128, NT]], base=0, channel_multiplier=1,
            allow_small_or_imprecise_dtypes=True), wait=[p0[0]], inc=[(s_gp, 1)])
        p0 = b.op("gpsimd", lambda e: e.memset(ONES[:], 1.0),
                  wait=[p0[0]], inc=[(s_gp, 1)])
        pgp = b.op("gpsimd", lambda e: e.memset(ONESB[:], 1.0),
                   wait=[p0[0]], inc=[(s_gp, 1)])
        b.wait("tensor", [pgp[0]])
        b.wait("vector", [pgp[0]])
        b.wait("scalar", [pgp[0]])

        t1 = b.op("sync", lambda e: e.dma_start(
            out=XC[0:1, 0:D], in_=g["L0_attn_out_b"][None, :]), inc=[(s_dma, 16)])
        tv1 = b.op("vector", lambda e: e.tensor_copy(ROWSB[:, 0:D], XC[0:1, 0:D]),
                   wait=[t1[0]], inc=[(s_ve, 1)])
        t2 = b.op("sync", lambda e: e.dma_start(
            out=XC[0:1, 0:D], in_=g["L1_attn_bo"][None, :]),
            wait=[tv1[0]], inc=[(s_dma, 16)])
        tv2 = b.op("vector", lambda e: e.tensor_copy(ROWSB[:, D:2 * D], XC[0:1, 0:D]),
                   wait=[t2[0]], inc=[(s_ve, 1)])
        b.op("sync", lambda e: e.dma_start(
            out=RBROW[:], in_=g["L1_ffn_router_b"][None, :]), inc=[(s_dma, 16)])
        b.op("gpsimd", lambda e: e.dma_start(
            out=BVROW[:], in_=g["L1_attn_bv"][None, :]), inc=[(s_gdma, 16)])

        def chan_cols(name, dest, scale=1.0, prev=None):
            tl = b.op("sync", lambda e, name=name: e.dma_start(
                out=XN[0:8, 0:128],
                in_=g[name].rearrange("(a p) -> a p", p=128)),
                wait=[prev] if prev else [tv2[0]], inc=[(s_dma, 16)])
            tt = b.op("tensor", lambda e: e.transpose(
                psf(0, 8), XN[0:8, 0:128], IDF[0:8, 0:8]),
                wait=[tl[0]], inc=[(s_te, 1)])
            te_ = b.op("scalar", lambda e, dest=dest, scale=scale: e.activation(
                dest[:, 0:8], psf(0, 8), AF.Copy, scale=scale),
                wait=[tt[0]], inc=[(s_se, 1)])
            b.wait("tensor", [te_[0]])
            return te_[0]

        p = None
        p = chan_cols("L0_attn_norm_w", NWC, prev=p)
        p = chan_cols("L0_ffn_gamma_light", GLC, prev=p)
        p = chan_cols("L0_ffn_gamma_heavy", GHC, prev=p)
        p = chan_cols("L0_attn_A_b", ABC, prev=p)
        p = chan_cols("L1_attn_bq", BQC, scale=0.125, prev=p)

        tk = b.op("sync", lambda e: e.dma_start(
            out=XN[0:1, 0:64], in_=g["L1_attn_bk"][None, :]),
            wait=[p], inc=[(s_dma, 16)])
        tkt = b.op("tensor", lambda e: e.transpose(
            psf(0, 1)[0:64, :], XN[0:1, 0:64], IDF[0:1, 0:1]),
            wait=[tk[0]], inc=[(s_te, 1)])
        tke = b.op("scalar", lambda e: e.copy(BKC[:], psf(0, 1)[0:64, 0:1]),
                   wait=[tkt[0]], inc=[(s_se, 1)])
        b.wait("tensor", [tke[0]])

        tr = b.op("sync", lambda e: e.dma_start(
            out=XN[0:1, 0:D], in_=g["L0_ffn_q_routing_token"][None, :]),
            wait=[tke[0]], inc=[(s_dma, 16)])
        prev_ev = None
        for dt in range(ND):
            bank = dt % 2
            tm = b.op("tensor", lambda e, dt=dt, bank=bank: e.matmul(
                psf(bank, 128), ONES[:], XN[0:1, dt * 128:(dt + 1) * 128],
                start=True, stop=True),
                wait=[tr[0]] + ([prev_ev] if prev_ev else []), inc=[(s_te, 1)])
            tev = b.op("scalar", lambda e, dt=dt, bank=bank: e.copy(
                RB[:, dt * 128:(dt + 1) * 128], psf(bank, 128)),
                wait=[tm[0]], inc=[(s_se, 1)])
            prev_ev = tev[0]
        b.wait("tensor", [prev_ev])

        b.op("sync", lambda e: e.dma_start(
            out=RWF.rearrange("p (a c) -> p a c", a=ND),
            in_=g["L1_ffn_router_w"].rearrange("(a p) c -> p a c", p=128)),
            wait=[prev_ev], inc=[(s_dma, 16)])
        b.barrier()

    def part_reduce(src_col, dst_col, alu, extra_wait=()):
        tt = b.op("tensor", lambda e, src_col=src_col: e.transpose(
            psf(5, 128)[0:1, :], src_col, IDF[:]),
            wait=list(extra_wait), inc=[(s_te, 1)])
        te_ = b.op("scalar", lambda e: e.copy(ROW128[:], psf(5, 128)[0:1, :]),
                   wait=[tt[0]], inc=[(s_se, 1)])
        b.wait("tensor", [te_[0]])
        tr_ = b.op("vector", lambda e, dst_col=dst_col, alu=alu: e.tensor_reduce(
            RSC[0:1, dst_col:dst_col + 1], ROW128[:], axis=AX.X, op=alu),
            wait=[te_[0]], inc=[(s_ve, 1)])
        return tr_[0]

    def bcast_cols(src_row_off, n, dst, extra_wait=()):
        tm = b.op("tensor", lambda e, src_row_off=src_row_off, n=n: e.matmul(
            psf(5, n), ONES[:], RSC[0:1, src_row_off:src_row_off + n],
            start=True, stop=True), wait=list(extra_wait), inc=[(s_te, 1)])
        te_ = b.op("scalar", lambda e, dst=dst, n=n: e.copy(dst, psf(5, n)),
                   wait=[tm[0]], inc=[(s_se, 1)])
        b.wait("tensor", [te_[0]])
        return te_[0]

    def stage_ln(bi, which):
        first = (which == 0)
        b.ln_bankfree = {0: None, 1: None}
        tap_xn = tap("xn%d_b%d" % (which, bi), (T, D))
        for t in range(NT):
            if first:
                tl = b.op("sync", lambda e, t=t: e.dma_start(
                    out=bcol(X, t, D), in_=hbm_tiles(g["x"][bi])[t]),
                    inc=[(s_dma, 16)])
                b.wait("vector", [tl[0]])
            xt = bcol(X, t, D)
            tm = b.op("vector", lambda e, xt=xt: e.reduce_sum(
                TCOL[:, 0:1], xt, axis=AX.X), inc=[(s_ve, 1)])
            tnm = b.op("vector", lambda e: e.tensor_scalar(
                TCOL[:, 1:2], TCOL[:, 0:1], -1.0 / D, None, ALU.mult),
                wait=[tm[0]], inc=[(s_ve, 1)])
            txc = b.op("scalar", lambda e, xt=xt: e.activation(
                XC[:], xt, AF.Identity, bias=TCOL[:, 1:2]),
                wait=[tnm[0]], inc=[(s_se, 1)])
            tsq = b.op("scalar", lambda e: e.activation(
                XN[:], XC[:], AF.Square, accum_out=TCOL[:, 2:3]),
                wait=[txc[0]], inc=[(s_se, 1)])
            tv = b.op("vector", lambda e: e.tensor_scalar(
                TCOL[:, 3:4], TCOL[:, 2:3], 1.0 / D, 1e-5, ALU.mult, ALU.add),
                wait=[tsq[0]], inc=[(s_ve, 1)])
            tln = b.op("scalar", lambda e: e.activation(
                TCOL[:, 4:5], TCOL[:, 3:4], AF.Ln), wait=[tv[0]], inc=[(s_se, 1)])
            trs = b.op("scalar", lambda e: e.activation(
                TCOL[:, 5:6], TCOL[:, 4:5], AF.Exp, scale=-0.5), inc=[(s_se, 1)])
            txn = b.op("scalar", lambda e: e.activation(
                XN[:], XC[:], AF.Identity, scale=TCOL[:, 5:6]), inc=[(s_se, 1)])
            if tap_xn is not None:
                b.op("sync", lambda e, t=t, tap_xn=tap_xn: e.dma_start(
                    out=hbm_tiles(tap_xn)[t], in_=XN[:]),
                    wait=[txn[0]], inc=[(s_out, 16)])

            if first:
                tsd0 = b.op("vector", lambda e: e.tensor_mul(
                    XC[:], XN[:], RB[:]), wait=[txn[0]], inc=[(s_ve, 1)])
                tsdot = b.op("vector", lambda e, t=t: e.reduce_sum(
                    bcol(S128, t), XC[:], axis=AX.X), inc=[(s_ve, 1)])
                tl2 = b.op("scalar", lambda e: e.activation(
                    XC[:], XN[:], AF.Square, accum_out=TCOL[:, 6:7]),
                    wait=[tsdot[0]], inc=[(s_se, 1)])
                tv2 = b.op("vector", lambda e: e.tensor_scalar(
                    TCOL[:, 7:8], TCOL[:, 6:7], 1.0 / D, 1e-5, ALU.mult, ALU.add),
                    wait=[tl2[0]], inc=[(s_ve, 1)])
                tln2 = b.op("scalar", lambda e: e.activation(
                    TCOL2[:, 0:1], TCOL[:, 7:8], AF.Ln), wait=[tv2[0]],
                    inc=[(s_se, 1)])
                trn = b.op("scalar", lambda e, t=t: e.activation(
                    bcol(RNC, t), TCOL2[:, 0:1], AF.Exp, scale=-0.5),
                    inc=[(s_se, 1)])
                tmx = b.op("vector", lambda e: e.tensor_scalar(
                    TCOL2[:, 3:4], TCOL[:, 6:7], 1e-24, None, ALU.max),
                    wait=[tl2[0]], inc=[(s_ve, 1)])
                tsl = b.op("scalar", lambda e: e.activation(
                    TCOL2[:, 1:2], TCOL2[:, 3:4], AF.Ln, scale=1.0 / 1024.0),
                    wait=[tmx[0]], inc=[(s_se, 1)])
                tbf = b.op("scalar", lambda e, t=t: e.activation(
                    bcol(BFC, t), TCOL2[:, 1:2], AF.Exp, scale=-0.5),
                    inc=[(s_se, 1)])
                tbase = b.op("scalar", lambda e, t=t: e.activation(
                    TBF1[:], XN[:], AF.Identity, scale=bcol(BFC, t)),
                    inc=[(s_se, 1)])
                txnr = b.op("scalar", lambda e, t=t: e.activation(
                    TBF2[:], XN[:], AF.Identity, scale=bcol(RNC, t)),
                    wait=[trn[0]], inc=[(s_se, 1)])
                b.op("vector", lambda e, xt=xt: e.tensor_add(xt, xt, XN[:]),
                     wait=[txn[0]], inc=[(s_ve, 1)])
                srcdst = [(TBF1, BASET, tbase, 0), (TBF2, XNT, txnr, 1)]
            else:
                tcast = b.op("scalar", lambda e: e.copy(TBF2[:], XN[:]),
                             inc=[(s_se, 1)])
                srcdst = [(TBF2, XNT, tcast, 1)]

            for src, dst, tok, bank in srcdst:
                scaled = first and (dst is XNT)
                for dt in range(ND):
                    w0 = ([tok[0]] + ([b.ln_bankfree[bank]]
                                      if b.ln_bankfree[bank] else []))
                    tt = b.op("tensor", lambda e, src=src, dt=dt, bank=bank:
                              e.transpose(psb(bank, 128, off=dt * 128),
                                          src[:, dt * 128:(dt + 1) * 128], IDB[:]),
                              wait=(w0 if dt == 0 else []),
                              inc=[(s_te, 1)])
                if scaled:
                    for dt in range(ND):
                        tev = b.op("scalar",
                                   lambda e, dst=dst, dt=dt, t=t, bank=bank:
                                   e.activation(
                            dst[:, dt * T + t * 128: dt * T + (t + 1) * 128],
                            psb(bank, 128, off=dt * 128), AF.Identity,
                            scale=bcol(NWC, dt)),
                            wait=[tt[0]] if dt == 0 else [], inc=[(s_se, 1)])
                else:
                    tev = b.op("scalar", lambda e, dst=dst, t=t, bank=bank:
                               e.copy(
                        dst.rearrange("p (a q) -> p a q", a=ND)[
                            :, :, t * 128:(t + 1) * 128],
                        psb(bank, 1024).rearrange("p (a q) -> p a q", a=ND)),
                        wait=[tt[0]], inc=[(s_se, 1)])
                b.ln_bankfree[bank] = tev[0]

            if not first:
                prevt = None
                for dt in range(ND):
                    fb = 2 + (dt % 2)
                    tt3 = b.op("tensor", lambda e, dt=dt, fb=fb: e.transpose(
                        psf(fb, 128), XN[:, dt * 128:(dt + 1) * 128], IDF[:]),
                        wait=[txn[0]] + ([prevt] if prevt else []),
                        inc=[(s_te, 1)])
                    tev3 = b.op("scalar", lambda e, dt=dt, fb=fb: e.copy(
                        XC[:, dt * 128:(dt + 1) * 128], psf(fb, 128)),
                        wait=[tt3[0]], inc=[(s_se, 1)])
                    prevt = tev3[0]
                lg = psf(4, NEXP)
                for dt in range(ND):
                    b.op("tensor", lambda e, dt=dt, lg=lg: e.matmul(
                        lg, XC[:, dt * 128:(dt + 1) * 128],
                        RWF.rearrange("p (a c) -> p a c", a=ND)[:, dt],
                        start=(dt == 0), stop=False, skip_group_check=True),
                        wait=[prevt] if dt == 0 else [], inc=[(s_te, 1)])
                tbias = b.op("tensor", lambda e, lg=lg: e.matmul(
                    lg, ONES[:], RBROW[:], start=False, stop=True,
                    skip_group_check=True), inc=[(s_te, 1)])
                tnm2 = b.op("vector", lambda e, lg=lg: e.reduce_max(
                    TCOL2[:, 4:5], lg, axis=AX.X, negate=True),
                    wait=[tbias[0]], inc=[(s_ve, 1)])
                tz = b.op("scalar", lambda e, lg=lg: e.activation(
                    TCOL2[:, 8:16], lg, AF.Exp, bias=TCOL2[:, 4:5],
                    accum_out=TCOL2[:, 5:6]),
                    wait=[tnm2[0]], inc=[(s_se, 1)])
                tlz = b.op("scalar", lambda e: e.activation(
                    TCOL2[:, 6:7], TCOL2[:, 5:6], AF.Ln),
                    wait=[tz[0]], inc=[(s_se, 1)])
                tmp_ = b.op("scalar", lambda e: e.activation(
                    TCOL2[:, 7:8], TCOL2[:, 6:7], AF.Exp, scale=-1.0),
                    inc=[(s_se, 1)])
                pv = b.op("vector", lambda e, lg=lg: e.tensor_scalar(
                    TCOL[:, 0:8], lg, TCOL2[:, 4:5], 0.0, ALU.add, ALU.is_equal),
                    wait=[tz[0]], inc=[(s_ve, 1)])
                b.wait("tensor", [pv[0]])
                pv = b.op("vector", lambda e: e.memset(TCOL[:, 8:9], 0.0),
                          wait=[pv[0]], inc=[(s_ve, 1)])
                for j in range(1, NEXP):
                    pv = b.op("vector", lambda e, j=j: e.tensor_add(
                        TCOL[:, 8 + j:9 + j], TCOL[:, 7 + j:8 + j],
                        TCOL[:, j - 1:j]), wait=[pv[0]], inc=[(s_ve, 1)])
                pv = b.op("vector", lambda e: e.tensor_scalar(
                    TCOL2[:, 8:16], TCOL[:, 8:16], 0.0, None, ALU.is_equal),
                    wait=[pv[0]], inc=[(s_ve, 1)])
                pv = b.op("vector", lambda e: e.tensor_mul(
                    TCOL[:, 8:16], TCOL2[:, 8:16], TCOL[:, 0:8]),
                    wait=[pv[0]], inc=[(s_ve, 1)])
                b.op("vector", lambda e, t=t: e.tensor_scalar(
                    SELC[:, t * NEXP:(t + 1) * NEXP], TCOL[:, 8:16],
                    TCOL2[:, 7:8], None, ALU.mult),
                    wait=[tmp_[0], pv[0]], inc=[(s_ve, 1)])
        b.barrier()

        nm = "baset_b%d" % bi
        if first and tap(nm, (128, ND * T), BF16) is not None:
            b.op("sync", lambda e: e.dma_start(out=tap_handles[nm], in_=BASET[:]),
                 inc=[(s_out, 16)])
        nm2 = "xnt%d_b%d" % (which, bi)
        if tap(nm2, (128, ND * T), BF16) is not None:
            b.op("sync", lambda e: e.dma_start(out=tap_handles[nm2], in_=XNT[:]),
                 inc=[(s_out, 16)])
        if first and tap("s128_b%d" % bi, (128, NT)) is not None:
            b.op("sync", lambda e, bi=bi: e.dma_start(
                out=tap_handles["s128_b%d" % bi], in_=S128[:]), inc=[(s_out, 16)])
        if not first and tap("selc_b%d" % bi, (128, NT * NEXP)) is not None:
            b.op("sync", lambda e, bi=bi: e.dma_start(
                out=tap_handles["selc_b%d" % bi], in_=SELC[:]), inc=[(s_out, 16)])

    def offload_X(bi, dst=None):
        dst = scratch_x1 if dst is None else dst
        toks = []
        for t in range(NT):
            tk = b.op("sync", lambda e, t=t, dst=dst: e.dma_start(
                out=hbm_tiles(dst[bi])[t], in_=bcol(X, t, D)),
                inc=[(s_dma, 16)])
            toks.append(tk[0])
        return toks

    HT = arb(96 * KB, ND * T)
    OWT = arb(128 * KB, ND * 1024)
    SA = arf(144 * KB, T)
    SU = arf(152 * KB, T)
    SH = arf(0 * KB, T)
    SCc = arf(8 * KB, T)
    SD = arf(16 * KB, T)
    WROW = arf(24 * KB, 2048)
    WT4 = arb(128 * KB, 4 * ND * 128)

    def wt(m, dt):
        return WT4[:, (m * ND + dt) * 128:(m * ND + dt + 1) * 128]

    def stage_ssm(bi):
        mats = ["L0_attn_A_w", "L0_attn_B_w", "L0_attn_C_w", "L0_attn_D_w"]
        last_wrow_consumer = None
        evs = []
        mevs = {}
        scan_tok = None
        for nt in range(ND):
            prev_evs = list(evs)
            tl1 = b.op("sync", lambda e, nt=nt: e.dma_start(
                out=WROW[:, 0:1024],
                in_=g[mats[0]][nt * 128:(nt + 1) * 128, :]),
                wait=[last_wrow_consumer] if last_wrow_consumer else [],
                inc=[(s_dma, 16)])
            tl2 = b.op("sync", lambda e, nt=nt: e.dma_start(
                out=WROW[:, 1024:2048],
                in_=g[mats[1]][nt * 128:(nt + 1) * 128, :]), inc=[(s_dma, 16)])
            trans_done = []
            for m in range(4):
                if m == 2:
                    tl1 = b.op("sync", lambda e, nt=nt: e.dma_start(
                        out=WROW[:, 0:1024],
                        in_=g[mats[2]][nt * 128:(nt + 1) * 128, :]),
                        wait=trans_done[-2:], inc=[(s_dma, 16)])
                    tl2 = b.op("sync", lambda e, nt=nt: e.dma_start(
                        out=WROW[:, 1024:2048],
                        in_=g[mats[3]][nt * 128:(nt + 1) * 128, :]),
                        inc=[(s_dma, 16)])
                src_off = (m % 2) * 1024
                dtok = tl1 if m % 2 == 0 else tl2
                tb0 = 2 if m % 2 == 0 else 6
                w0 = ([dtok[0]] + mevs.get(("bank", tb0), []) +
                      mevs.get(("bank", tb0 + 1), []) + prev_evs)
                for dt in range(ND):
                    bank = tb0 + dt // 4
                    off = (dt % 4) * 128
                    tt = b.op("tensor",
                              lambda e, src_off=src_off, dt=dt, bank=bank,
                              off=off: e.transpose(
                                  psf(bank, 128, off=off),
                                  WROW[:, src_off + dt * 128:
                                       src_off + (dt + 1) * 128], IDF[:]),
                              wait=(w0 if dt == 0 else []),
                              inc=[(s_te, 1)])
                tev = b.op("scalar", lambda e, m=m, tb0=tb0: e.copy(
                    WT4.rearrange("p (a q) -> p a q", a=4 * ND)[
                        :, m * ND:(m + 1) * ND],
                    PSF[:, tb0 * 512:(tb0 + 2) * 512].rearrange(
                        "p (a q) -> p a q", a=ND)),
                    wait=[tt[0]], inc=[(s_se, 1)])
                mevs[("bank", tb0)] = [tev[0]]
                mevs[("bank", tb0 + 1)] = [tev[0]]
                trans_done.append(tev[0])
                b.wait("tensor", [tev[0]])
            last_wrow_consumer = trans_done[-1]
            wt_ready = trans_done[-1]

            order = [0, 2, 3, 1]
            quad = {0: 0, 2: 4, 3: 0, 1: 4}
            gstop = {}
            ev_quad = {0: evs, 4: evs}
            sa_ev = sc_ev = sd_ev = None
            u_evs = []
            for m in order:
                qb = quad[m]
                for dt in range(ND):
                    for tc in range(4):
                        tmm = b.op("tensor",
                                   lambda e, m=m, dt=dt, tc=tc, qb=qb: e.matmul(
                                       psf(qb + tc, 512), wt(m, dt),
                                       XNT[:, dt * T + tc * 512:
                                           dt * T + (tc + 1) * 512],
                                       start=(dt == 0), stop=(dt == ND - 1),
                                       skip_group_check=True),
                                   wait=(([wt_ready] + list(ev_quad[qb]))
                                         if dt == 0 and tc == 0 else []),
                                   inc=[(s_te, 1)])
                gstop[m] = tmm[0]
                if m == 0:
                    toks = []
                    for tc in range(4):
                        ta_ = b.op("scalar", lambda e, tc=tc, nt=nt, qb=qb:
                                   e.activation(
                            SA[:, tc * 512:(tc + 1) * 512], psf(qb + tc, 512),
                            AF.Sigmoid, bias=bcol(ABC, nt)),
                            wait=[tmm[0]] if tc == 0 else [], inc=[(s_se, 1)])
                        toks.append(ta_[0])
                    sa_ev = toks
                    ev_quad[qb] = toks
                elif m == 2:
                    toks = []
                    for tc in range(4):
                        tc_ = b.op("scalar", lambda e, tc=tc, qb=qb:
                                   e.activation(
                            SCc[:, tc * 512:(tc + 1) * 512], psf(qb + tc, 512),
                            AF.Sigmoid),
                            wait=[tmm[0]] if tc == 0 else [], inc=[(s_se, 1)])
                        toks.append(tc_[0])
                    sc_ev = toks
                    ev_quad[qb] = toks
                elif m == 3:
                    toks = []
                    for tc in range(4):
                        td_ = b.op("scalar", lambda e, tc=tc, qb=qb:
                                   e.activation(
                            SD[:, tc * 512:(tc + 1) * 512], psf(qb + tc, 512),
                            AF.Sigmoid),
                            wait=[tmm[0]] if tc == 0 else [], inc=[(s_se, 1)])
                        toks.append(td_[0])
                    sd_ev = toks
                    ev_quad[qb] = toks
                else:
                    toks = []
                    for tc in range(4):
                        tu_ = b.op("vector", lambda e, tc=tc, qb=qb:
                                   e.tensor_mul(
                            SU[:, tc * 512:(tc + 1) * 512],
                            SD[:, tc * 512:(tc + 1) * 512], psf(qb + tc, 512)),
                            wait=([tmm[0], sd_ev[-1]] if tc == 0 else []),
                            inc=[(s_ve, 1)])
                        toks.append(tu_[0])
                    u_evs = toks
                    ev_quad[qb] = toks
            tsc = b.op("vector", lambda e: e.tensor_tensor_scan(
                SH[:], SA[:], SU[:], 0.0, ALU.mult, ALU.add),
                wait=[sa_ev[-1], u_evs[-1]], inc=[(s_ve, 1)])
            thc = b.op("vector", lambda e, nt=nt: e.tensor_mul(
                arb(96 * KB + nt * 2 * T, T), SCc[:], SH[:]),
                wait=[tsc[0], sc_ev[-1]], inc=[(s_ve, 1)])
            evs = [sa_ev[-1], sc_ev[-1], sd_ev[-1], u_evs[-1], tsc[0], thc[0]]

        if tap("ht_b%d" % bi, (128, ND * T), BF16) is not None:
            b.barrier()
            b.op("sync", lambda e, bi=bi: e.dma_start(
                out=tap_handles["ht_b%d" % bi], in_=HT), inc=[(s_out, 16)])
        if tap("sascu_b%d" % bi, (128, 4 * T)) is not None:
            b.barrier()
            th_ = tap_handles["sascu_b%d" % bi]
            b.op("sync", lambda e, th_=th_: e.dma_start(out=th_[:, 0:T], in_=SA),
                 inc=[(s_out, 16)])
            b.op("sync", lambda e, th_=th_: e.dma_start(out=th_[:, T:2 * T], in_=SU),
                 inc=[(s_out, 16)])
            b.op("sync", lambda e, th_=th_: e.dma_start(out=th_[:, 2 * T:3 * T], in_=SH),
                 inc=[(s_out, 16)])
            b.op("sync", lambda e, th_=th_: e.dma_start(out=th_[:, 3 * T:4 * T], in_=SCc),
                 inc=[(s_out, 16)])

        prev = last_wrow_consumer
        first_extra = list(evs)
        for dtt in range(ND):
            tl = b.op("sync", lambda e, dtt=dtt: e.dma_start(
                out=WROW[:, 0:1024],
                in_=g["L0_attn_out_w"][dtt * 128:(dtt + 1) * 128, :]),
                wait=([prev] if prev else []) + (first_extra if dtt == 0 else []),
                inc=[(s_dma, 16)])
            first_extra = []
            for ntt in range(ND):
                bank = 2 + ntt // 4
                off = (ntt % 4) * 128
                tt = b.op("tensor", lambda e, ntt=ntt, bank=bank, off=off:
                          e.transpose(psf(bank, 128, off=off),
                                      WROW[:, ntt * 128:(ntt + 1) * 128],
                                      IDF[:]),
                          wait=([tl[0]] + ([prev] if prev else []))
                          if ntt == 0 else [],
                          inc=[(s_te, 1)])
            tev = b.op("scalar", lambda e, dtt=dtt: e.copy(
                OWT.rearrange("p (a q) -> p a q", a=ND)[:, :,
                                                        dtt * 128:(dtt + 1) * 128],
                PSF[:, 2 * 512: 4 * 512].rearrange("p (a q) -> p a q", a=ND)),
                wait=[tt[0]], inc=[(s_se, 1)])
            prev = tev[0]
            b.wait("tensor", [prev])
        b.barrier()

    def stage_yadd(bi):
        XSTG = [arf(0 * KB, D), arf(4 * KB, D)]
        slot_free = [None, None]
        prev_add = {}
        for t in range(NT):
            sl = t % 2
            tk = b.op("sync", lambda e, t=t, sl=sl: e.dma_start(
                out=XSTG[sl][:], in_=hbm_tiles(scratch_x1[bi])[t]),
                wait=[slot_free[sl]] if slot_free[sl] else [],
                inc=[(s_dma, 16)])
            adds = []
            for dc in range(2):
                bank = (t * 2 + dc) % 4
                for ntt in range(ND):
                    b.op("tensor", lambda e, t=t, dc=dc, ntt=ntt, bank=bank:
                         e.matmul(psf(bank, 512),
                                  HT[:, ntt * T + t * 128: ntt * T + (t + 1) * 128],
                                  OWT[:, ntt * 1024 + dc * 512:
                                      ntt * 1024 + (dc + 1) * 512],
                                  start=(ntt == 0), stop=False,
                                  skip_group_check=True),
                         wait=([prev_add[bank]] if ntt == 0 and bank in prev_add
                               else []),
                         inc=[(s_te, 1)])
                tb_ = b.op("tensor", lambda e, dc=dc, bank=bank: e.matmul(
                    psf(bank, 512), ONESB[:],
                    ROWSB[0:1, dc * 512:(dc + 1) * 512],
                    start=False, stop=True, skip_group_check=True),
                    inc=[(s_te, 1)])
                tadd = b.op("vector", lambda e, sl=sl, dc=dc, bank=bank:
                            e.tensor_add(
                                XSTG[sl][:, dc * 512:(dc + 1) * 512],
                                XSTG[sl][:, dc * 512:(dc + 1) * 512],
                                psf(bank, 512)),
                            wait=[tb_[0], tk[0]], inc=[(s_ve, 1)])
                prev_add[bank] = tadd[0]
                adds.append(tadd[0])
            tw = b.op("sync", lambda e, t=t, sl=sl: e.dma_start(
                out=hbm_tiles(scratch_x1[bi])[t], in_=XSTG[sl][:]),
                wait=adds, inc=[(s_dma, 16)])
            slot_free[sl] = tw[0]
        b.barrier()

    def routing_steps(bi):
        steps = []
        VV = RT[:, 0:NT]
        TMP = RT[:, 16:32]
        BC1 = RT[:, 32:33]
        BC2 = RT[:, 33:34]
        PC = RT[:, 34:35]
        EQM = RT[:, 35:51]
        ch = {"t": None}
        SEMS = {"vector": s_ve, "scalar": s_se, "tensor": s_te, "sync": s_out}

        def rop(eng, fn, extra=()):
            tok = b.op(eng, fn,
                       wait=([ch["t"]] if ch["t"] else []) + list(extra),
                       inc=[(SEMS[eng], 1)] if eng != "sync" else [(s_out, 16)])
            ch["t"] = tok[0]
            return tok[0]

        def rpart_reduce(src_col, dst_col, alu):
            rop("tensor", lambda e, src_col=src_col: e.transpose(
                psf(5, 128)[0:1, :], src_col, IDF[:]))
            rop("scalar", lambda e: e.copy(ROW128[:], psf(5, 128)[0:1, :]))
            b.wait("tensor", [ch["t"]])
            rop("vector", lambda e, dst_col=dst_col, alu=alu: e.tensor_reduce(
                RSC[0:1, dst_col:dst_col + 1], ROW128[:], axis=AX.X, op=alu))

        def rbcast(src_off, n, dst):
            rop("tensor", lambda e, src_off=src_off, n=n: e.matmul(
                psf(5, n), ONES[:], RSC[0:1, src_off:src_off + n],
                start=True, stop=True))
            rop("scalar", lambda e, dst=dst, n=n: e.copy(dst, psf(5, n)))
            b.wait("tensor", [ch["t"]])

        def s_init():
            rop("vector", lambda e: e.reduce_max(PC, S128[:], axis=AX.X))
            rpart_reduce(PC, 2, ALU.max)
            rop("vector", lambda e: e.memset(RSC[0:1, 0:1], 0.5753641128540039))
        steps.append(s_init)

        def cd_iter(_it):
            rop("vector", lambda e: e.tensor_tensor(
                RSC[0:1, 9:10], RSC[0:1, 2:3], RSC[0:1, 0:1], ALU.min))
            rop("vector", lambda e: e.tensor_scalar(
                RSC[0:1, 1:2], RSC[0:1, 9:10], -1.0, None, ALU.mult))
            rbcast(0, 2, RT[:, 32:34])
            rop("vector", lambda e: e.tensor_scalar(
                TMP, S128[:], BC1, BC2, ALU.min, ALU.add))
            rop("scalar", lambda e: e.activation(
                TMP, TMP, AF.Exp, accum_out=PC))
            rpart_reduce(PC, 3, ALU.add)
            rop("scalar", lambda e: e.activation(
                RSC[0:1, 3:4], RSC[0:1, 3:4], AF.Ln))
            rop("vector", lambda e: e.tensor_tensor(
                RSC[0:1, 9:10], RSC[0:1, 3:4], RSC[0:1, 1:2], ALU.subtract))
            rop("vector", lambda e: e.tensor_scalar(
                RSC[0:1, 0:1], RSC[0:1, 9:10], -LOGK, None, ALU.add))
        for it in range(NITER_CD - 1):
            steps.append(lambda it=it: cd_iter(it))

        def v_init():
            rop("vector", lambda e: e.tensor_scalar(
                RSC[0:1, 9:10], RSC[0:1, 0:1], -1.0, None, ALU.mult))
            rbcast(9, 1, RT[:, 32:33])
            rop("vector", lambda e: e.tensor_scalar(
                VV, S128[:], BC1, 0.0, ALU.add, ALU.min))
            rop("vector", lambda e: e.tensor_reduce(
                PC, VV, axis=AX.X, op=ALU.min))
            rpart_reduce(PC, 9, ALU.min)
            rop("vector", lambda e: e.tensor_scalar(
                RSC[0:1, 5:6], RSC[0:1, 9:10], -1.0, None, ALU.add))
            rop("vector", lambda e: e.memset(RSC[0:1, 6:7], 1.0))
            rop("vector", lambda e: e.memset(RSC[0:1, 12:13], 1024.0))
        steps.append(v_init)

        def count_vv(thr_col, cmp_op, out_col):
            rbcast(thr_col, 1, RT[:, 32:33])
            rop("vector", lambda e, cmp_op=cmp_op: e.tensor_scalar(
                TMP, VV, BC1, None, cmp_op))
            rop("vector", lambda e: e.reduce_sum(PC, TMP, axis=AX.X))
            rpart_reduce(PC, out_col, ALU.add)

        def count_ix():
            rbcast(15, 1, RT[:, 32:33])
            rop("vector", lambda e: e.tensor_scalar(
                TMP, IOTA[:], BC1, None, ALU.is_le))
            rop("vector", lambda e: e.tensor_mul(TMP, TMP, EQM))
            rop("vector", lambda e: e.reduce_sum(PC, TMP, axis=AX.X))
            rpart_reduce(PC, 4, ALU.add)

        def bis_iter(locol, hicol, midcol, cntfn, invert):
            rop("vector", lambda e: e.tensor_tensor(
                RSC[0:1, midcol:midcol + 1], RSC[0:1, locol:locol + 1],
                RSC[0:1, hicol:hicol + 1], ALU.add))
            rop("vector", lambda e: e.tensor_scalar(
                RSC[0:1, midcol:midcol + 1], RSC[0:1, midcol:midcol + 1],
                0.5, None, ALU.mult))
            cntfn()
            cmp_op = ALU.is_lt if invert else ALU.is_ge
            rop("vector", lambda e, cmp_op=cmp_op: e.tensor_tensor(
                RSC[0:1, 8:9], RSC[0:1, 4:5], RSC[0:1, 12:13], cmp_op))
            rop("vector", lambda e: e.tensor_tensor(
                RSC[0:1, 9:10], RSC[0:1, midcol:midcol + 1],
                RSC[0:1, locol:locol + 1], ALU.subtract))
            rop("vector", lambda e: e.tensor_mul(
                RSC[0:1, 9:10], RSC[0:1, 9:10], RSC[0:1, 8:9]))
            rop("vector", lambda e: e.tensor_add(
                RSC[0:1, locol:locol + 1], RSC[0:1, locol:locol + 1],
                RSC[0:1, 9:10]))
            rop("vector", lambda e: e.tensor_scalar(
                RSC[0:1, 8:9], RSC[0:1, 8:9], -1.0, 1.0, ALU.mult, ALU.add))
            rop("vector", lambda e: e.tensor_tensor(
                RSC[0:1, 9:10], RSC[0:1, midcol:midcol + 1],
                RSC[0:1, hicol:hicol + 1], ALU.subtract))
            rop("vector", lambda e: e.tensor_mul(
                RSC[0:1, 9:10], RSC[0:1, 9:10], RSC[0:1, 8:9]))
            rop("vector", lambda e: e.tensor_add(
                RSC[0:1, hicol:hicol + 1], RSC[0:1, hicol:hicol + 1],
                RSC[0:1, 9:10]))

        for it in range(NITER_TH):
            steps.append(lambda it=it: bis_iter(
                5, 6, 7, lambda: count_vv(7, ALU.is_gt, 4), False))

        def theta_final():
            rbcast(6, 1, RT[:, 32:33])
            rop("vector", lambda e: e.tensor_scalar(
                TMP, VV, BC1, None, ALU.is_le))
            rop("vector", lambda e: e.tensor_scalar(
                EQM, TMP, -1.0, 1.0, ALU.mult, ALU.add))
            rop("vector", lambda e: e.tensor_scalar(
                EQM, EQM, -1e30, None, ALU.mult))
            rop("vector", lambda e: e.tensor_mul(TMP, TMP, VV))
            rop("vector", lambda e: e.tensor_add(TMP, TMP, EQM))
            rop("vector", lambda e: e.reduce_max(PC, TMP, axis=AX.X))
            rpart_reduce(PC, 10, ALU.max)
            count_vv(10, ALU.is_gt, 11)
            rop("vector", lambda e: e.tensor_scalar(
                RSC[0:1, 12:13], RSC[0:1, 11:12], -1.0, 1024.0, ALU.mult,
                ALU.add))
            rop("vector", lambda e: e.tensor_scalar(
                EQM, VV, BC1, None, ALU.is_equal))
            rop("vector", lambda e: e.memset(RSC[0:1, 13:14], -1.0))
            rop("vector", lambda e: e.memset(RSC[0:1, 14:15], 2047.0))
        steps.append(theta_final)

        for it in range(NITER_IX):
            steps.append(lambda it=it: bis_iter(13, 14, 15, count_ix, True))

        def mask_final():
            rbcast(10, 1, RT[:, 32:33])
            rop("vector", lambda e: e.tensor_scalar(
                TMP, VV, BC1, None, ALU.is_gt))
            rbcast(14, 1, RT[:, 33:34])
            rop("vector", lambda e: e.tensor_scalar(
                MASKC[:], IOTA[:], BC2, None, ALU.is_le))
            rop("vector", lambda e: e.tensor_mul(MASKC[:], MASKC[:], EQM))
            rop("vector", lambda e: e.tensor_add(MASKC[:], MASKC[:], TMP))
            if tap("maskc_b%d" % bi, (128, NT)) is not None:
                rop("sync", lambda e, bi=bi: e.dma_start(
                    out=tap_handles["maskc_b%d" % bi], in_=MASKC[:]))
            if tap("vsel_b%d" % bi, (128, NT + 2)) is not None:
                th_ = tap_handles["vsel_b%d" % bi]
                rop("sync", lambda e, th_=th_: e.dma_start(
                    out=th_[:, 0:NT], in_=VV))
                rop("sync", lambda e, th_=th_: e.dma_start(
                    out=th_[0:1, NT:NT + 2], in_=RSC[0:1, 10:12]))
        steps.append(mask_final)
        return steps

    WQL = arb(0 * KB, ND * 512)
    WKVL = arb(8 * KB, ND * 1024)
    WOL = arb(24 * KB, 4 * 1024)
    OT = arb(32 * KB, 4 * T)
    QTW = arb(48 * KB, 512)
    KTW = arb(49 * KB, 512)
    VW = arb(50 * KB, 512)
    EW = arb(51 * KB, 128)
    ETW = arb(52 * KB, 128)
    OLW = arb(53 * KB, 512)
    ZL8 = arf(54 * KB, 8)
    RZ8 = arf(55 * KB, 8)
    WSTG = arf(56 * KB, 1024)

    def load_wcast(dst, src_ap, cols, scale_col=None, wait=(), tag=0):
        tl = b.op("sync", lambda e, src_ap=src_ap, cols=cols: e.dma_start(
            out=WSTG[:, 0:cols], in_=src_ap), wait=list(wait), inc=[(s_dma, 16)])
        if scale_col is None:
            tc = b.op("scalar", lambda e, dst=dst, cols=cols: e.copy(
                dst, WSTG[:, 0:cols]), wait=[tl[0]], inc=[(s_se, 1)])
        else:
            tc = b.op("scalar", lambda e, dst=dst, cols=cols,
                      scale_col=scale_col: e.activation(
                dst, WSTG[:, 0:cols], AF.Identity, scale=scale_col),
                wait=[tl[0]], inc=[(s_se, 1)])
        return tc[0]

    def stage_light(bi, rsteps):
        xload = []
        for t in range(NT):
            tk = b.op("sync", lambda e, t=t: e.dma_start(
                out=bcol(X, t, D), in_=hbm_tiles(scratch_x1[bi])[t]),
                inc=[(s_dma, 16)])
            xload.append(tk[0])
        b.xload_light = xload
        wtoks = []
        prev = None
        for dt in range(ND):
            prev = load_wcast(bcol(WQL, dt, 512),
                              g["L0_ffn_wq_light"][dt * 128:(dt + 1) * 128, :],
                              512, scale_col=bcol(GLC, dt),
                              wait=[prev] if prev else [])
            wtoks.append(prev)
        for dt in range(ND):
            prev = load_wcast(bcol(WKVL, dt, 1024),
                              g["L0_ffn_wkv_light"][dt * 128:(dt + 1) * 128, :],
                              1024, scale_col=bcol(GLC, dt), wait=[prev])
            wtoks.append(prev)
        for ht in range(4):
            prev = load_wcast(bcol(WOL, ht, 1024),
                              g["L0_ffn_wo_light"][ht * 128:(ht + 1) * 128, :],
                              1024, wait=[prev])
            wtoks.append(prev)
        wq_ready = wtoks[ND - 1]
        wkv_ready = wtoks[2 * ND - 1]
        wo_ready = wtoks[-1]

        rs_i = 0
        prev_otev = None
        for t in range(NT):
            n_per = (len(rsteps) + NT - 1) // NT
            for _ in range(n_per):
                if rs_i < len(rsteps):
                    rsteps[rs_i]()
                    rs_i += 1
            evq = []
            for mj in range(4):
                bank = mj % 2
                for dt in range(ND):
                    tmm = b.op("tensor", lambda e, mj=mj, dt=dt, t=t, bank=bank:
                               e.matmul(psf(bank, 128),
                                        WQL[:, dt * 512 + mj * 128:
                                            dt * 512 + (mj + 1) * 128],
                                        BASET[:, dt * T + t * 128:
                                              dt * T + (t + 1) * 128],
                                        start=(dt == 0), stop=(dt == ND - 1),
                                        skip_group_check=True),
                               wait=([wq_ready] if dt == 0 and t == 0 and mj < 2
                                     else ([evq[mj - 2]] if dt == 0 and mj >= 2
                                           else ([prev_otev] if dt == 0 else []))),
                               inc=[(s_te, 1)])
                tev = b.op("scalar", lambda e, mj=mj, bank=bank: e.activation(
                    bcol(QTW, mj, 128), psf(bank, 128), AF.Copy, scale=0.125),
                    wait=[tmm[0]], inc=[(s_se, 1)])
                evq.append(tev[0])
            evk = []
            for mj in range(4):
                bank = 2 + mj % 2
                for dt in range(ND):
                    tmm = b.op("tensor", lambda e, mj=mj, dt=dt, t=t, bank=bank:
                               e.matmul(psf(bank, 128),
                                        WKVL[:, dt * 1024 + mj * 128:
                                            dt * 1024 + (mj + 1) * 128],
                                        BASET[:, dt * T + t * 128:
                                              dt * T + (t + 1) * 128],
                                        start=(dt == 0), stop=(dt == ND - 1),
                                        skip_group_check=True),
                               wait=([wkv_ready] if dt == 0 and t == 0 and mj < 2
                                     else ([evk[mj - 2]] if dt == 0 and mj >= 2
                                           else [])),
                               inc=[(s_te, 1)])
                tev = b.op("scalar", lambda e, mj=mj, bank=bank: e.copy(
                    bcol(KTW, mj, 128), psf(bank, 128)),
                    wait=[tmm[0]], inc=[(s_se, 1)])
                evk.append(tev[0])
            for dt in range(ND):
                tmm = b.op("tensor", lambda e, dt=dt, t=t: e.matmul(
                    psf(4, 512),
                    BASET[:, dt * T + t * 128: dt * T + (t + 1) * 128],
                    WKVL[:, dt * 1024 + 512: dt * 1024 + 1024],
                    start=(dt == 0), stop=(dt == ND - 1),
                    skip_group_check=True),
                    wait=([evk[-1]] if dt == 0 and t == 0 else []),
                    inc=[(s_te, 1)])
            tvev = b.op("scalar", lambda e: e.copy(VW[:], psf(4, 512)),
                        wait=[tmm[0]], inc=[(s_se, 1)])
            b.wait("tensor", [tvev[0]])

            zevs = []
            oev = None
            for h in range(8):
                mj = h // 2
                pb_ = (h % 2) * 64
                tsc_ = b.op("tensor", lambda e, mj=mj, pb_=pb_: e.matmul(
                    psf(2, 128),
                    QTW[pb_:pb_ + 64, mj * 128:(mj + 1) * 128],
                    KTW[pb_:pb_ + 64, mj * 128:(mj + 1) * 128],
                    start=True, stop=True),
                    wait=[evq[mj], evk[mj]] + ([zevs[-1]] if zevs else []),
                    inc=[(s_te, 1)])
                tnm_ = b.op("vector", lambda e: e.reduce_max(
                    TCOL[:, 8:9], psf(2, 128), axis=AX.X, negate=True),
                    wait=[tsc_[0]], inc=[(s_ve, 1)])
                tex_ = b.op("scalar", lambda e, h=h: e.activation(
                    EW[:], psf(2, 128), AF.Exp, bias=TCOL[:, 8:9],
                    accum_out=bcol(ZL8, h)), wait=[tnm_[0]], inc=[(s_se, 1)])
                ttr_ = b.op("tensor", lambda e: e.transpose(
                    psb(0, 128), EW[:], IDB[:]),
                    wait=[tex_[0]], inc=[(s_te, 1)])
                tet_ = b.op("scalar", lambda e: e.copy(ETW[:], psb(0, 128)),
                            wait=[ttr_[0]], inc=[(s_se, 1)])
                tav_ = b.op("tensor", lambda e, h=h: e.matmul(
                    psf(3, 512)[:, h * 64:(h + 1) * 64], ETW[:],
                    VW[:, h * 64:(h + 1) * 64],
                    start=(h == 0), stop=(h == 7), skip_group_check=True),
                    wait=[tet_[0]] + ([oev] if h == 0 and oev else []),
                    inc=[(s_te, 1)])
                zevs.append(tex_[0])
            tlz = b.op("scalar", lambda e: e.activation(
                RZ8[:, 0:8], ZL8[:, 0:8], AF.Ln), wait=zevs, inc=[(s_se, 1)])
            trz = b.op("scalar", lambda e: e.activation(
                RZ8[:, 0:8], RZ8[:, 0:8], AF.Exp, scale=-1.0), inc=[(s_se, 1)])
            prev_t = None
            for h in range(8):
                tevo = b.op("scalar", lambda e, h=h: e.activation(
                    bcol(OLW, h, 64), psf(3, 512)[:, h * 64:(h + 1) * 64],
                    AF.Identity, scale=bcol(RZ8, h)),
                    wait=[tav_[0]] if h == 0 else [], inc=[(s_se, 1)])
                tto = b.op("tensor", lambda e, h=h: e.transpose(
                    psb(1, 128)[0:64, :], bcol(OLW, h, 64), IDB[:]),
                    wait=[tevo[0]] + ([prev_t] if prev_t else []),
                    inc=[(s_te, 1)])
                tote = b.op("scalar", lambda e, h=h, t=t: e.copy(
                    OT[(h % 2) * 64:(h % 2) * 64 + 64,
                       (h // 2) * T + t * 128:(h // 2) * T + (t + 1) * 128],
                    psb(1, 128)[0:64, 0:128]),
                    wait=[tto[0]], inc=[(s_se, 1)])
                prev_t = tote[0]
            oev = prev_t
            prev_otev = prev_t

            for dc in range(2):
                bank = dc
                for ht in range(4):
                    tmm = b.op("tensor", lambda e, ht=ht, dc=dc, t=t, bank=bank:
                               e.matmul(psf(bank, 512),
                                        OT[:, ht * T + t * 128:
                                           ht * T + (t + 1) * 128],
                                        WOL[:, ht * 1024 + dc * 512:
                                            ht * 1024 + (dc + 1) * 512],
                                        start=(ht == 0), stop=(ht == 3),
                                        skip_group_check=True),
                               wait=([prev_t, wo_ready] if ht == 0 else []),
                               inc=[(s_te, 1)])
                tadd = b.op("vector", lambda e, t=t, dc=dc, bank=bank:
                            e.tensor_add(
                                X[:, t * D + dc * 512: t * D + (dc + 1) * 512],
                                X[:, t * D + dc * 512: t * D + (dc + 1) * 512],
                                psf(bank, 512)),
                            wait=[tmm[0], xload[t]], inc=[(s_ve, 1)])
                b.wait("tensor", [tadd[0]])
        while rs_i < len(rsteps):
            rsteps[rs_i]()
            rs_i += 1
        b.barrier()

    WQH = arb(0 * KB, ND * 512)
    WKVTH = arb(8 * KB, ND * 1024)
    ETH = arb(24 * KB, NT * 256)
    OTH = arb(32 * KB, 4 * T)
    WSTGH = arf(48 * KB, 1024)
    QTM0 = arb(52 * KB, T)
    QTM1 = arb(56 * KB, T)
    KTO = arb(60 * KB, T)
    VH = arb(48 * KB, NT * 66)
    OHS = arb(50 * KB + 256, 512)
    MRW = arf(51 * KB + 256, 128)

    def stage_heavy(bi):
        b.ohs_free = None
        b.psb1_free = None

        def load_wcast_h(dst, src_ap, cols, scale_col=None, wait=()):
            tl = b.op("sync", lambda e, src_ap=src_ap, cols=cols: e.dma_start(
                out=WSTGH[:, 0:cols], in_=src_ap), wait=list(wait),
                inc=[(s_dma, 16)])
            if scale_col is None:
                tc = b.op("scalar", lambda e, dst=dst, cols=cols: e.copy(
                    dst, WSTGH[:, 0:cols]), wait=[tl[0]], inc=[(s_se, 1)])
            else:
                tc = b.op("scalar", lambda e, dst=dst, cols=cols,
                          scale_col=scale_col: e.activation(
                    dst, WSTGH[:, 0:cols], AF.Identity, scale=scale_col),
                    wait=[tl[0]], inc=[(s_se, 1)])
            return tc[0]

        prev = None
        for dt in range(ND):
            prev = load_wcast_h(bcol(WQH, dt, 512),
                                g["L0_ffn_wq_heavy"][dt * 128:(dt + 1) * 128, :],
                                512, scale_col=bcol(GHC, dt),
                                wait=[prev] if prev else [])
        for rt_ in range(ND):
            tl = b.op("sync", lambda e, rt_=rt_: e.dma_start(
                out=WSTGH[:, 0:1024],
                in_=g["L0_ffn_wkv_heavy"][rt_ * 128:(rt_ + 1) * 128, :]),
                wait=[prev], inc=[(s_dma, 16)])
            for dt in range(ND):
                fb = 4 + (dt % 2)
                tt = b.op("tensor", lambda e, dt=dt, fb=fb: e.transpose(
                    psf(fb, 128), WSTGH[:, dt * 128:(dt + 1) * 128], IDF[:]),
                    wait=[tl[0]] + ([prev] if prev else []), inc=[(s_te, 1)])
                tev = b.op("scalar", lambda e, dt=dt, rt_=rt_, fb=fb:
                           e.activation(
                    WKVTH[:, dt * 1024 + rt_ * 128: dt * 1024 + (rt_ + 1) * 128],
                    psf(fb, 128), AF.Identity, scale=bcol(GHC, dt)),
                    wait=[tt[0]], inc=[(s_se, 1)])
                prev = tev[0]
            b.wait("tensor", [prev])
        wall = prev

        tko = b.op("gpsimd", lambda e: e.memset(KTO[:], 1.0),
                   wait=[wall], inc=[(s_gp, 1)])
        tvo0 = b.op("gpsimd", lambda e: e.memset(VH[:], 1.0),
                    wait=[tko[0]], inc=[(s_gp, 1)])

        QTMs = [QTM0, QTM1]
        qtm_free = [None, None]
        kto_free = None
        vh_free = None
        last_ot = None
        NQC = 8
        for j in range(4):
            pevs = []
            for tc in range(4):
                bank = tc % 2
                for dt in range(ND):
                    tmm = b.op("tensor", lambda e, j=j, dt=dt, tc=tc, bank=bank:
                               e.matmul(psf(bank, 512),
                                        WQH[:, dt * 512 + j * 128:
                                            dt * 512 + (j + 1) * 128],
                                        BASET[:, dt * T + tc * 512:
                                              dt * T + (tc + 1) * 512],
                                        start=(dt == 0), stop=(dt == ND - 1),
                                        skip_group_check=True),
                               wait=(([wall] + [x for x in qtm_free if x] +
                                      (pevs[-2:] if pevs else []))
                                     if dt == 0 else []),
                               inc=[(s_te, 1)])
                ev2 = []
                for hh in range(2):
                    tev = b.op("scalar",
                               lambda e, tc=tc, hh=hh, bank=bank: e.activation(
                        QTMs[hh][hh * 0 + 0:64, tc * 512:(tc + 1) * 512]
                        if False else
                        QTMs[hh].rearrange("p f -> p f")[0:64,
                                                        tc * 512:(tc + 1) * 512],
                        psf(bank, 512)[hh * 64:hh * 64 + 64, :],
                        AF.Copy, scale=0.125),
                        wait=[tmm[0]], inc=[(s_se, 1)])
                    ev2.append(tev[0])
                pevs.append(ev2[-1])
            b.wait("tensor", [pevs[-1]])
            qproj_done = pevs[-1]

            for hh in range(2):
                h = 2 * j + hh
                QTM = QTMs[hh]
                kevs = []
                for tc in range(4):
                    for dt in range(ND):
                        tmm = b.op("tensor",
                                   lambda e, h=h, dt=dt, tc=tc: e.matmul(
                            psf(2, 512)[0:64, :],
                            WKVTH[:, dt * 1024 + h * 128:
                                  dt * 1024 + h * 128 + 64],
                            BASET[:, dt * T + tc * 512:
                                  dt * T + (tc + 1) * 512],
                            start=(dt == 0), stop=(dt == ND - 1),
                            skip_group_check=True),
                            wait=(([kevs[-1]] if kevs else
                                   [tko[0]] + ([kto_free] if kto_free else []))
                                  if dt == 0 else []),
                            inc=[(s_te, 1)])
                    tev = b.op("scalar", lambda e, tc=tc: e.copy(
                        KTO[0:64, tc * 512:(tc + 1) * 512],
                        psf(2, 512)[0:64, :]), wait=[tmm[0]], inc=[(s_se, 1)])
                    kevs.append(tev[0])
                vevs = []
                for kt in range(NT):
                    for dt in range(ND):
                        tmm = b.op("tensor",
                                   lambda e, h=h, dt=dt, kt=kt: e.matmul(
                            psf(3, 64),
                            BASET[:, dt * T + kt * 128: dt * T + (kt + 1) * 128],
                            WKVTH[:, dt * 1024 + h * 128 + 64:
                                  dt * 1024 + h * 128 + 128],
                            start=(dt == 0), stop=(dt == ND - 1),
                            skip_group_check=True),
                            wait=(([vevs[-1]] if vevs else
                                   [tvo0[0]] + ([vh_free] if vh_free else []))
                                  if dt == 0 else []),
                            inc=[(s_te, 1)])
                    tev = b.op("scalar", lambda e, kt=kt: e.copy(
                        VH[:, kt * 66: kt * 66 + 64], psf(3, 64)),
                        wait=[tmm[0]], inc=[(s_se, 1)])
                    vevs.append(tev[0])

                M128 = RT[:, 35:51]
                p1prev = {}
                for t in range(NT):
                    bank = 2 + t % 2
                    tmm1 = b.op("tensor",
                                lambda e, t=t, bank=bank, QTM=QTM: e.matmul(
                        psf(bank, 512),
                        QTM[0:64, t * 128:(t + 1) * 128],
                        KTO[0:64, 0:512], start=True, stop=True),
                        wait=([qproj_done, kevs[-1], vevs[-1]] if t == 0 else
                              ([p1prev[bank]] if bank in p1prev else [])),
                        inc=[(s_te, 1)])
                    trm = b.op("vector", lambda e, t=t, bank=bank, M128=M128:
                               e.reduce_max(
                        bcol(M128, t), psf(bank, 512),
                        axis=AX.X, negate=True),
                        wait=[tmm1[0]], inc=[(s_ve, 1)])
                    p1prev[bank] = trm[0]
                p1prev = trm[0]
                ttm = b.op("tensor", lambda e, M128=M128: e.transpose(
                    psf(2, 128)[0:16, :], M128, IDF[:]),
                    wait=[p1prev], inc=[(s_te, 1)])
                tme = b.op("scalar", lambda e: e.copy(
                    MRW[0:16, 0:128], psf(2, 128)[0:16, :]),
                    wait=[ttm[0]], inc=[(s_se, 1)])
                b.wait("tensor", [tme[0]])
                tmd = b.op("gpsimd", lambda e, QTM=QTM: e.dma_start(
                    out=QTM[64:65, 0:T], in_=MRW[0:16, 0:128]),
                    wait=[tme[0]], inc=[(s_gdma, 16)])

                et_last2 = [tmd[0], tmd[0]]
                avprev = [None, None]
                for qc in range(NQC):
                    etevs = []
                    for kt in range(NT):
                        bank = (kt % 4) // 2
                        off = (kt % 2) * 256
                        tmm = b.op("tensor",
                                   lambda e, kt=kt, qc=qc, bank=bank, off=off,
                                   QTM=QTM: e.matmul(
                            psf(bank, 256, off=off),
                            KTO[0:65, kt * 128:(kt + 1) * 128],
                            QTM[0:65, qc * 256:(qc + 1) * 256],
                            start=True, stop=True),
                            wait=([et_last2[bank]] if kt < 4 and kt % 2 == 0
                                  else ([etevs[kt // 2 - 2]]
                                        if kt % 2 == 0 and kt >= 4 else [])),
                            inc=[(s_te, 1)])
                        if kt % 2 == 1:
                            tex = b.op("scalar", lambda e, kt=kt, bank=bank:
                                       e.activation(
                                ETH[:, (kt - 1) * 256:(kt + 1) * 256],
                                psf(bank, 512), AF.Exp),
                                wait=[tmm[0]], inc=[(s_se, 1)])
                            etevs.append(tex[0])
                    et_last2 = [etevs[-2], etevs[-1]]
                    for ms in range(2):
                        bank = 4 + ms
                        for kt in range(NT):
                            tmm = b.op("tensor",
                                       lambda e, kt=kt, ms=ms, bank=bank:
                                       e.matmul(
                                PSF[:, bank * 512: bank * 512 + 66],
                                ETH[:, kt * 256 + ms * 128:
                                    kt * 256 + ms * 128 + 128],
                                VH[:, kt * 66:(kt + 1) * 66],
                                start=(kt == 0), stop=(kt == NT - 1),
                                skip_group_check=True),
                                wait=(([etevs[-1]] +
                                       ([avprev[ms]] if avprev[ms] else []))
                                      if kt == 0 else []),
                                inc=[(s_te, 1)])
                        sl = (qc * 2 + ms) % 8
                        trz = b.op("vector", lambda e, bank=bank, sl=sl:
                                   e.reciprocal(
                            TCOL2[:, sl:sl + 1],
                            PSF[:, bank * 512 + 64: bank * 512 + 65]),
                            wait=[tmm[0]] + ([b.ohs_free] if sl == 0 and
                                             b.ohs_free else []),
                            inc=[(s_ve, 1)])
                        tevo = b.op("vector", lambda e, bank=bank, sl=sl:
                                    e.tensor_scalar(
                            OHS[:, sl * 64:(sl + 1) * 64],
                            PSF[:, bank * 512: bank * 512 + 64],
                            TCOL2[:, sl:sl + 1], None, ALU.mult),
                            wait=[trz[0]], inc=[(s_ve, 1)])
                        avprev[ms] = tevo[0]
                        if sl == 7:
                            for j2 in range(8):
                                tto = b.op("tensor", lambda e, j2=j2:
                                           e.transpose(
                                    psb(1, 128, off=j2 * 128)[0:64, :],
                                    OHS[:, j2 * 64:(j2 + 1) * 64], IDB[:]),
                                    wait=([tevo[0]] + ([b.psb1_free] if
                                          b.psb1_free else []))
                                    if j2 == 0 else [],
                                    inc=[(s_te, 1)])
                            half8 = qc // 4
                            tote = b.op("scalar", lambda e, h=h, half8=half8:
                                        e.copy(
                                OTH[(h % 2) * 64:(h % 2) * 64 + 64,
                                    (h // 2) * T + half8 * 1024:
                                    (h // 2) * T + (half8 + 1) * 1024],
                                psb(1, 1024)[0:64, :]),
                                wait=[tto[0]], inc=[(s_se, 1)])
                            b.ohs_free = tto[0]
                            b.psb1_free = tote[0]
                            last_ot = tote[0]
                qtm_free[hh] = et_last2[1]
                kto_free = et_last2[1]
                vh_free = last_ot

        WOH2 = arb(0 * KB, 4 * 1024)
        prevw = None
        for ht in range(4):
            prevw = load_wcast_h(bcol(WOH2, ht, 1024),
                                 g["L0_ffn_wo_heavy"][ht * 128:(ht + 1) * 128, :],
                                 1024,
                                 wait=[last_ot] if ht == 0 else [prevw])
        woadd_prev = None
        for t in range(NT):
            for dc in range(2):
                bank = dc
                for ht in range(4):
                    tmm = b.op("tensor", lambda e, ht=ht, dc=dc, t=t, bank=bank:
                               e.matmul(psf(bank, 512),
                                        OTH[:, ht * T + t * 128:
                                            ht * T + (t + 1) * 128],
                                        WOH2[:, ht * 1024 + dc * 512:
                                             ht * 1024 + (dc + 1) * 512],
                                        start=(ht == 0), stop=(ht == 3),
                                        skip_group_check=True),
                               wait=(([prevw, last_ot] +
                                      ([woadd_prev] if woadd_prev else []))
                                     if ht == 0 else []),
                               inc=[(s_te, 1)])
                tadd = b.op("vector", lambda e, t=t, dc=dc, bank=bank:
                            e.scalar_tensor_tensor(
                                X[:, t * D + dc * 512: t * D + (dc + 1) * 512],
                                psf(bank, 512), bcol(MASKC, t),
                                X[:, t * D + dc * 512: t * D + (dc + 1) * 512],
                                ALU.mult, ALU.add),
                            wait=[tmm[0]], inc=[(s_ve, 1)])
                woadd_prev = tadd[0]
        b.barrier()

    WQ1 = arb(0 * KB, ND * 1024)
    WK1 = arb(16 * KB, ND * 64)
    WV1 = arb(17 * KB, ND * 64)
    ETH1 = arb(18 * KB, NT * 256)
    OT1H = arb(64 * KB, 8 * 1024)
    QTM1s = [arb(80 * KB, 1024), arb(82 * KB, 1024)]
    KTO1 = arb(84 * KB, T)
    V1 = arb(88 * KB, NT * 66)
    OHS1 = arb(91 * KB, 512)
    MRW1 = arf(92 * KB, 128)

    def stage_l1attn(bi):
        b.ohs_free = None
        b.psb1_free = None
        tk_ = b.op("gpsimd", lambda e: e.dma_start(
            out=WK1.rearrange("p (a c) -> p a c", a=ND),
            in_=g["L1_attn_Wk"].rearrange("(a p) c -> p a c", p=128)),
            inc=[(s_gdma, 16)])
        tv_ = b.op("gpsimd", lambda e: e.dma_start(
            out=WV1.rearrange("p (a c) -> p a c", a=ND),
            in_=g["L1_attn_Wv"].rearrange("(a p) c -> p a c", p=128)),
            inc=[(s_gdma, 16)])

        tko = b.op("gpsimd", lambda e: e.memset(KTO1[:], 1.0), inc=[(s_gp, 1)])
        kevs = []
        for tc in range(4):
            for dt in range(ND):
                tmm = b.op("tensor", lambda e, dt=dt, tc=tc: e.matmul(
                    psf(2, 512)[0:64, :],
                    WK1[:, dt * 64:(dt + 1) * 64],
                    XNT[:, dt * T + tc * 512: dt * T + (tc + 1) * 512],
                    start=(dt == 0), stop=(dt == ND - 1),
                    skip_group_check=True),
                    wait=(([tk_[0], tko[0]] + ([kevs[-1]] if kevs else []))
                          if dt == 0 else []),
                    inc=[(s_te, 1)])
            tev = b.op("scalar", lambda e, tc=tc: e.activation(
                KTO1[0:64, tc * 512:(tc + 1) * 512], psf(2, 512)[0:64, :],
                AF.Identity, bias=BKC[0:64, 0:1]),
                wait=[tmm[0]], inc=[(s_se, 1)])
            kevs.append(tev[0])
        tvo = b.op("gpsimd", lambda e: e.memset(V1[:], 1.0),
                   wait=[tko[0]], inc=[(s_gp, 1)])
        vevs = []
        for kt in range(NT):
            for dt in range(ND):
                tmm = b.op("tensor", lambda e, dt=dt, kt=kt: e.matmul(
                    psf(3, 64),
                    XNT[:, dt * T + kt * 128: dt * T + (kt + 1) * 128],
                    WV1[:, dt * 64:(dt + 1) * 64],
                    start=(dt == 0), stop=False, skip_group_check=True),
                    wait=(([tv_[0], tvo[0]] + ([vevs[-1]] if vevs else []))
                          if dt == 0 else []),
                    inc=[(s_te, 1)])
            tbb = b.op("tensor", lambda e: e.matmul(
                psf(3, 64), ONESB[:], BVROW[:], start=False, stop=True,
                skip_group_check=True), inc=[(s_te, 1)])
            tev = b.op("scalar", lambda e, kt=kt: e.copy(
                V1[:, kt * 66: kt * 66 + 64], psf(3, 64)),
                wait=[tbb[0]], inc=[(s_se, 1)])
            vevs.append(tev[0])

        woadd_prev = None
        wo_used = None
        for hf in range(2):
            hoff = hf * 1024
            tq = b.op("gpsimd", lambda e: e.dma_start(
                out=WQ1.rearrange("p (a c) -> p a c", a=ND),
                in_=g["L1_attn_Wq"].rearrange("(a p) c -> p a c", p=128)),
                wait=[wo_used] if wo_used else [], inc=[(s_gdma, 16)])
            qtm_free = [None, None]
            last_ot = None
            for j in range(8):
                pevs = []
                for tc in range(2):
                    bank = tc % 2
                    for dt in range(ND):
                        tmm = b.op("tensor",
                                   lambda e, j=j, dt=dt, tc=tc, bank=bank,
                                   hoff=hoff: e.matmul(
                            psf(bank, 512),
                            WQ1[:, dt * 1024 + j * 128: dt * 1024 + (j + 1) * 128],
                            XNT[:, dt * T + hoff + tc * 512:
                                dt * T + hoff + (tc + 1) * 512],
                            start=(dt == 0), stop=(dt == ND - 1),
                            skip_group_check=True),
                            wait=(([tq[0]] + [x for x in qtm_free if x] +
                                   (pevs[-2:] if pevs else []))
                                  if dt == 0 else []),
                            inc=[(s_te, 1)])
                    ev2 = []
                    for hh in range(2):
                        tev = b.op("scalar",
                                   lambda e, j=j, tc=tc, hh=hh, bank=bank:
                                   e.activation(
                            QTM1s[hh][0:64, tc * 512:(tc + 1) * 512],
                            psf(bank, 512)[hh * 64:hh * 64 + 64, :],
                            AF.Identity, scale=0.125,
                            bias=BQC[hh * 64:hh * 64 + 64, j:j + 1]),
                            wait=[tmm[0]], inc=[(s_se, 1)])
                        ev2.append(tev[0])
                    pevs.append(ev2[-1])
                b.wait("tensor", [pevs[-1]])
                qproj_done = pevs[-1]

                for hh in range(2):
                    h = 2 * j + hh
                    QTM = QTM1s[hh]
                    M128 = RT[:, 35:51]
                    p1prev = {}
                    for t8 in range(8):
                        bank = 2 + t8 % 2
                        tmm1 = b.op("tensor",
                                    lambda e, t8=t8, bank=bank, QTM=QTM:
                                    e.matmul(
                            psf(bank, 512),
                            QTM[0:64, t8 * 128:(t8 + 1) * 128],
                            KTO1[0:64, 0:512], start=True, stop=True),
                            wait=([qproj_done, kevs[-1], vevs[-1]] if t8 == 0
                                  else ([p1prev[bank]] if bank in p1prev
                                        else [])),
                            inc=[(s_te, 1)])
                        trm = b.op("vector",
                                   lambda e, t8=t8, bank=bank, M128=M128:
                                   e.reduce_max(
                            bcol(M128, t8), psf(bank, 512),
                            axis=AX.X, negate=True),
                            wait=[tmm1[0]], inc=[(s_ve, 1)])
                        p1prev[bank] = trm[0]
                    p1prev = trm[0]
                    ttm = b.op("tensor", lambda e, M128=M128: e.transpose(
                        psf(2, 128)[0:16, :], M128[:, 0:16], IDF[:]),
                        wait=[p1prev], inc=[(s_te, 1)])
                    tme = b.op("scalar", lambda e: e.copy(
                        MRW1[0:8, 0:128], psf(2, 128)[0:8, :]),
                        wait=[ttm[0]], inc=[(s_se, 1)])
                    b.wait("tensor", [tme[0]])
                    tmd = b.op("gpsimd", lambda e, QTM=QTM: e.dma_start(
                        out=QTM[64:65, 0:1024], in_=MRW1[0:8, 0:128]),
                        wait=[tme[0]], inc=[(s_gdma, 16)])

                    et_last2 = [tmd[0], tmd[0]]
                    avprev = [None, None]
                    for qc in range(4):
                        etevs = []
                        for kt in range(NT):
                            bank = (kt % 4) // 2
                            off = (kt % 2) * 256
                            tmm = b.op("tensor",
                                       lambda e, kt=kt, qc=qc, bank=bank,
                                       off=off, QTM=QTM: e.matmul(
                                psf(bank, 256, off=off),
                                KTO1[0:65, kt * 128:(kt + 1) * 128],
                                QTM[0:65, qc * 256:(qc + 1) * 256],
                                start=True, stop=True),
                                wait=([et_last2[bank]] if kt < 4 and kt % 2 == 0
                                      else ([etevs[kt // 2 - 2]]
                                            if kt % 2 == 0 and kt >= 4 else [])),
                                inc=[(s_te, 1)])
                            if kt % 2 == 1:
                                tex = b.op("scalar", lambda e, kt=kt, bank=bank:
                                           e.activation(
                                    ETH1[:, (kt - 1) * 256:(kt + 1) * 256],
                                    psf(bank, 512), AF.Exp),
                                    wait=[tmm[0]], inc=[(s_se, 1)])
                                etevs.append(tex[0])
                        et_last2 = [etevs[-2], etevs[-1]]
                        for ms in range(2):
                            bank = 4 + ms
                            for kt in range(NT):
                                tmm = b.op("tensor",
                                           lambda e, kt=kt, ms=ms, bank=bank:
                                           e.matmul(
                                    PSF[:, bank * 512: bank * 512 + 66],
                                    ETH1[:, kt * 256 + ms * 128:
                                         kt * 256 + ms * 128 + 128],
                                    V1[:, kt * 66:(kt + 1) * 66],
                                    start=(kt == 0), stop=(kt == NT - 1),
                                    skip_group_check=True),
                                    wait=(([etevs[-1]] +
                                           ([avprev[ms]] if avprev[ms] else []))
                                          if kt == 0 else []),
                                    inc=[(s_te, 1)])
                            sl = qc * 2 + ms
                            trz = b.op("vector", lambda e, bank=bank, sl=sl:
                                       e.reciprocal(
                                TCOL2[:, sl:sl + 1],
                                PSF[:, bank * 512 + 64: bank * 512 + 65]),
                                wait=[tmm[0]] + ([b.ohs_free] if sl == 0 and
                                                 b.ohs_free else []),
                                inc=[(s_ve, 1)])
                            tevo = b.op("vector", lambda e, bank=bank, sl=sl:
                                        e.tensor_scalar(
                                OHS1[:, sl * 64:(sl + 1) * 64],
                                PSF[:, bank * 512: bank * 512 + 64],
                                TCOL2[:, sl:sl + 1], None, ALU.mult),
                                wait=[trz[0]], inc=[(s_ve, 1)])
                            avprev[ms] = tevo[0]
                            if sl == 7:
                                for j2 in range(8):
                                    tto = b.op("tensor", lambda e, j2=j2:
                                               e.transpose(
                                        psb(1, 128, off=j2 * 128)[0:64, :],
                                        OHS1[:, j2 * 64:(j2 + 1) * 64],
                                        IDB[:]),
                                        wait=([tevo[0]] + ([b.psb1_free] if
                                              b.psb1_free else []))
                                        if j2 == 0 else [],
                                        inc=[(s_te, 1)])
                                tote = b.op("scalar", lambda e, h=h: e.copy(
                                    OT1H[(h % 2) * 64:(h % 2) * 64 + 64,
                                         (h // 2) * 1024:(h // 2 + 1) * 1024],
                                    psb(1, 1024)[0:64, :]),
                                    wait=[tto[0]], inc=[(s_se, 1)])
                                b.ohs_free = tto[0]
                                b.psb1_free = tote[0]
                                last_ot = tote[0]
                    qtm_free[hh] = et_last2[1]

            two = b.op("gpsimd", lambda e: e.dma_start(
                out=WQ1.rearrange("p (a c) -> p a c", a=ND),
                in_=g["L1_attn_Wo"].rearrange("(a p) c -> p a c", p=128)),
                wait=[qproj_done], inc=[(s_gdma, 16)])
            for t8 in range(8):
                t = hf * 8 + t8
                for dc in range(2):
                    bank = dc
                    for ht in range(8):
                        tmm = b.op("tensor",
                                   lambda e, ht=ht, dc=dc, t8=t8, bank=bank:
                                   e.matmul(
                            psf(bank, 512),
                            OT1H[:, ht * 1024 + t8 * 128:
                                 ht * 1024 + (t8 + 1) * 128],
                            WQ1[:, ht * 1024 + dc * 512:
                                ht * 1024 + (dc + 1) * 512],
                            start=(ht == 0), stop=False,
                            skip_group_check=True),
                            wait=(([last_ot, two[0]] +
                                   ([woadd_prev] if woadd_prev else []))
                                  if ht == 0 else []),
                            inc=[(s_te, 1)])
                    tbb = b.op("tensor", lambda e, dc=dc, bank=bank: e.matmul(
                        psf(bank, 512), ONESB[:],
                        ROWSB[0:1, D + dc * 512: D + (dc + 1) * 512],
                        start=False, stop=True, skip_group_check=True),
                        inc=[(s_te, 1)])
                    tadd = b.op("vector", lambda e, t=t, dc=dc, bank=bank:
                                e.tensor_add(
                        X[:, t * D + dc * 512: t * D + (dc + 1) * 512],
                        X[:, t * D + dc * 512: t * D + (dc + 1) * 512],
                        psf(bank, 512)),
                        wait=[tbb[0]], inc=[(s_ve, 1)])
                    woadd_prev = tadd[0]
            wo_used = tmm[0]
        b.barrier()

    OUTACC = arb(64 * KB, 8 * 1024)
    SILU = arf(80 * KB, 1024)
    XSTG2 = [arf(84 * KB, D), arf(88 * KB, D)]

    def w1buf(p, ag):
        return arb((p * 16 + ag * 8) * KB, ND * 512)

    def w2buf(p):
        return arb(p * 16 * KB, 32 * 256)

    def t3t(kt):
        return arb(96 * KB + kt * 2 * KB, 1024)

    def stage_moe(bi):
        acc_free = None
        wreg_free = None
        t3_free = None
        out_done = []
        for c in range(2):
            tms = b.op("gpsimd", lambda e: e.memset(OUTACC[:], 0.0),
                       wait=[acc_free] if acc_free else [], inc=[(s_gp, 1)])
            acc_prev = tms[0]
            for e_ in range(NEXP):
                last_t3 = None
                pair_ev = {}
                dma_toks = {}
                for grp in range(8):
                    p = grp % 2
                    wa = b.op("gpsimd", lambda e, e_=e_, grp=grp, p=p:
                              e.dma_start(
                        out=w1buf(p, 0).rearrange("q (a c) -> q a c", a=ND),
                        in_=g["L1_ffn_w1"][e_].rearrange(
                            "(a q) f -> q a f", q=128)[:, :,
                            grp * 512:(grp + 1) * 512]),
                        wait=([wreg_free] if wreg_free and grp < 2 else []) +
                             ([dma_toks[grp - 2][2]] if grp >= 2 else []),
                        inc=[(s_gdma, 16)])
                    wg = b.op("gpsimd", lambda e, e_=e_, grp=grp, p=p:
                              e.dma_start(
                        out=w1buf(p, 1).rearrange("q (a c) -> q a c", a=ND),
                        in_=g["L1_ffn_w1"][e_].rearrange(
                            "(a q) f -> q a f", q=128)[:, :,
                            FFN + grp * 512: FFN + (grp + 1) * 512]),
                        inc=[(s_gdma, 16)])
                    grp_last = None
                    for i in range(4):
                        pp = i % 2
                        ab = 4 * pp
                        gb = 4 * pp + 2
                        ja = grp * 4 + i
                        for nch in range(2):
                            for dt in range(ND):
                                tma = b.op("tensor",
                                           lambda e, i=i, dt=dt, nch=nch,
                                           c=c, p=p, ab=ab: e.matmul(
                                    psf(ab + nch, 512),
                                    w1buf(p, 0)[:, dt * 512 + i * 128:
                                                dt * 512 + (i + 1) * 128],
                                    XNT[:, dt * T + c * 1024 + nch * 512:
                                        dt * T + c * 1024 + (nch + 1) * 512],
                                    start=(dt == 0), stop=(dt == ND - 1),
                                    skip_group_check=True),
                                    wait=(([wa[0]] +
                                           ([pair_ev[pp][0]] if pp in pair_ev
                                            else []) +
                                           ([t3_free] if t3_free and grp == 0
                                            and i == 0 else []))
                                          if dt == 0 and nch == 0 else []),
                                    inc=[(s_te, 1)])
                            for dt in range(ND):
                                tmg = b.op("tensor",
                                           lambda e, i=i, dt=dt, nch=nch,
                                           c=c, p=p, gb=gb: e.matmul(
                                    psf(gb + nch, 512),
                                    w1buf(p, 1)[:, dt * 512 + i * 128:
                                                dt * 512 + (i + 1) * 128],
                                    XNT[:, dt * T + c * 1024 + nch * 512:
                                        dt * T + c * 1024 + (nch + 1) * 512],
                                    start=(dt == 0), stop=(dt == ND - 1),
                                    skip_group_check=True),
                                    wait=(([wg[0]] +
                                           ([pair_ev[pp][1]] if pp in pair_ev
                                            else []))
                                          if dt == 0 and nch == 0 else []),
                                    inc=[(s_te, 1)])
                        tsi = b.op("scalar", lambda e, ab=ab: e.activation(
                            SILU[:], PSF[:, ab * 512:(ab + 2) * 512],
                            AF.Silu), wait=[tma[0]], inc=[(s_se, 1)])
                        tt3 = b.op("vector", lambda e, ja=ja, gb=gb: e.tensor_mul(
                            t3t(ja), SILU[:], PSF[:, gb * 512:(gb + 2) * 512]),
                            wait=[tsi[0], tmg[0]], inc=[(s_ve, 1)])
                        pair_ev[pp] = (tt3[0], tt3[0])
                        grp_last = tt3[0]
                        last_t3 = tt3[0]
                    dma_toks[grp] = (wa[0], wg[0], grp_last)
                w2toks = {}
                y2prev = {}
                for nq in range(4):
                    p = nq % 2
                    wts = []
                    for piece in range(4):
                        tw2 = b.op("gpsimd",
                                   lambda e, e_=e_, nq=nq, p=p, piece=piece:
                                   e.dma_start(
                            out=w2buf(p).rearrange(
                                "q (a c) -> q a c", a=32)[:, piece * 8:
                                                          (piece + 1) * 8],
                            in_=g["L1_ffn_w2"][e_].rearrange(
                                "(a q) f -> q a f", q=128)[:, piece * 8:
                                                           (piece + 1) * 8,
                                nq * 256:(nq + 1) * 256]),
                            wait=([last_t3] if nq < 2 and piece == 0 else []) +
                                 ([w2toks[nq - 2]] if nq >= 2 and piece == 0
                                  else []),
                            inc=[(s_gdma, 16)])
                        wts.append(tw2[0])
                    for mt in range(8):
                        bank = mt % 2
                        for kt in range(32):
                            tmy = b.op("tensor",
                                       lambda e, kt=kt, mt=mt, nq=nq, p=p,
                                       bank=bank: e.matmul(
                                psf(bank, 256),
                                t3t(kt)[:, mt * 128:(mt + 1) * 128],
                                w2buf(p)[:, kt * 256:(kt + 1) * 256],
                                start=(kt == 0), stop=(kt == 31),
                                skip_group_check=True),
                                wait=((wts + [last_t3] +
                                       ([y2prev[bank]] if bank in y2prev
                                        else []))
                                      if kt == 0 else []),
                                inc=[(s_te, 1)])
                        tya = b.op("vector",
                                   lambda e, mt=mt, nq=nq, c=c, e_=e_,
                                   bank=bank: e.scalar_tensor_tensor(
                            OUTACC[:, mt * 1024 + nq * 256:
                                   mt * 1024 + (nq + 1) * 256],
                            psf(bank, 256),
                            SELC[:, (8 * c + mt) * NEXP + e_:
                                 (8 * c + mt) * NEXP + e_ + 1],
                            OUTACC[:, mt * 1024 + nq * 256:
                                   mt * 1024 + (nq + 1) * 256],
                            ALU.mult, ALU.add),
                            wait=[tmy[0], acc_prev], inc=[(s_ve, 1)])
                        y2prev[bank] = tya[0]
                    w2toks[nq] = tmy[0]
                acc_prev = tya[0]
                wreg_free = tya[0]
                t3_free = tya[0]
            slot_busy = [None, None]
            for mt in range(8):
                tg = 8 * c + mt
                sl = mt % 2
                tl = b.op("sync", lambda e, tg=tg, sl=sl: e.dma_start(
                    out=XSTG2[sl][:], in_=hbm_tiles(scratch_x1[bi])[tg]),
                    wait=[slot_busy[sl]] if slot_busy[sl] else [],
                    inc=[(s_dma, 16)])
                ta_ = b.op("vector", lambda e, mt=mt, sl=sl: e.tensor_add(
                    XSTG2[sl][:], XSTG2[sl][:],
                    OUTACC[:, mt * 1024:(mt + 1) * 1024]),
                    wait=[tl[0], acc_prev], inc=[(s_ve, 1)])
                tw = b.op("sync", lambda e, tg=tg, sl=sl, bi=bi: e.dma_start(
                    out=hbm_tiles(out[bi])[tg], in_=XSTG2[sl][:]),
                    wait=[ta_[0]], inc=[(s_out, 16)])
                slot_busy[sl] = tw[0]
                out_done.append(tw[0])
            acc_free = ta_[0]
        b.barrier()

    setup_constants()
    for bi in range(NB):
        stage_ln(bi, 0)
        if stop_after == "ln0":
            for t in range(NT):
                b.op("sync", lambda e, t=t, bi=bi: e.dma_start(
                    out=hbm_tiles(out[bi])[t], in_=bcol(X, t, D)),
                    inc=[(s_out, 16)])
            continue
        offload_X(bi)
        stage_ssm(bi)
        stage_yadd(bi)
        rsteps = routing_steps(bi)
        if stop_after == "ssm":
            for r in rsteps:
                r()
            b.barrier()
            b.op("sync", lambda e, bi=bi: e.dma_start(
                out=out[bi], in_=scratch_x1[bi]), inc=[(s_out, 16)])
            continue
        stage_light(bi, rsteps)
        stage_heavy(bi)
        if stop_after == "block0":
            for t in range(NT):
                b.op("sync", lambda e, t=t, bi=bi: e.dma_start(
                    out=hbm_tiles(out[bi])[t], in_=bcol(X, t, D)),
                    inc=[(s_out, 16)])
            continue
        stage_ln(bi, 1)
        stage_l1attn(bi)
        if stop_after == "l1attn":
            for t in range(NT):
                b.op("sync", lambda e, t=t, bi=bi: e.dma_start(
                    out=hbm_tiles(out[bi])[t], in_=bcol(X, t, D)),
                    inc=[(s_out, 16)])
            continue
        offload_X(bi)
        stage_moe(bi)

    b.wait("sync", [(s_out, s_out.n)])
    b.barrier()
    b.finalize()
    es.close()
    return nc, tap_handles


def kernel(**inputs):
    nc, _ = build_program(stop_after=None)
    xs = np.ascontiguousarray(inputs["L0_attn_x"], np.float32)
    common = {k: np.ascontiguousarray(np.asarray(v), np.float32)
              for k, v in inputs.items() if k != "L0_attn_x"}
    in_maps = []
    for c in range(8):
        m = dict(common)
        m["x"] = xs[2 * c:2 * c + 2]
        in_maps.append(m)
    res = run_bass_kernel_spmd(nc, in_maps, core_ids=list(range(8)))
    return np.concatenate([r["out"] for r in res.results], axis=0)


# version 04
# speedup vs baseline: 1.2436x; 1.0208x over previous
import numpy as np
import ml_dtypes
from contextlib import ExitStack

import concourse.bass as bass
import concourse.mybir as mybir
from concourse.bass_utils import run_bass_kernel_spmd

F32 = mybir.dt.float32
BF16 = mybir.dt.bfloat16
AF = mybir.ActivationFunctionType
ALU = mybir.AluOpType
AX = mybir.AxisListType

T = 2048
NT = 16
D = 1024
ND = 8
FFN = 4096
NEXP = 8
NB = 2
NITER_CD = 50
NITER_TH = 48
NITER_IX = 14
LOGK = float(np.log(np.float32(1152.0)))


class Sem:
    def __init__(self, h):
        self.h = h
        self.n = 0


class Builder:
    ENGINES = ("tensor", "vector", "scalar", "gpsimd", "sync")

    def __init__(self, nc, es):
        self.nc = nc
        self.es = es
        self.q = {e: [] for e in self.ENGINES}
        self._sems = {}
        self.tapnames = set()

    def sem(self, name):
        if name not in self._sems:
            self._sems[name] = Sem(self.es.enter_context(self.nc.semaphore(name)))
        return self._sems[name]

    def sb(self, name, shape, dt=F32, **kw):
        return self.es.enter_context(self.nc.sbuf_tensor(name, shape, dt, **kw))

    def ps(self, name, shape, dt=F32):
        return self.es.enter_context(self.nc.psum_tensor(name, shape, dt))

    def dma_pool(self, tag):
        if not hasattr(self, "_dmapools"):
            self._dmapools = {}
            self._dmarr = {}
        if tag not in self._dmapools:
            self._dmapools[tag] = [self.sem("dmas_%s%d" % (tag, i))
                                   for i in range(8)]
            self._dmarr[tag] = -1
        return self._dmapools[tag]

    def op(self, eng, fn, wait=(), inc=()):
        toks = []
        incs = []
        extra_wait = []
        for s, k in inc:
            if getattr(s, "isdma", False):
                tag = s.tag
                pool = self.dma_pool(tag)
                self._dmarr[tag] = (self._dmarr[tag] + 1) % len(pool)
                slot = pool[self._dmarr[tag]]
                extra_wait.append((slot, slot.n))
                slot.n += 16
                incs.append((slot.h, 16))
                toks.append((slot, slot.n))
            else:
                s.n += k
                incs.append((s.h, k))
                toks.append((s, s.n))
        waits = tuple((s.h, v) for (s, v) in tuple(wait) + tuple(extra_wait)
                      if s is not None)

        def run(engine, fn=fn, waits=waits, incs=incs):
            for h, v in waits:
                engine.wait_ge(h, v)
            r = fn(engine)
            for h, k in incs:
                r = r.then_inc(h, k)

        self.q[eng].append(run)
        return toks

    def wait(self, eng, tokens):
        waits = tuple((s.h, v) for (s, v) in tokens if s is not None)

        def run(engine, waits=waits):
            for h, v in waits:
                engine.wait_ge(h, v)

        self.q[eng].append(run)

    def barrier(self, engines=None):
        engines = engines or self.ENGINES
        s = self.sem("bar")
        for e in engines:
            self.op(e, lambda eng: eng.nop(), inc=[(s, 1)])
        target = s.n
        for e in engines:
            self.wait(e, [(s, target)])
        return (s, target)

    def finalize(self):
        with self.nc.Block() as blk:
            for eng in self.ENGINES:
                def body(engine, eng=eng):
                    for f in self.q[eng]:
                        f(engine)
                getattr(blk, eng)(body)


def bcol(ap, j, w=1):
    return ap[:, j * w:(j + 1) * w]


def build_program(taps=(), stop_after=None):
    nc = bass.Bass("TRN2", target_bir_lowering=False, detect_race_conditions=False)
    es = ExitStack()
    b = Builder(nc, es)
    b.tapnames = set(taps)

    g = {}

    def din(name, shape):
        g[name] = nc.dram_tensor(name, list(shape), F32, kind="ExternalInput").ap()

    din("x", (NB, T, D))
    din("L0_attn_norm_w", (D,))
    din("L0_attn_A_w", (D, D)); din("L0_attn_A_b", (D,))
    din("L0_attn_B_w", (D, D)); din("L0_attn_C_w", (D, D))
    din("L0_attn_D_w", (D, D))
    din("L0_attn_out_w", (D, D)); din("L0_attn_out_b", (D,))
    din("L0_ffn_gamma_light", (D,)); din("L0_ffn_wq_light", (D, 512))
    din("L0_ffn_wkv_light", (D, 1024)); din("L0_ffn_wo_light", (512, D))
    din("L0_ffn_q_routing_token", (D,)); din("L0_ffn_kv_routing_token", (D,))
    din("L0_ffn_gamma_heavy", (D,)); din("L0_ffn_wq_heavy", (D, 512))
    din("L0_ffn_wkv_heavy", (1024, D)); din("L0_ffn_wo_heavy", (512, D))
    din("L1_attn_Wq", (D, 1024)); din("L1_attn_bq", (1024,))
    din("L1_attn_Wk", (D, 64)); din("L1_attn_bk", (64,))
    din("L1_attn_Wv", (D, 64)); din("L1_attn_bv", (64,))
    din("L1_attn_Wo", (1024, D)); din("L1_attn_bo", (D,))
    din("L1_ffn_router_w", (D, NEXP)); din("L1_ffn_router_b", (NEXP,))
    din("L1_ffn_w1", (NEXP, D, 2 * FFN)); din("L1_ffn_w2", (NEXP, FFN, D))

    out = nc.dram_tensor("out", [NB, T, D], F32, kind="ExternalOutput").ap()
    scratch_x1 = nc.dram_tensor("scratch_x1", [NB, T, D], F32, kind="Internal").ap()

    tap_handles = {}

    def tap(name, shape, dt=F32):
        if name in b.tapnames and name not in tap_handles:
            tap_handles[name] = nc.dram_tensor(
                "tap_" + name, list(shape), dt, kind="ExternalOutput").ap()
        return tap_handles.get(name)

    def hbm_tiles(ap2d):
        return ap2d.rearrange("(t p) d -> t p d", p=128)

    S128 = b.sb("S128", [128, NT])
    MASKC = b.sb("MASKC", [128, NT])
    BFC = b.sb("BFC", [128, NT])
    RNC = b.sb("RNC", [128, NT])
    IOTA = b.sb("IOTA", [128, NT])
    SELC = b.sb("SELC", [128, NT * NEXP])
    NWC = b.sb("NWC", [128, ND])
    GLC = b.sb("GLC", [128, ND])
    GHC = b.sb("GHC", [128, ND])
    ABC = b.sb("ABC", [128, ND])
    BQC = b.sb("BQC", [128, ND])
    RB = b.sb("RB", [128, D])
    ONES = b.sb("ONES", [1, 128])
    ONESB = b.sb("ONESB", [1, 128], BF16)
    ROWSB = b.sb("ROWSB", [1, 2 * D], BF16)
    BVROW = b.sb("BVROW", [1, 64], BF16)
    RWF = b.sb("RWF", [128, ND * NEXP])
    RBROW = b.sb("RBROW", [1, NEXP])
    BKC = b.sb("BKC", [64, 1])
    IDB = b.sb("IDB", [128, 128], BF16)
    IDF = b.sb("IDF", [128, 128])
    XC = b.sb("XC", [128, D])
    XN = b.sb("XN", [128, D])
    XC2 = b.sb("XC2", [128, D])
    XN2 = b.sb("XN2", [128, D])
    XCs = [XC, XC2]
    XNs = [XN, XN2]
    TBF1 = b.sb("TBF1", [128, D], BF16)
    TBF2 = b.sb("TBF2", [128, D], BF16)
    TBF1b = b.sb("TBF1b", [128, D], BF16)
    TBF2b = b.sb("TBF2b", [128, D], BF16)
    TBF1s = [TBF1, TBF1b]
    TBF2s = [TBF2, TBF2b]
    RTC = b.sb("RTC", [128, 32])
    RTC2 = b.sb("RTC2", [128, 16])
    RTC3 = b.sb("RTC3", [128, 16])
    TCOL = b.sb("TCOL", [128, 16])
    TCOL2 = b.sb("TCOL2", [128, 16])
    RT = b.sb("RT", [128, 64])
    RSC = b.sb("RSC", [1, 32])
    ROW128 = b.sb("ROW128", [1, 128])

    AR_BYTES = 160 * 1024
    ARENA = b.sb("ARENA", [128, AR_BYTES // 4], F32)
    ARB = ARENA.bitcast(BF16)

    def arf(off, cols):
        assert off % 4 == 0 and off + cols * 4 <= AR_BYTES, (off, cols)
        return ARENA[:, off // 4: off // 4 + cols]

    def arb(off, cols):
        assert off % 2 == 0 and off + cols * 2 <= AR_BYTES, (off, cols)
        return ARB[:, off // 2: off // 2 + cols]

    KB = 1024
    A_X = 96 * KB
    X = arf(A_X, NT * D)
    BASET = arb(64 * KB, ND * T)
    XNT = arb(32 * KB, ND * T)

    PSF = b.ps("PSF", [128, 4096], F32)
    PSFB = PSF.bitcast(BF16)

    def psf(bank, w=512, off=0):
        return PSF[:, bank * 512 + off: bank * 512 + off + w]

    def psb(bank, w=1024, off=0):
        return PSFB[:, (6 + bank) * 1024 + off: (6 + bank) * 1024 + off + w]

    s_dma = b.sem("s_dma")
    s_dma.isdma = True
    s_dma.tag = "sy"
    s_gdma = b.sem("s_gdma")
    s_gdma.isdma = True
    s_gdma.tag = "gp"
    s_te = b.sem("s_te")
    s_ve = b.sem("s_ve")
    s_se = b.sem("s_se")
    s_gp = b.sem("s_gp")
    s_out = b.sem("s_out")

    def setup_constants():
        p0 = b.op("gpsimd", lambda e: e.memset(IDB[:], 0.0), inc=[(s_gp, 1)])
        p0 = b.op("gpsimd", lambda e: e.affine_select(
            out=IDB[:], in_=IDB[:], compare_op=ALU.not_equal, fill=1.0, base=0,
            pattern=[[-1, 128]], channel_multiplier=1),
            wait=[p0[0]], inc=[(s_gp, 1)])
        p0 = b.op("gpsimd", lambda e: e.memset(IDF[:], 0.0),
                  wait=[p0[0]], inc=[(s_gp, 1)])
        p0 = b.op("gpsimd", lambda e: e.affine_select(
            out=IDF[:], in_=IDF[:], compare_op=ALU.not_equal, fill=1.0, base=0,
            pattern=[[-1, 128]], channel_multiplier=1),
            wait=[p0[0]], inc=[(s_gp, 1)])
        p0 = b.op("gpsimd", lambda e: e.iota(
            IOTA[:], pattern=[[128, NT]], base=0, channel_multiplier=1,
            allow_small_or_imprecise_dtypes=True), wait=[p0[0]], inc=[(s_gp, 1)])
        p0 = b.op("gpsimd", lambda e: e.memset(ONES[:], 1.0),
                  wait=[p0[0]], inc=[(s_gp, 1)])
        pgp = b.op("gpsimd", lambda e: e.memset(ONESB[:], 1.0),
                   wait=[p0[0]], inc=[(s_gp, 1)])
        b.wait("tensor", [pgp[0]])
        b.wait("vector", [pgp[0]])
        b.wait("scalar", [pgp[0]])

        t1 = b.op("sync", lambda e: e.dma_start(
            out=XC[0:1, 0:D], in_=g["L0_attn_out_b"][None, :]), inc=[(s_dma, 16)])
        tv1 = b.op("vector", lambda e: e.tensor_copy(ROWSB[:, 0:D], XC[0:1, 0:D]),
                   wait=[t1[0]], inc=[(s_ve, 1)])
        t2 = b.op("sync", lambda e: e.dma_start(
            out=XC[0:1, 0:D], in_=g["L1_attn_bo"][None, :]),
            wait=[tv1[0]], inc=[(s_dma, 16)])
        tv2 = b.op("vector", lambda e: e.tensor_copy(ROWSB[:, D:2 * D], XC[0:1, 0:D]),
                   wait=[t2[0]], inc=[(s_ve, 1)])
        b.op("sync", lambda e: e.dma_start(
            out=RBROW[:], in_=g["L1_ffn_router_b"][None, :]), inc=[(s_dma, 16)])
        b.op("gpsimd", lambda e: e.dma_start(
            out=BVROW[:], in_=g["L1_attn_bv"][None, :]), inc=[(s_gdma, 16)])

        def chan_cols(name, dest, scale=1.0, prev=None):
            tl = b.op("sync", lambda e, name=name: e.dma_start(
                out=XN[0:8, 0:128],
                in_=g[name].rearrange("(a p) -> a p", p=128)),
                wait=[prev] if prev else [tv2[0]], inc=[(s_dma, 16)])
            tt = b.op("tensor", lambda e: e.transpose(
                psf(0, 8), XN[0:8, 0:128], IDF[0:8, 0:8]),
                wait=[tl[0]], inc=[(s_te, 1)])
            te_ = b.op("scalar", lambda e, dest=dest, scale=scale: e.activation(
                dest[:, 0:8], psf(0, 8), AF.Copy, scale=scale),
                wait=[tt[0]], inc=[(s_se, 1)])
            b.wait("tensor", [te_[0]])
            return te_[0]

        p = None
        p = chan_cols("L0_attn_norm_w", NWC, prev=p)
        p = chan_cols("L0_ffn_gamma_light", GLC, prev=p)
        p = chan_cols("L0_ffn_gamma_heavy", GHC, prev=p)
        p = chan_cols("L0_attn_A_b", ABC, prev=p)
        p = chan_cols("L1_attn_bq", BQC, scale=0.125, prev=p)

        tk = b.op("sync", lambda e: e.dma_start(
            out=XN[0:1, 0:64], in_=g["L1_attn_bk"][None, :]),
            wait=[p], inc=[(s_dma, 16)])
        tkt = b.op("tensor", lambda e: e.transpose(
            psf(0, 1)[0:64, :], XN[0:1, 0:64], IDF[0:1, 0:1]),
            wait=[tk[0]], inc=[(s_te, 1)])
        tke = b.op("scalar", lambda e: e.copy(BKC[:], psf(0, 1)[0:64, 0:1]),
                   wait=[tkt[0]], inc=[(s_se, 1)])
        b.wait("tensor", [tke[0]])

        tr = b.op("sync", lambda e: e.dma_start(
            out=XN[0:1, 0:D], in_=g["L0_ffn_q_routing_token"][None, :]),
            wait=[tke[0]], inc=[(s_dma, 16)])
        prev_ev = None
        for dt in range(ND):
            bank = dt % 2
            tm = b.op("tensor", lambda e, dt=dt, bank=bank: e.matmul(
                psf(bank, 128), ONES[:], XN[0:1, dt * 128:(dt + 1) * 128],
                start=True, stop=True),
                wait=[tr[0]] + ([prev_ev] if prev_ev else []), inc=[(s_te, 1)])
            tev = b.op("scalar", lambda e, dt=dt, bank=bank: e.copy(
                RB[:, dt * 128:(dt + 1) * 128], psf(bank, 128)),
                wait=[tm[0]], inc=[(s_se, 1)])
            prev_ev = tev[0]
        b.wait("tensor", [prev_ev])

        b.op("sync", lambda e: e.dma_start(
            out=RWF.rearrange("p (a c) -> p a c", a=ND),
            in_=g["L1_ffn_router_w"].rearrange("(a p) c -> p a c", p=128)),
            wait=[prev_ev], inc=[(s_dma, 16)])
        b.barrier()

    def part_reduce(src_col, dst_col, alu, extra_wait=()):
        tt = b.op("tensor", lambda e, src_col=src_col: e.transpose(
            psf(5, 128)[0:1, :], src_col, IDF[:]),
            wait=list(extra_wait), inc=[(s_te, 1)])
        te_ = b.op("scalar", lambda e: e.copy(ROW128[:], psf(5, 128)[0:1, :]),
                   wait=[tt[0]], inc=[(s_se, 1)])
        b.wait("tensor", [te_[0]])
        tr_ = b.op("vector", lambda e, dst_col=dst_col, alu=alu: e.tensor_reduce(
            RSC[0:1, dst_col:dst_col + 1], ROW128[:], axis=AX.X, op=alu),
            wait=[te_[0]], inc=[(s_ve, 1)])
        return tr_[0]

    def bcast_cols(src_row_off, n, dst, extra_wait=()):
        tm = b.op("tensor", lambda e, src_row_off=src_row_off, n=n: e.matmul(
            psf(5, n), ONES[:], RSC[0:1, src_row_off:src_row_off + n],
            start=True, stop=True), wait=list(extra_wait), inc=[(s_te, 1)])
        te_ = b.op("scalar", lambda e, dst=dst, n=n: e.copy(dst, psf(5, n)),
                   wait=[tm[0]], inc=[(s_se, 1)])
        b.wait("tensor", [te_[0]])
        return te_[0]

    def stage_ln(bi, which):
        first = (which == 0)
        b.ln_bankfree = {0: None, 1: None, 2: None, 3: None}
        tap_xn = tap("xn%d_b%d" % (which, bi), (T, D))
        rstate = {"lgfree": None, "trfree": None}
        for t in range(NT):
            par = t % 2
            XCP = XCs[par]
            XNP = XNs[par]
            TB1P = TBF1s[par]
            TB2P = TBF2s[par]
            C = lambda j, par=par: TCOL[:, par * 8 + j: par * 8 + j + 1]
            C2 = lambda j, par=par: TCOL2[:, par * 4 + j: par * 4 + j + 1]
            if first:
                tl = b.op("sync", lambda e, t=t: e.dma_start(
                    out=bcol(X, t, D), in_=hbm_tiles(g["x"][bi])[t]),
                    inc=[(s_dma, 16)])
                b.wait("vector", [tl[0]])
            xt = bcol(X, t, D)
            tm = b.op("vector", lambda e, xt=xt, C=C: e.reduce_sum(
                C(0), xt, axis=AX.X), inc=[(s_ve, 1)])
            tnm = b.op("vector", lambda e, C=C: e.tensor_scalar(
                C(1), C(0), -1.0 / D, None, ALU.mult),
                wait=[tm[0]], inc=[(s_ve, 1)])
            txc = b.op("scalar", lambda e, xt=xt, XCP=XCP, C=C: e.activation(
                XCP[:], xt, AF.Identity, bias=C(1)),
                wait=[tnm[0]], inc=[(s_se, 1)])
            tsq = b.op("scalar", lambda e, XCP=XCP, XNP=XNP, C=C: e.activation(
                XNP[:], XCP[:], AF.Square, accum_out=C(2)),
                wait=[txc[0]], inc=[(s_se, 1)])
            tv = b.op("vector", lambda e, C=C: e.tensor_scalar(
                C(3), C(2), 1.0 / D, 1e-5, ALU.mult, ALU.add),
                wait=[tsq[0]], inc=[(s_ve, 1)])
            tln = b.op("scalar", lambda e, C=C: e.activation(
                C(4), C(3), AF.Ln), wait=[tv[0]], inc=[(s_se, 1)])
            trs = b.op("scalar", lambda e, C=C: e.activation(
                C(5), C(4), AF.Exp, scale=-0.5), inc=[(s_se, 1)])
            txn = b.op("scalar", lambda e, XCP=XCP, XNP=XNP, C=C: e.activation(
                XNP[:], XCP[:], AF.Identity, scale=C(5)), inc=[(s_se, 1)])
            if tap_xn is not None:
                b.op("sync", lambda e, t=t, XNP=XNP, tap_xn=tap_xn: e.dma_start(
                    out=hbm_tiles(tap_xn)[t], in_=XNP[:]),
                    wait=[txn[0]], inc=[(s_out, 16)])

            if first:
                tsd0 = b.op("vector", lambda e, XCP=XCP, XNP=XNP: e.tensor_mul(
                    XCP[:], XNP[:], RB[:]), wait=[txn[0]], inc=[(s_ve, 1)])
                tsdot = b.op("vector", lambda e, t=t, XCP=XCP: e.reduce_sum(
                    bcol(S128, t), XCP[:], axis=AX.X), inc=[(s_ve, 1)])
                tl2 = b.op("scalar", lambda e, XCP=XCP, XNP=XNP, C=C:
                           e.activation(
                    XCP[:], XNP[:], AF.Square, accum_out=C(6)),
                    wait=[tsdot[0]], inc=[(s_se, 1)])
                tv2 = b.op("vector", lambda e, C=C: e.tensor_scalar(
                    C(7), C(6), 1.0 / D, 1e-5, ALU.mult, ALU.add),
                    wait=[tl2[0]], inc=[(s_ve, 1)])
                tln2 = b.op("scalar", lambda e, C=C, C2=C2: e.activation(
                    C2(0), C(7), AF.Ln), wait=[tv2[0]], inc=[(s_se, 1)])
                trn = b.op("scalar", lambda e, t=t, C2=C2: e.activation(
                    bcol(RNC, t), C2(0), AF.Exp, scale=-0.5), inc=[(s_se, 1)])
                tmx = b.op("vector", lambda e, C=C, C2=C2: e.tensor_scalar(
                    C2(3), C(6), 1e-24, None, ALU.max),
                    wait=[tl2[0]], inc=[(s_ve, 1)])
                tsl = b.op("scalar", lambda e, C2=C2: e.activation(
                    C2(1), C2(3), AF.Ln, scale=1.0 / 1024.0),
                    wait=[tmx[0]], inc=[(s_se, 1)])
                tbf = b.op("scalar", lambda e, t=t, C2=C2: e.activation(
                    bcol(BFC, t), C2(1), AF.Exp, scale=-0.5), inc=[(s_se, 1)])
                tbase = b.op("scalar", lambda e, t=t, TB1P=TB1P, XNP=XNP:
                             e.activation(
                    TB1P[:], XNP[:], AF.Identity, scale=bcol(BFC, t)),
                    inc=[(s_se, 1)])
                txnr = b.op("scalar", lambda e, t=t, TB2P=TB2P, XNP=XNP:
                            e.activation(
                    TB2P[:], XNP[:], AF.Identity, scale=bcol(RNC, t)),
                    wait=[trn[0]], inc=[(s_se, 1)])
                b.op("vector", lambda e, xt=xt, XNP=XNP: e.tensor_add(
                    xt, xt, XNP[:]), wait=[txn[0]], inc=[(s_ve, 1)])
                srcdst = [(TB1P, BASET, tbase, par), (TB2P, XNT, txnr, 2 + par)]
            else:
                tcast = b.op("scalar", lambda e, TB2P=TB2P, XNP=XNP: e.copy(
                    TB2P[:], XNP[:]), inc=[(s_se, 1)])
                srcdst = [(TB2P, XNT, tcast, par)]

            for srcb, dst, tok, slot in srcdst:
                bank = slot % 2
                scaled = first and (dst is XNT)
                w0 = [tok[0]] + ([b.ln_bankfree[slot]] if b.ln_bankfree[slot]
                                 else [])
                other = slot ^ 2
                if b.ln_bankfree.get(other):
                    w0 = w0 + [b.ln_bankfree[other]]
                for dt in range(ND):
                    tt = b.op("tensor", lambda e, srcb=srcb, dt=dt, bank=bank:
                              e.transpose(psb(bank, 128, off=dt * 128),
                                          srcb[:, dt * 128:(dt + 1) * 128],
                                          IDB[:]),
                              wait=(w0 if dt == 0 else []),
                              inc=[(s_te, 1)])
                if scaled:
                    for dt in range(ND):
                        tev = b.op("scalar",
                                   lambda e, dst=dst, dt=dt, t=t, bank=bank:
                                   e.activation(
                            dst[:, dt * T + t * 128: dt * T + (t + 1) * 128],
                            psb(bank, 128, off=dt * 128), AF.Identity,
                            scale=bcol(NWC, dt)),
                            wait=[tt[0]] if dt == 0 else [], inc=[(s_se, 1)])
                else:
                    tev = b.op("scalar", lambda e, dst=dst, t=t, bank=bank:
                               e.copy(
                        dst.rearrange("p (a q) -> p a q", a=ND)[
                            :, :, t * 128:(t + 1) * 128],
                        psb(bank, 1024).rearrange("p (a q) -> p a q", a=ND)),
                        wait=[tt[0]], inc=[(s_se, 1)])
                b.ln_bankfree[slot] = tev[0]

            if not first:
                w0 = [txn[0]] + ([rstate["trfree"]] if rstate["trfree"] else [])
                for dt in range(ND):
                    bank = 2 + dt // 4
                    tt3 = b.op("tensor", lambda e, dt=dt, bank=bank, XNP=XNP:
                               e.transpose(
                        psf(bank, 128, off=(dt % 4) * 128),
                        XNP[:, dt * 128:(dt + 1) * 128], IDF[:]),
                        wait=(w0 if dt == 0 else []), inc=[(s_te, 1)])
                tev3 = b.op("scalar", lambda e, XCP=XCP: e.copy(
                    XCP[:], PSF[:, 2 * 512: 4 * 512]),
                    wait=[tt3[0]], inc=[(s_se, 1)])
                rstate["trfree"] = tev3[0]
                lg = psf(4 + par, NEXP)
                for dt in range(ND):
                    b.op("tensor", lambda e, dt=dt, lg=lg, XCP=XCP: e.matmul(
                        lg, XCP[:, dt * 128:(dt + 1) * 128],
                        RWF.rearrange("p (a c) -> p a c", a=ND)[:, dt],
                        start=(dt == 0), stop=False, skip_group_check=True),
                        wait=(([tev3[0]] + ([rstate["lgfree"]] if
                                            rstate["lgfree"] else []))
                              if dt == 0 else []),
                        inc=[(s_te, 1)])
                tbias = b.op("tensor", lambda e, lg=lg: e.matmul(
                    lg, ONES[:], RBROW[:], start=False, stop=True,
                    skip_group_check=True), inc=[(s_te, 1)])
                R = lambda j, par=par: RTC[:, par * 16 + j: par * 16 + j + 1]
                tnm2 = b.op("vector", lambda e, lg=lg, R=R: e.reduce_max(
                    R(0), lg, axis=AX.X, negate=True),
                    wait=[tbias[0]], inc=[(s_ve, 1)])
                tz = b.op("scalar", lambda e, lg=lg, R=R, par=par:
                          e.activation(
                    RTC[:, par * 16 + 8: par * 16 + 16], lg, AF.Exp,
                    bias=R(0), accum_out=R(1)),
                    wait=[tnm2[0]], inc=[(s_se, 1)])
                tlz = b.op("scalar", lambda e, R=R: e.activation(
                    R(2), R(1), AF.Ln), wait=[tz[0]], inc=[(s_se, 1)])
                tmp_ = b.op("scalar", lambda e, R=R: e.activation(
                    R(3), R(2), AF.Exp, scale=-1.0), inc=[(s_se, 1)])
                EQ8 = RTC2[:, par * 8: par * 8 + 8]
                CU8 = RTC3[:, par * 8: par * 8 + 8]
                pv = b.op("vector", lambda e, lg=lg, R=R, EQ8=EQ8:
                          e.tensor_scalar(
                    EQ8, lg, R(0), 0.0, ALU.add, ALU.is_equal),
                    wait=[tz[0]], inc=[(s_ve, 1)])
                rstate["lgfree"] = pv[0]
                pv = b.op("vector", lambda e, CU8=CU8: e.memset(
                    CU8[:, 0:1], 0.0), wait=[pv[0]], inc=[(s_ve, 1)])
                for j in range(1, NEXP):
                    pv = b.op("vector", lambda e, j=j, CU8=CU8, EQ8=EQ8:
                              e.tensor_add(
                        CU8[:, j:j + 1], CU8[:, j - 1:j], EQ8[:, j - 1:j]),
                        wait=[pv[0]], inc=[(s_ve, 1)])
                pv = b.op("vector", lambda e, CU8=CU8: e.tensor_scalar(
                    CU8, CU8, 0.0, None, ALU.is_equal),
                    wait=[pv[0]], inc=[(s_ve, 1)])
                pv = b.op("vector", lambda e, CU8=CU8, EQ8=EQ8: e.tensor_mul(
                    CU8, CU8, EQ8), wait=[pv[0]], inc=[(s_ve, 1)])
                b.op("vector", lambda e, t=t, CU8=CU8, R=R: e.tensor_scalar(
                    SELC[:, t * NEXP:(t + 1) * NEXP], CU8,
                    R(3), None, ALU.mult),
                    wait=[tmp_[0], pv[0]], inc=[(s_ve, 1)])
        b.barrier()

        nm = "baset_b%d" % bi
        if first and tap(nm, (128, ND * T), BF16) is not None:
            b.op("sync", lambda e: e.dma_start(out=tap_handles[nm], in_=BASET[:]),
                 inc=[(s_out, 16)])
        nm2 = "xnt%d_b%d" % (which, bi)
        if tap(nm2, (128, ND * T), BF16) is not None:
            b.op("sync", lambda e: e.dma_start(out=tap_handles[nm2], in_=XNT[:]),
                 inc=[(s_out, 16)])
        if first and tap("s128_b%d" % bi, (128, NT)) is not None:
            b.op("sync", lambda e, bi=bi: e.dma_start(
                out=tap_handles["s128_b%d" % bi], in_=S128[:]), inc=[(s_out, 16)])
        if not first and tap("selc_b%d" % bi, (128, NT * NEXP)) is not None:
            b.op("sync", lambda e, bi=bi: e.dma_start(
                out=tap_handles["selc_b%d" % bi], in_=SELC[:]), inc=[(s_out, 16)])

    def offload_X(bi, dst=None):
        dst = scratch_x1 if dst is None else dst
        toks = []
        for t in range(NT):
            tk = b.op("sync", lambda e, t=t, dst=dst: e.dma_start(
                out=hbm_tiles(dst[bi])[t], in_=bcol(X, t, D)),
                inc=[(s_dma, 16)])
            toks.append(tk[0])
        return toks

    HT = arb(96 * KB, ND * T)
    OWT = arb(128 * KB, ND * 1024)
    SA = arf(144 * KB, T)
    SU = arf(152 * KB, T)
    SH = arf(0 * KB, T)
    SCc = arf(8 * KB, T)
    SD = arf(16 * KB, T)
    WROW = arf(24 * KB, 2048)
    WT4 = arb(128 * KB, 4 * ND * 128)

    def wt(m, dt):
        return WT4[:, (m * ND + dt) * 128:(m * ND + dt + 1) * 128]

    def stage_ssm(bi):
        mats = ["L0_attn_A_w", "L0_attn_B_w", "L0_attn_C_w", "L0_attn_D_w"]
        last_wrow_consumer = None
        evs = []
        mevs = {}
        scan_tok = None
        for nt in range(ND):
            prev_evs = list(evs)
            tl1 = b.op("sync", lambda e, nt=nt: e.dma_start(
                out=WROW[:, 0:1024],
                in_=g[mats[0]][nt * 128:(nt + 1) * 128, :]),
                wait=[last_wrow_consumer] if last_wrow_consumer else [],
                inc=[(s_dma, 16)])
            tl2 = b.op("sync", lambda e, nt=nt: e.dma_start(
                out=WROW[:, 1024:2048],
                in_=g[mats[1]][nt * 128:(nt + 1) * 128, :]), inc=[(s_dma, 16)])
            trans_done = []
            for m in range(4):
                if m == 2:
                    tl1 = b.op("sync", lambda e, nt=nt: e.dma_start(
                        out=WROW[:, 0:1024],
                        in_=g[mats[2]][nt * 128:(nt + 1) * 128, :]),
                        wait=trans_done[-2:], inc=[(s_dma, 16)])
                    tl2 = b.op("sync", lambda e, nt=nt: e.dma_start(
                        out=WROW[:, 1024:2048],
                        in_=g[mats[3]][nt * 128:(nt + 1) * 128, :]),
                        inc=[(s_dma, 16)])
                src_off = (m % 2) * 1024
                dtok = tl1 if m % 2 == 0 else tl2
                tb0 = 2 if m % 2 == 0 else 6
                w0 = ([dtok[0]] + mevs.get(("bank", tb0), []) +
                      mevs.get(("bank", tb0 + 1), []) + prev_evs)
                for dt in range(ND):
                    bank = tb0 + dt // 4
                    off = (dt % 4) * 128
                    tt = b.op("tensor",
                              lambda e, src_off=src_off, dt=dt, bank=bank,
                              off=off: e.transpose(
                                  psf(bank, 128, off=off),
                                  WROW[:, src_off + dt * 128:
                                       src_off + (dt + 1) * 128], IDF[:]),
                              wait=(w0 if dt == 0 else []),
                              inc=[(s_te, 1)])
                tev = b.op("scalar", lambda e, m=m, tb0=tb0: e.copy(
                    WT4.rearrange("p (a q) -> p a q", a=4 * ND)[
                        :, m * ND:(m + 1) * ND],
                    PSF[:, tb0 * 512:(tb0 + 2) * 512].rearrange(
                        "p (a q) -> p a q", a=ND)),
                    wait=[tt[0]], inc=[(s_se, 1)])
                mevs[("bank", tb0)] = [tev[0]]
                mevs[("bank", tb0 + 1)] = [tev[0]]
                trans_done.append(tev[0])
                b.wait("tensor", [tev[0]])
            last_wrow_consumer = trans_done[-1]
            wt_ready = trans_done[-1]

            order = [0, 2, 3, 1]
            quad = {0: 0, 2: 4, 3: 0, 1: 4}
            gstop = {}
            ev_quad = {0: evs, 4: evs}
            sa_ev = sc_ev = sd_ev = None
            u_evs = []
            for m in order:
                qb = quad[m]
                for dt in range(ND):
                    for tc in range(4):
                        tmm = b.op("tensor",
                                   lambda e, m=m, dt=dt, tc=tc, qb=qb: e.matmul(
                                       psf(qb + tc, 512), wt(m, dt),
                                       XNT[:, dt * T + tc * 512:
                                           dt * T + (tc + 1) * 512],
                                       start=(dt == 0), stop=(dt == ND - 1),
                                       skip_group_check=True),
                                   wait=(([wt_ready] + list(ev_quad[qb]))
                                         if dt == 0 and tc == 0 else []),
                                   inc=[(s_te, 1)])
                gstop[m] = tmm[0]
                if m == 0:
                    toks = []
                    for tc in range(4):
                        ta_ = b.op("scalar", lambda e, tc=tc, nt=nt, qb=qb:
                                   e.activation(
                            SA[:, tc * 512:(tc + 1) * 512], psf(qb + tc, 512),
                            AF.Sigmoid, bias=bcol(ABC, nt)),
                            wait=[tmm[0]] if tc == 0 else [], inc=[(s_se, 1)])
                        toks.append(ta_[0])
                    sa_ev = toks
                    ev_quad[qb] = toks
                elif m == 2:
                    toks = []
                    for tc in range(4):
                        tc_ = b.op("scalar", lambda e, tc=tc, qb=qb:
                                   e.activation(
                            SCc[:, tc * 512:(tc + 1) * 512], psf(qb + tc, 512),
                            AF.Sigmoid),
                            wait=[tmm[0]] if tc == 0 else [], inc=[(s_se, 1)])
                        toks.append(tc_[0])
                    sc_ev = toks
                    ev_quad[qb] = toks
                elif m == 3:
                    toks = []
                    for tc in range(4):
                        td_ = b.op("scalar", lambda e, tc=tc, qb=qb:
                                   e.activation(
                            SD[:, tc * 512:(tc + 1) * 512], psf(qb + tc, 512),
                            AF.Sigmoid),
                            wait=[tmm[0]] if tc == 0 else [], inc=[(s_se, 1)])
                        toks.append(td_[0])
                    sd_ev = toks
                    ev_quad[qb] = toks
                else:
                    toks = []
                    for tc in range(4):
                        tu_ = b.op("vector", lambda e, tc=tc, qb=qb:
                                   e.tensor_mul(
                            SU[:, tc * 512:(tc + 1) * 512],
                            SD[:, tc * 512:(tc + 1) * 512], psf(qb + tc, 512)),
                            wait=([tmm[0], sd_ev[-1]] if tc == 0 else []),
                            inc=[(s_ve, 1)])
                        toks.append(tu_[0])
                    u_evs = toks
                    ev_quad[qb] = toks
            tsc = b.op("vector", lambda e: e.tensor_tensor_scan(
                SH[:], SA[:], SU[:], 0.0, ALU.mult, ALU.add),
                wait=[sa_ev[-1], u_evs[-1]], inc=[(s_ve, 1)])
            thc = b.op("vector", lambda e, nt=nt: e.tensor_mul(
                arb(96 * KB + nt * 2 * T, T), SCc[:], SH[:]),
                wait=[tsc[0], sc_ev[-1]], inc=[(s_ve, 1)])
            evs = [sa_ev[-1], sc_ev[-1], sd_ev[-1], u_evs[-1], tsc[0], thc[0]]

        if tap("ht_b%d" % bi, (128, ND * T), BF16) is not None:
            b.barrier()
            b.op("sync", lambda e, bi=bi: e.dma_start(
                out=tap_handles["ht_b%d" % bi], in_=HT), inc=[(s_out, 16)])
        if tap("sascu_b%d" % bi, (128, 4 * T)) is not None:
            b.barrier()
            th_ = tap_handles["sascu_b%d" % bi]
            b.op("sync", lambda e, th_=th_: e.dma_start(out=th_[:, 0:T], in_=SA),
                 inc=[(s_out, 16)])
            b.op("sync", lambda e, th_=th_: e.dma_start(out=th_[:, T:2 * T], in_=SU),
                 inc=[(s_out, 16)])
            b.op("sync", lambda e, th_=th_: e.dma_start(out=th_[:, 2 * T:3 * T], in_=SH),
                 inc=[(s_out, 16)])
            b.op("sync", lambda e, th_=th_: e.dma_start(out=th_[:, 3 * T:4 * T], in_=SCc),
                 inc=[(s_out, 16)])

        prev = last_wrow_consumer
        first_extra = list(evs)
        for dtt in range(ND):
            tl = b.op("sync", lambda e, dtt=dtt: e.dma_start(
                out=WROW[:, 0:1024],
                in_=g["L0_attn_out_w"][dtt * 128:(dtt + 1) * 128, :]),
                wait=([prev] if prev else []) + (first_extra if dtt == 0 else []),
                inc=[(s_dma, 16)])
            first_extra = []
            for ntt in range(ND):
                bank = 2 + ntt // 4
                off = (ntt % 4) * 128
                tt = b.op("tensor", lambda e, ntt=ntt, bank=bank, off=off:
                          e.transpose(psf(bank, 128, off=off),
                                      WROW[:, ntt * 128:(ntt + 1) * 128],
                                      IDF[:]),
                          wait=([tl[0]] + ([prev] if prev else []))
                          if ntt == 0 else [],
                          inc=[(s_te, 1)])
            tev = b.op("scalar", lambda e, dtt=dtt: e.copy(
                OWT.rearrange("p (a q) -> p a q", a=ND)[:, :,
                                                        dtt * 128:(dtt + 1) * 128],
                PSF[:, 2 * 512: 4 * 512].rearrange("p (a q) -> p a q", a=ND)),
                wait=[tt[0]], inc=[(s_se, 1)])
            prev = tev[0]
            b.wait("tensor", [prev])
        b.barrier()

    def stage_yadd(bi):
        XSTG = [arf(0 * KB, D), arf(4 * KB, D)]
        slot_free = [None, None]
        prev_add = {}
        for t in range(NT):
            sl = t % 2
            tk = b.op("sync", lambda e, t=t, sl=sl: e.dma_start(
                out=XSTG[sl][:], in_=hbm_tiles(scratch_x1[bi])[t]),
                wait=[slot_free[sl]] if slot_free[sl] else [],
                inc=[(s_dma, 16)])
            adds = []
            for dc in range(2):
                bank = (t * 2 + dc) % 4
                for ntt in range(ND):
                    b.op("tensor", lambda e, t=t, dc=dc, ntt=ntt, bank=bank:
                         e.matmul(psf(bank, 512),
                                  HT[:, ntt * T + t * 128: ntt * T + (t + 1) * 128],
                                  OWT[:, ntt * 1024 + dc * 512:
                                      ntt * 1024 + (dc + 1) * 512],
                                  start=(ntt == 0), stop=False,
                                  skip_group_check=True),
                         wait=([prev_add[bank]] if ntt == 0 and bank in prev_add
                               else []),
                         inc=[(s_te, 1)])
                tb_ = b.op("tensor", lambda e, dc=dc, bank=bank: e.matmul(
                    psf(bank, 512), ONESB[:],
                    ROWSB[0:1, dc * 512:(dc + 1) * 512],
                    start=False, stop=True, skip_group_check=True),
                    inc=[(s_te, 1)])
                tadd = b.op("vector", lambda e, sl=sl, dc=dc, bank=bank:
                            e.tensor_add(
                                XSTG[sl][:, dc * 512:(dc + 1) * 512],
                                XSTG[sl][:, dc * 512:(dc + 1) * 512],
                                psf(bank, 512)),
                            wait=[tb_[0], tk[0]], inc=[(s_ve, 1)])
                prev_add[bank] = tadd[0]
                adds.append(tadd[0])
            tw = b.op("sync", lambda e, t=t, sl=sl: e.dma_start(
                out=hbm_tiles(scratch_x1[bi])[t], in_=XSTG[sl][:]),
                wait=adds, inc=[(s_dma, 16)])
            slot_free[sl] = tw[0]
        b.barrier()

    def routing_steps(bi):
        steps = []
        VV = RT[:, 0:NT]
        TMP = RT[:, 16:32]
        BC1 = RT[:, 32:33]
        BC2 = RT[:, 33:34]
        PC = RT[:, 34:35]
        EQM = RT[:, 35:51]
        ch = {"t": None}
        SEMS = {"vector": s_ve, "scalar": s_se, "tensor": s_te, "sync": s_out}

        def rop(eng, fn, extra=()):
            tok = b.op(eng, fn,
                       wait=([ch["t"]] if ch["t"] else []) + list(extra),
                       inc=[(SEMS[eng], 1)] if eng != "sync" else [(s_out, 16)])
            ch["t"] = tok[0]
            return tok[0]

        def rpart_reduce(src_col, dst_col, alu):
            rop("tensor", lambda e, src_col=src_col: e.transpose(
                psf(5, 128)[0:1, :], src_col, IDF[:]))
            rop("scalar", lambda e: e.copy(ROW128[:], psf(5, 128)[0:1, :]))
            b.wait("tensor", [ch["t"]])
            rop("vector", lambda e, dst_col=dst_col, alu=alu: e.tensor_reduce(
                RSC[0:1, dst_col:dst_col + 1], ROW128[:], axis=AX.X, op=alu))

        def rbcast(src_off, n, dst):
            rop("tensor", lambda e, src_off=src_off, n=n: e.matmul(
                psf(5, n), ONES[:], RSC[0:1, src_off:src_off + n],
                start=True, stop=True))
            rop("scalar", lambda e, dst=dst, n=n: e.copy(dst, psf(5, n)))
            b.wait("tensor", [ch["t"]])

        def s_init():
            rop("vector", lambda e: e.reduce_max(PC, S128[:], axis=AX.X))
            rpart_reduce(PC, 2, ALU.max)
            rop("vector", lambda e: e.memset(RSC[0:1, 0:1], 0.5753641128540039))
        steps.append(s_init)

        def cd_iter(_it):
            rop("vector", lambda e: e.tensor_tensor(
                RSC[0:1, 9:10], RSC[0:1, 2:3], RSC[0:1, 0:1], ALU.min))
            rop("vector", lambda e: e.tensor_scalar(
                RSC[0:1, 1:2], RSC[0:1, 9:10], -1.0, None, ALU.mult))
            rbcast(0, 2, RT[:, 32:34])
            rop("vector", lambda e: e.tensor_scalar(
                TMP, S128[:], BC1, BC2, ALU.min, ALU.add))
            rop("scalar", lambda e: e.activation(
                TMP, TMP, AF.Exp, accum_out=PC))
            rpart_reduce(PC, 3, ALU.add)
            rop("scalar", lambda e: e.activation(
                RSC[0:1, 3:4], RSC[0:1, 3:4], AF.Ln))
            rop("vector", lambda e: e.tensor_tensor(
                RSC[0:1, 9:10], RSC[0:1, 3:4], RSC[0:1, 1:2], ALU.subtract))
            rop("vector", lambda e: e.tensor_scalar(
                RSC[0:1, 0:1], RSC[0:1, 9:10], -LOGK, None, ALU.add))
        for it in range(NITER_CD - 1):
            steps.append(lambda it=it: cd_iter(it))

        def v_init():
            rop("vector", lambda e: e.tensor_scalar(
                RSC[0:1, 9:10], RSC[0:1, 0:1], -1.0, None, ALU.mult))
            rbcast(9, 1, RT[:, 32:33])
            rop("vector", lambda e: e.tensor_scalar(
                VV, S128[:], BC1, 0.0, ALU.add, ALU.min))
            rop("vector", lambda e: e.tensor_reduce(
                PC, VV, axis=AX.X, op=ALU.min))
            rpart_reduce(PC, 9, ALU.min)
            rop("vector", lambda e: e.tensor_scalar(
                RSC[0:1, 5:6], RSC[0:1, 9:10], -1.0, None, ALU.add))
            rop("vector", lambda e: e.memset(RSC[0:1, 6:7], 1.0))
            rop("vector", lambda e: e.memset(RSC[0:1, 12:13], 1024.0))
        steps.append(v_init)

        def count_vv(thr_col, cmp_op, out_col):
            rbcast(thr_col, 1, RT[:, 32:33])
            rop("vector", lambda e, cmp_op=cmp_op: e.tensor_scalar(
                TMP, VV, BC1, None, cmp_op))
            rop("vector", lambda e: e.reduce_sum(PC, TMP, axis=AX.X))
            rpart_reduce(PC, out_col, ALU.add)

        def count_ix():
            rbcast(15, 1, RT[:, 32:33])
            rop("vector", lambda e: e.tensor_scalar(
                TMP, IOTA[:], BC1, None, ALU.is_le))
            rop("vector", lambda e: e.tensor_mul(TMP, TMP, EQM))
            rop("vector", lambda e: e.reduce_sum(PC, TMP, axis=AX.X))
            rpart_reduce(PC, 4, ALU.add)

        def bis_iter(locol, hicol, midcol, cntfn, invert):
            rop("vector", lambda e: e.tensor_tensor(
                RSC[0:1, midcol:midcol + 1], RSC[0:1, locol:locol + 1],
                RSC[0:1, hicol:hicol + 1], ALU.add))
            rop("vector", lambda e: e.tensor_scalar(
                RSC[0:1, midcol:midcol + 1], RSC[0:1, midcol:midcol + 1],
                0.5, None, ALU.mult))
            cntfn()
            cmp_op = ALU.is_lt if invert else ALU.is_ge
            rop("vector", lambda e, cmp_op=cmp_op: e.tensor_tensor(
                RSC[0:1, 8:9], RSC[0:1, 4:5], RSC[0:1, 12:13], cmp_op))
            rop("vector", lambda e: e.tensor_tensor(
                RSC[0:1, 9:10], RSC[0:1, midcol:midcol + 1],
                RSC[0:1, locol:locol + 1], ALU.subtract))
            rop("vector", lambda e: e.tensor_mul(
                RSC[0:1, 9:10], RSC[0:1, 9:10], RSC[0:1, 8:9]))
            rop("vector", lambda e: e.tensor_add(
                RSC[0:1, locol:locol + 1], RSC[0:1, locol:locol + 1],
                RSC[0:1, 9:10]))
            rop("vector", lambda e: e.tensor_scalar(
                RSC[0:1, 8:9], RSC[0:1, 8:9], -1.0, 1.0, ALU.mult, ALU.add))
            rop("vector", lambda e: e.tensor_tensor(
                RSC[0:1, 9:10], RSC[0:1, midcol:midcol + 1],
                RSC[0:1, hicol:hicol + 1], ALU.subtract))
            rop("vector", lambda e: e.tensor_mul(
                RSC[0:1, 9:10], RSC[0:1, 9:10], RSC[0:1, 8:9]))
            rop("vector", lambda e: e.tensor_add(
                RSC[0:1, hicol:hicol + 1], RSC[0:1, hicol:hicol + 1],
                RSC[0:1, 9:10]))

        for it in range(NITER_TH):
            steps.append(lambda it=it: bis_iter(
                5, 6, 7, lambda: count_vv(7, ALU.is_gt, 4), False))

        def theta_final():
            rbcast(6, 1, RT[:, 32:33])
            rop("vector", lambda e: e.tensor_scalar(
                TMP, VV, BC1, None, ALU.is_le))
            rop("vector", lambda e: e.tensor_scalar(
                EQM, TMP, -1.0, 1.0, ALU.mult, ALU.add))
            rop("vector", lambda e: e.tensor_scalar(
                EQM, EQM, -1e30, None, ALU.mult))
            rop("vector", lambda e: e.tensor_mul(TMP, TMP, VV))
            rop("vector", lambda e: e.tensor_add(TMP, TMP, EQM))
            rop("vector", lambda e: e.reduce_max(PC, TMP, axis=AX.X))
            rpart_reduce(PC, 10, ALU.max)
            count_vv(10, ALU.is_gt, 11)
            rop("vector", lambda e: e.tensor_scalar(
                RSC[0:1, 12:13], RSC[0:1, 11:12], -1.0, 1024.0, ALU.mult,
                ALU.add))
            rop("vector", lambda e: e.tensor_scalar(
                EQM, VV, BC1, None, ALU.is_equal))
            rop("vector", lambda e: e.memset(RSC[0:1, 13:14], -1.0))
            rop("vector", lambda e: e.memset(RSC[0:1, 14:15], 2047.0))
        steps.append(theta_final)

        for it in range(NITER_IX):
            steps.append(lambda it=it: bis_iter(13, 14, 15, count_ix, True))

        def mask_final():
            rbcast(10, 1, RT[:, 32:33])
            rop("vector", lambda e: e.tensor_scalar(
                TMP, VV, BC1, None, ALU.is_gt))
            rbcast(14, 1, RT[:, 33:34])
            rop("vector", lambda e: e.tensor_scalar(
                MASKC[:], IOTA[:], BC2, None, ALU.is_le))
            rop("vector", lambda e: e.tensor_mul(MASKC[:], MASKC[:], EQM))
            rop("vector", lambda e: e.tensor_add(MASKC[:], MASKC[:], TMP))
            if tap("maskc_b%d" % bi, (128, NT)) is not None:
                rop("sync", lambda e, bi=bi: e.dma_start(
                    out=tap_handles["maskc_b%d" % bi], in_=MASKC[:]))
            if tap("vsel_b%d" % bi, (128, NT + 2)) is not None:
                th_ = tap_handles["vsel_b%d" % bi]
                rop("sync", lambda e, th_=th_: e.dma_start(
                    out=th_[:, 0:NT], in_=VV))
                rop("sync", lambda e, th_=th_: e.dma_start(
                    out=th_[0:1, NT:NT + 2], in_=RSC[0:1, 10:12]))
        steps.append(mask_final)
        return steps

    WQL = arb(0 * KB, ND * 512)
    WKVL = arb(8 * KB, ND * 1024)
    WOL = arb(24 * KB, 4 * 1024)
    OT = arb(32 * KB, 4 * T)
    QTW = arb(48 * KB, 512)
    KTW = arb(49 * KB, 512)
    VW = arb(50 * KB, 512)
    EW = arb(51 * KB, 128)
    ETW = arb(52 * KB, 128)
    OLW = arb(53 * KB, 512)
    ZL8 = arf(54 * KB, 8)
    RZ8 = arf(55 * KB, 8)
    WSTG = arf(56 * KB, 1024)

    def load_wcast(dst, src_ap, cols, scale_col=None, wait=(), tag=0):
        tl = b.op("sync", lambda e, src_ap=src_ap, cols=cols: e.dma_start(
            out=WSTG[:, 0:cols], in_=src_ap), wait=list(wait), inc=[(s_dma, 16)])
        if scale_col is None:
            tc = b.op("scalar", lambda e, dst=dst, cols=cols: e.copy(
                dst, WSTG[:, 0:cols]), wait=[tl[0]], inc=[(s_se, 1)])
        else:
            tc = b.op("scalar", lambda e, dst=dst, cols=cols,
                      scale_col=scale_col: e.activation(
                dst, WSTG[:, 0:cols], AF.Identity, scale=scale_col),
                wait=[tl[0]], inc=[(s_se, 1)])
        return tc[0]

    def stage_light(bi, rsteps):
        xload = []
        for t in range(NT):
            tk = b.op("sync", lambda e, t=t: e.dma_start(
                out=bcol(X, t, D), in_=hbm_tiles(scratch_x1[bi])[t]),
                inc=[(s_dma, 16)])
            xload.append(tk[0])
        b.xload_light = xload
        wtoks = []
        prev = None
        for dt in range(ND):
            prev = load_wcast(bcol(WQL, dt, 512),
                              g["L0_ffn_wq_light"][dt * 128:(dt + 1) * 128, :],
                              512, scale_col=bcol(GLC, dt),
                              wait=[prev] if prev else [])
            wtoks.append(prev)
        for dt in range(ND):
            prev = load_wcast(bcol(WKVL, dt, 1024),
                              g["L0_ffn_wkv_light"][dt * 128:(dt + 1) * 128, :],
                              1024, scale_col=bcol(GLC, dt), wait=[prev])
            wtoks.append(prev)
        for ht in range(4):
            prev = load_wcast(bcol(WOL, ht, 1024),
                              g["L0_ffn_wo_light"][ht * 128:(ht + 1) * 128, :],
                              1024, wait=[prev])
            wtoks.append(prev)
        wq_ready = wtoks[ND - 1]
        wkv_ready = wtoks[2 * ND - 1]
        wo_ready = wtoks[-1]

        rs_i = 0
        prev_otev = None
        for t in range(NT):
            n_per = (len(rsteps) + NT - 1) // NT
            for _ in range(n_per):
                if rs_i < len(rsteps):
                    rsteps[rs_i]()
                    rs_i += 1
            evq = []
            for mj in range(4):
                bank = mj % 2
                for dt in range(ND):
                    tmm = b.op("tensor", lambda e, mj=mj, dt=dt, t=t, bank=bank:
                               e.matmul(psf(bank, 128),
                                        WQL[:, dt * 512 + mj * 128:
                                            dt * 512 + (mj + 1) * 128],
                                        BASET[:, dt * T + t * 128:
                                              dt * T + (t + 1) * 128],
                                        start=(dt == 0), stop=(dt == ND - 1),
                                        skip_group_check=True),
                               wait=([wq_ready] if dt == 0 and t == 0 and mj < 2
                                     else ([evq[mj - 2]] if dt == 0 and mj >= 2
                                           else ([prev_otev] if dt == 0 else []))),
                               inc=[(s_te, 1)])
                tev = b.op("scalar", lambda e, mj=mj, bank=bank: e.activation(
                    bcol(QTW, mj, 128), psf(bank, 128), AF.Copy, scale=0.125),
                    wait=[tmm[0]], inc=[(s_se, 1)])
                evq.append(tev[0])
            evk = []
            for mj in range(4):
                bank = 2 + mj % 2
                for dt in range(ND):
                    tmm = b.op("tensor", lambda e, mj=mj, dt=dt, t=t, bank=bank:
                               e.matmul(psf(bank, 128),
                                        WKVL[:, dt * 1024 + mj * 128:
                                            dt * 1024 + (mj + 1) * 128],
                                        BASET[:, dt * T + t * 128:
                                              dt * T + (t + 1) * 128],
                                        start=(dt == 0), stop=(dt == ND - 1),
                                        skip_group_check=True),
                               wait=([wkv_ready] if dt == 0 and t == 0 and mj < 2
                                     else ([evk[mj - 2]] if dt == 0 and mj >= 2
                                           else [])),
                               inc=[(s_te, 1)])
                tev = b.op("scalar", lambda e, mj=mj, bank=bank: e.copy(
                    bcol(KTW, mj, 128), psf(bank, 128)),
                    wait=[tmm[0]], inc=[(s_se, 1)])
                evk.append(tev[0])
            for dt in range(ND):
                tmm = b.op("tensor", lambda e, dt=dt, t=t: e.matmul(
                    psf(4, 512),
                    BASET[:, dt * T + t * 128: dt * T + (t + 1) * 128],
                    WKVL[:, dt * 1024 + 512: dt * 1024 + 1024],
                    start=(dt == 0), stop=(dt == ND - 1),
                    skip_group_check=True),
                    wait=([evk[-1]] if dt == 0 and t == 0 else []),
                    inc=[(s_te, 1)])
            tvev = b.op("scalar", lambda e: e.copy(VW[:], psf(4, 512)),
                        wait=[tmm[0]], inc=[(s_se, 1)])
            b.wait("tensor", [tvev[0]])

            zevs = []
            oev = None
            for h in range(8):
                mj = h // 2
                pb_ = (h % 2) * 64
                tsc_ = b.op("tensor", lambda e, mj=mj, pb_=pb_: e.matmul(
                    psf(2, 128),
                    QTW[pb_:pb_ + 64, mj * 128:(mj + 1) * 128],
                    KTW[pb_:pb_ + 64, mj * 128:(mj + 1) * 128],
                    start=True, stop=True),
                    wait=[evq[mj], evk[mj]] + ([zevs[-1]] if zevs else []),
                    inc=[(s_te, 1)])
                tnm_ = b.op("vector", lambda e: e.reduce_max(
                    TCOL[:, 8:9], psf(2, 128), axis=AX.X, negate=True),
                    wait=[tsc_[0]], inc=[(s_ve, 1)])
                tex_ = b.op("scalar", lambda e, h=h: e.activation(
                    EW[:], psf(2, 128), AF.Exp, bias=TCOL[:, 8:9],
                    accum_out=bcol(ZL8, h)), wait=[tnm_[0]], inc=[(s_se, 1)])
                ttr_ = b.op("tensor", lambda e: e.transpose(
                    psb(0, 128), EW[:], IDB[:]),
                    wait=[tex_[0]], inc=[(s_te, 1)])
                tet_ = b.op("scalar", lambda e: e.copy(ETW[:], psb(0, 128)),
                            wait=[ttr_[0]], inc=[(s_se, 1)])
                tav_ = b.op("tensor", lambda e, h=h: e.matmul(
                    psf(3, 512)[:, h * 64:(h + 1) * 64], ETW[:],
                    VW[:, h * 64:(h + 1) * 64],
                    start=(h == 0), stop=(h == 7), skip_group_check=True),
                    wait=[tet_[0]] + ([oev] if h == 0 and oev else []),
                    inc=[(s_te, 1)])
                zevs.append(tex_[0])
            tlz = b.op("scalar", lambda e: e.activation(
                RZ8[:, 0:8], ZL8[:, 0:8], AF.Ln), wait=zevs, inc=[(s_se, 1)])
            trz = b.op("scalar", lambda e: e.activation(
                RZ8[:, 0:8], RZ8[:, 0:8], AF.Exp, scale=-1.0), inc=[(s_se, 1)])
            prev_t = None
            for h in range(8):
                tevo = b.op("scalar", lambda e, h=h: e.activation(
                    bcol(OLW, h, 64), psf(3, 512)[:, h * 64:(h + 1) * 64],
                    AF.Identity, scale=bcol(RZ8, h)),
                    wait=[tav_[0]] if h == 0 else [], inc=[(s_se, 1)])
                tto = b.op("tensor", lambda e, h=h: e.transpose(
                    psb(1, 128)[0:64, :], bcol(OLW, h, 64), IDB[:]),
                    wait=[tevo[0]] + ([prev_t] if prev_t else []),
                    inc=[(s_te, 1)])
                tote = b.op("scalar", lambda e, h=h, t=t: e.copy(
                    OT[(h % 2) * 64:(h % 2) * 64 + 64,
                       (h // 2) * T + t * 128:(h // 2) * T + (t + 1) * 128],
                    psb(1, 128)[0:64, 0:128]),
                    wait=[tto[0]], inc=[(s_se, 1)])
                prev_t = tote[0]
            oev = prev_t
            prev_otev = prev_t

            for dc in range(2):
                bank = dc
                for ht in range(4):
                    tmm = b.op("tensor", lambda e, ht=ht, dc=dc, t=t, bank=bank:
                               e.matmul(psf(bank, 512),
                                        OT[:, ht * T + t * 128:
                                           ht * T + (t + 1) * 128],
                                        WOL[:, ht * 1024 + dc * 512:
                                            ht * 1024 + (dc + 1) * 512],
                                        start=(ht == 0), stop=(ht == 3),
                                        skip_group_check=True),
                               wait=([prev_t, wo_ready] if ht == 0 else []),
                               inc=[(s_te, 1)])
                tadd = b.op("vector", lambda e, t=t, dc=dc, bank=bank:
                            e.tensor_add(
                                X[:, t * D + dc * 512: t * D + (dc + 1) * 512],
                                X[:, t * D + dc * 512: t * D + (dc + 1) * 512],
                                psf(bank, 512)),
                            wait=[tmm[0], xload[t]], inc=[(s_ve, 1)])
                b.wait("tensor", [tadd[0]])
        while rs_i < len(rsteps):
            rsteps[rs_i]()
            rs_i += 1
        b.barrier()

    WQH = arb(0 * KB, ND * 512)
    WKVTH = arb(8 * KB, ND * 1024)
    ETH = arb(24 * KB, NT * 256)
    OTH = arb(32 * KB, 4 * T)
    WSTGH = arf(48 * KB, 1024)
    QTM0 = arb(52 * KB, T)
    QTM1 = arb(56 * KB, T)
    KTO = arb(60 * KB, T)
    VH = arb(48 * KB, NT * 66)
    OHS = arb(50 * KB + 256, 512)
    MRW = arf(51 * KB + 256, 128)

    def stage_heavy(bi):
        b.ohs_free = None
        b.psb1_free = None

        def load_wcast_h(dst, src_ap, cols, scale_col=None, wait=()):
            tl = b.op("sync", lambda e, src_ap=src_ap, cols=cols: e.dma_start(
                out=WSTGH[:, 0:cols], in_=src_ap), wait=list(wait),
                inc=[(s_dma, 16)])
            if scale_col is None:
                tc = b.op("scalar", lambda e, dst=dst, cols=cols: e.copy(
                    dst, WSTGH[:, 0:cols]), wait=[tl[0]], inc=[(s_se, 1)])
            else:
                tc = b.op("scalar", lambda e, dst=dst, cols=cols,
                          scale_col=scale_col: e.activation(
                    dst, WSTGH[:, 0:cols], AF.Identity, scale=scale_col),
                    wait=[tl[0]], inc=[(s_se, 1)])
            return tc[0]

        prev = None
        for dt in range(ND):
            prev = load_wcast_h(bcol(WQH, dt, 512),
                                g["L0_ffn_wq_heavy"][dt * 128:(dt + 1) * 128, :],
                                512, scale_col=bcol(GHC, dt),
                                wait=[prev] if prev else [])
        for rt_ in range(ND):
            tl = b.op("sync", lambda e, rt_=rt_: e.dma_start(
                out=WSTGH[:, 0:1024],
                in_=g["L0_ffn_wkv_heavy"][rt_ * 128:(rt_ + 1) * 128, :]),
                wait=[prev], inc=[(s_dma, 16)])
            for dt in range(ND):
                fb = 4 + (dt % 2)
                tt = b.op("tensor", lambda e, dt=dt, fb=fb: e.transpose(
                    psf(fb, 128), WSTGH[:, dt * 128:(dt + 1) * 128], IDF[:]),
                    wait=[tl[0]] + ([prev] if prev else []), inc=[(s_te, 1)])
                tev = b.op("scalar", lambda e, dt=dt, rt_=rt_, fb=fb:
                           e.activation(
                    WKVTH[:, dt * 1024 + rt_ * 128: dt * 1024 + (rt_ + 1) * 128],
                    psf(fb, 128), AF.Identity, scale=bcol(GHC, dt)),
                    wait=[tt[0]], inc=[(s_se, 1)])
                prev = tev[0]
            b.wait("tensor", [prev])
        wall = prev

        tko = b.op("gpsimd", lambda e: e.memset(KTO[:], 1.0),
                   wait=[wall], inc=[(s_gp, 1)])
        tvo0 = b.op("gpsimd", lambda e: e.memset(VH[:], 1.0),
                    wait=[tko[0]], inc=[(s_gp, 1)])

        QTMs = [QTM0, QTM1]
        qtm_free = [None, None]
        kto_free = None
        vh_free = None
        last_ot = None
        NQC = 8
        for j in range(4):
            pevs = []
            for tc in range(4):
                bank = tc % 2
                for dt in range(ND):
                    tmm = b.op("tensor", lambda e, j=j, dt=dt, tc=tc, bank=bank:
                               e.matmul(psf(bank, 512),
                                        WQH[:, dt * 512 + j * 128:
                                            dt * 512 + (j + 1) * 128],
                                        BASET[:, dt * T + tc * 512:
                                              dt * T + (tc + 1) * 512],
                                        start=(dt == 0), stop=(dt == ND - 1),
                                        skip_group_check=True),
                               wait=(([wall] + [x for x in qtm_free if x] +
                                      (pevs[-2:] if pevs else []))
                                     if dt == 0 else []),
                               inc=[(s_te, 1)])
                ev2 = []
                for hh in range(2):
                    tev = b.op("scalar",
                               lambda e, tc=tc, hh=hh, bank=bank: e.activation(
                        QTMs[hh][hh * 0 + 0:64, tc * 512:(tc + 1) * 512]
                        if False else
                        QTMs[hh].rearrange("p f -> p f")[0:64,
                                                        tc * 512:(tc + 1) * 512],
                        psf(bank, 512)[hh * 64:hh * 64 + 64, :],
                        AF.Copy, scale=0.125),
                        wait=[tmm[0]], inc=[(s_se, 1)])
                    ev2.append(tev[0])
                pevs.append(ev2[-1])
            b.wait("tensor", [pevs[-1]])
            qproj_done = pevs[-1]

            for hh in range(2):
                h = 2 * j + hh
                QTM = QTMs[hh]
                kevs = []
                for tc in range(4):
                    for dt in range(ND):
                        tmm = b.op("tensor",
                                   lambda e, h=h, dt=dt, tc=tc: e.matmul(
                            psf(2, 512)[0:64, :],
                            WKVTH[:, dt * 1024 + h * 128:
                                  dt * 1024 + h * 128 + 64],
                            BASET[:, dt * T + tc * 512:
                                  dt * T + (tc + 1) * 512],
                            start=(dt == 0), stop=(dt == ND - 1),
                            skip_group_check=True),
                            wait=(([kevs[-1]] if kevs else
                                   [tko[0]] + ([kto_free] if kto_free else []))
                                  if dt == 0 else []),
                            inc=[(s_te, 1)])
                    tev = b.op("scalar", lambda e, tc=tc: e.copy(
                        KTO[0:64, tc * 512:(tc + 1) * 512],
                        psf(2, 512)[0:64, :]), wait=[tmm[0]], inc=[(s_se, 1)])
                    kevs.append(tev[0])
                vevs = []
                for kt in range(NT):
                    for dt in range(ND):
                        tmm = b.op("tensor",
                                   lambda e, h=h, dt=dt, kt=kt: e.matmul(
                            psf(3, 64),
                            BASET[:, dt * T + kt * 128: dt * T + (kt + 1) * 128],
                            WKVTH[:, dt * 1024 + h * 128 + 64:
                                  dt * 1024 + h * 128 + 128],
                            start=(dt == 0), stop=(dt == ND - 1),
                            skip_group_check=True),
                            wait=(([vevs[-1]] if vevs else
                                   [tvo0[0]] + ([vh_free] if vh_free else []))
                                  if dt == 0 else []),
                            inc=[(s_te, 1)])
                    tev = b.op("scalar", lambda e, kt=kt: e.copy(
                        VH[:, kt * 66: kt * 66 + 64], psf(3, 64)),
                        wait=[tmm[0]], inc=[(s_se, 1)])
                    vevs.append(tev[0])

                M128 = RT[:, 35:51]
                p1prev = {}
                for t in range(NT):
                    bank = 2 + t % 2
                    tmm1 = b.op("tensor",
                                lambda e, t=t, bank=bank, QTM=QTM: e.matmul(
                        psf(bank, 256),
                        QTM[0:64, t * 128:(t + 1) * 128],
                        KTO[0:64, 0:256], start=True, stop=True),
                        wait=([qproj_done, kevs[-1], vevs[-1]] if t == 0 else
                              ([p1prev[bank]] if bank in p1prev else [])),
                        inc=[(s_te, 1)])
                    trm = b.op("vector", lambda e, t=t, bank=bank, M128=M128:
                               e.reduce_max(
                        bcol(M128, t), psf(bank, 256),
                        axis=AX.X, negate=True),
                        wait=[tmm1[0]], inc=[(s_ve, 1)])
                    p1prev[bank] = trm[0]
                p1prev = trm[0]
                ttm = b.op("tensor", lambda e, M128=M128: e.transpose(
                    psf(2, 128)[0:16, :], M128, IDF[:]),
                    wait=[p1prev], inc=[(s_te, 1)])
                tme = b.op("scalar", lambda e: e.copy(
                    MRW[0:16, 0:128], psf(2, 128)[0:16, :]),
                    wait=[ttm[0]], inc=[(s_se, 1)])
                b.wait("tensor", [tme[0]])
                tmd = b.op("gpsimd", lambda e, QTM=QTM: e.dma_start(
                    out=QTM[64:65, 0:T], in_=MRW[0:16, 0:128]),
                    wait=[tme[0]], inc=[(s_gdma, 16)])

                et_last2 = [tmd[0], tmd[0]]
                avprev = [None, None]
                for qc in range(NQC):
                    etevs = []
                    for kt in range(NT):
                        bank = (kt % 4) // 2
                        off = (kt % 2) * 256
                        tmm = b.op("tensor",
                                   lambda e, kt=kt, qc=qc, bank=bank, off=off,
                                   QTM=QTM: e.matmul(
                            psf(bank, 256, off=off),
                            KTO[0:65, kt * 128:(kt + 1) * 128],
                            QTM[0:65, qc * 256:(qc + 1) * 256],
                            start=True, stop=True),
                            wait=([et_last2[bank]] if kt < 4 and kt % 2 == 0
                                  else ([etevs[kt // 2 - 2]]
                                        if kt % 2 == 0 and kt >= 4 else [])),
                            inc=[(s_te, 1)])
                        if kt % 2 == 1:
                            tex = b.op("scalar", lambda e, kt=kt, bank=bank:
                                       e.activation(
                                ETH[:, (kt - 1) * 256:(kt + 1) * 256],
                                psf(bank, 512), AF.Exp),
                                wait=[tmm[0]], inc=[(s_se, 1)])
                            etevs.append(tex[0])
                    et_last2 = [etevs[-2], etevs[-1]]
                    for ms in range(2):
                        bank = 4 + ms
                        for kt in range(NT):
                            tmm = b.op("tensor",
                                       lambda e, kt=kt, ms=ms, bank=bank:
                                       e.matmul(
                                PSF[:, bank * 512: bank * 512 + 66],
                                ETH[:, kt * 256 + ms * 128:
                                    kt * 256 + ms * 128 + 128],
                                VH[:, kt * 66:(kt + 1) * 66],
                                start=(kt == 0), stop=(kt == NT - 1),
                                skip_group_check=True),
                                wait=(([etevs[-1]] +
                                       ([avprev[ms]] if avprev[ms] else []))
                                      if kt == 0 else []),
                                inc=[(s_te, 1)])
                        sl = (qc * 2 + ms) % 8
                        trz = b.op("vector", lambda e, bank=bank, sl=sl:
                                   e.reciprocal(
                            TCOL2[:, sl:sl + 1],
                            PSF[:, bank * 512 + 64: bank * 512 + 65]),
                            wait=[tmm[0]] + ([b.ohs_free] if sl == 0 and
                                             b.ohs_free else []),
                            inc=[(s_ve, 1)])
                        tevo = b.op("vector", lambda e, bank=bank, sl=sl:
                                    e.tensor_scalar(
                            OHS[:, sl * 64:(sl + 1) * 64],
                            PSF[:, bank * 512: bank * 512 + 64],
                            TCOL2[:, sl:sl + 1], None, ALU.mult),
                            wait=[trz[0]], inc=[(s_ve, 1)])
                        avprev[ms] = tevo[0]
                        if sl == 7:
                            for j2 in range(8):
                                tto = b.op("tensor", lambda e, j2=j2:
                                           e.transpose(
                                    psb(1, 128, off=j2 * 128)[0:64, :],
                                    OHS[:, j2 * 64:(j2 + 1) * 64], IDB[:]),
                                    wait=([tevo[0]] + ([b.psb1_free] if
                                          b.psb1_free else []))
                                    if j2 == 0 else [],
                                    inc=[(s_te, 1)])
                            half8 = qc // 4
                            tote = b.op("scalar", lambda e, h=h, half8=half8:
                                        e.copy(
                                OTH[(h % 2) * 64:(h % 2) * 64 + 64,
                                    (h // 2) * T + half8 * 1024:
                                    (h // 2) * T + (half8 + 1) * 1024],
                                psb(1, 1024)[0:64, :]),
                                wait=[tto[0]], inc=[(s_se, 1)])
                            b.ohs_free = tto[0]
                            b.psb1_free = tote[0]
                            last_ot = tote[0]
                qtm_free[hh] = et_last2[1]
                kto_free = et_last2[1]
                vh_free = last_ot

        WOH2 = arb(0 * KB, 4 * 1024)
        prevw = None
        for ht in range(4):
            prevw = load_wcast_h(bcol(WOH2, ht, 1024),
                                 g["L0_ffn_wo_heavy"][ht * 128:(ht + 1) * 128, :],
                                 1024,
                                 wait=[last_ot] if ht == 0 else [prevw])
        woadd_prev = None
        for t in range(NT):
            for dc in range(2):
                bank = dc
                for ht in range(4):
                    tmm = b.op("tensor", lambda e, ht=ht, dc=dc, t=t, bank=bank:
                               e.matmul(psf(bank, 512),
                                        OTH[:, ht * T + t * 128:
                                            ht * T + (t + 1) * 128],
                                        WOH2[:, ht * 1024 + dc * 512:
                                             ht * 1024 + (dc + 1) * 512],
                                        start=(ht == 0), stop=(ht == 3),
                                        skip_group_check=True),
                               wait=(([prevw, last_ot] +
                                      ([woadd_prev] if woadd_prev else []))
                                     if ht == 0 else []),
                               inc=[(s_te, 1)])
                tadd = b.op("vector", lambda e, t=t, dc=dc, bank=bank:
                            e.scalar_tensor_tensor(
                                X[:, t * D + dc * 512: t * D + (dc + 1) * 512],
                                psf(bank, 512), bcol(MASKC, t),
                                X[:, t * D + dc * 512: t * D + (dc + 1) * 512],
                                ALU.mult, ALU.add),
                            wait=[tmm[0]], inc=[(s_ve, 1)])
                woadd_prev = tadd[0]
        b.barrier()

    WQ1 = arb(0 * KB, ND * 1024)
    WK1 = arb(16 * KB, ND * 64)
    WV1 = arb(17 * KB, ND * 64)
    ETH1 = arb(18 * KB, NT * 256)
    OT1H = arb(64 * KB, 8 * 1024)
    QTM1s = [arb(80 * KB, 1024), arb(82 * KB, 1024)]
    KTO1 = arb(84 * KB, T)
    V1 = arb(88 * KB, NT * 66)
    OHS1 = arb(91 * KB, 512)
    MRW1 = arf(92 * KB, 128)

    def stage_l1attn(bi):
        b.ohs_free = None
        b.psb1_free = None
        tk_ = b.op("gpsimd", lambda e: e.dma_start(
            out=WK1.rearrange("p (a c) -> p a c", a=ND),
            in_=g["L1_attn_Wk"].rearrange("(a p) c -> p a c", p=128)),
            inc=[(s_gdma, 16)])
        tv_ = b.op("gpsimd", lambda e: e.dma_start(
            out=WV1.rearrange("p (a c) -> p a c", a=ND),
            in_=g["L1_attn_Wv"].rearrange("(a p) c -> p a c", p=128)),
            inc=[(s_gdma, 16)])

        tko = b.op("gpsimd", lambda e: e.memset(KTO1[:], 1.0), inc=[(s_gp, 1)])
        kevs = []
        for tc in range(4):
            for dt in range(ND):
                tmm = b.op("tensor", lambda e, dt=dt, tc=tc: e.matmul(
                    psf(2, 512)[0:64, :],
                    WK1[:, dt * 64:(dt + 1) * 64],
                    XNT[:, dt * T + tc * 512: dt * T + (tc + 1) * 512],
                    start=(dt == 0), stop=(dt == ND - 1),
                    skip_group_check=True),
                    wait=(([tk_[0], tko[0]] + ([kevs[-1]] if kevs else []))
                          if dt == 0 else []),
                    inc=[(s_te, 1)])
            tev = b.op("scalar", lambda e, tc=tc: e.activation(
                KTO1[0:64, tc * 512:(tc + 1) * 512], psf(2, 512)[0:64, :],
                AF.Identity, bias=BKC[0:64, 0:1]),
                wait=[tmm[0]], inc=[(s_se, 1)])
            kevs.append(tev[0])
        tvo = b.op("gpsimd", lambda e: e.memset(V1[:], 1.0),
                   wait=[tko[0]], inc=[(s_gp, 1)])
        vevs = []
        for kt in range(NT):
            for dt in range(ND):
                tmm = b.op("tensor", lambda e, dt=dt, kt=kt: e.matmul(
                    psf(3, 64),
                    XNT[:, dt * T + kt * 128: dt * T + (kt + 1) * 128],
                    WV1[:, dt * 64:(dt + 1) * 64],
                    start=(dt == 0), stop=False, skip_group_check=True),
                    wait=(([tv_[0], tvo[0]] + ([vevs[-1]] if vevs else []))
                          if dt == 0 else []),
                    inc=[(s_te, 1)])
            tbb = b.op("tensor", lambda e: e.matmul(
                psf(3, 64), ONESB[:], BVROW[:], start=False, stop=True,
                skip_group_check=True), inc=[(s_te, 1)])
            tev = b.op("scalar", lambda e, kt=kt: e.copy(
                V1[:, kt * 66: kt * 66 + 64], psf(3, 64)),
                wait=[tbb[0]], inc=[(s_se, 1)])
            vevs.append(tev[0])

        woadd_prev = None
        wo_used = None
        for hf in range(2):
            hoff = hf * 1024
            tq = b.op("gpsimd", lambda e: e.dma_start(
                out=WQ1.rearrange("p (a c) -> p a c", a=ND),
                in_=g["L1_attn_Wq"].rearrange("(a p) c -> p a c", p=128)),
                wait=[wo_used] if wo_used else [], inc=[(s_gdma, 16)])
            qtm_free = [None, None]
            last_ot = None
            for j in range(8):
                pevs = []
                for tc in range(2):
                    bank = tc % 2
                    for dt in range(ND):
                        tmm = b.op("tensor",
                                   lambda e, j=j, dt=dt, tc=tc, bank=bank,
                                   hoff=hoff: e.matmul(
                            psf(bank, 512),
                            WQ1[:, dt * 1024 + j * 128: dt * 1024 + (j + 1) * 128],
                            XNT[:, dt * T + hoff + tc * 512:
                                dt * T + hoff + (tc + 1) * 512],
                            start=(dt == 0), stop=(dt == ND - 1),
                            skip_group_check=True),
                            wait=(([tq[0]] + [x for x in qtm_free if x] +
                                   (pevs[-2:] if pevs else []))
                                  if dt == 0 else []),
                            inc=[(s_te, 1)])
                    ev2 = []
                    for hh in range(2):
                        tev = b.op("scalar",
                                   lambda e, j=j, tc=tc, hh=hh, bank=bank:
                                   e.activation(
                            QTM1s[hh][0:64, tc * 512:(tc + 1) * 512],
                            psf(bank, 512)[hh * 64:hh * 64 + 64, :],
                            AF.Identity, scale=0.125,
                            bias=BQC[hh * 64:hh * 64 + 64, j:j + 1]),
                            wait=[tmm[0]], inc=[(s_se, 1)])
                        ev2.append(tev[0])
                    pevs.append(ev2[-1])
                b.wait("tensor", [pevs[-1]])
                qproj_done = pevs[-1]

                for hh in range(2):
                    h = 2 * j + hh
                    QTM = QTM1s[hh]
                    M128 = RT[:, 35:51]
                    p1prev = {}
                    for t8 in range(8):
                        bank = 2 + t8 % 2
                        tmm1 = b.op("tensor",
                                    lambda e, t8=t8, bank=bank, QTM=QTM:
                                    e.matmul(
                            psf(bank, 256),
                            QTM[0:64, t8 * 128:(t8 + 1) * 128],
                            KTO1[0:64, 0:256], start=True, stop=True),
                            wait=([qproj_done, kevs[-1], vevs[-1]] if t8 == 0
                                  else ([p1prev[bank]] if bank in p1prev
                                        else [])),
                            inc=[(s_te, 1)])
                        trm = b.op("vector",
                                   lambda e, t8=t8, bank=bank, M128=M128:
                                   e.reduce_max(
                            bcol(M128, t8), psf(bank, 256),
                            axis=AX.X, negate=True),
                            wait=[tmm1[0]], inc=[(s_ve, 1)])
                        p1prev[bank] = trm[0]
                    p1prev = trm[0]
                    ttm = b.op("tensor", lambda e, M128=M128: e.transpose(
                        psf(2, 128)[0:16, :], M128[:, 0:16], IDF[:]),
                        wait=[p1prev], inc=[(s_te, 1)])
                    tme = b.op("scalar", lambda e: e.copy(
                        MRW1[0:8, 0:128], psf(2, 128)[0:8, :]),
                        wait=[ttm[0]], inc=[(s_se, 1)])
                    b.wait("tensor", [tme[0]])
                    tmd = b.op("gpsimd", lambda e, QTM=QTM: e.dma_start(
                        out=QTM[64:65, 0:1024], in_=MRW1[0:8, 0:128]),
                        wait=[tme[0]], inc=[(s_gdma, 16)])

                    et_last2 = [tmd[0], tmd[0]]
                    avprev = [None, None]
                    for qc in range(4):
                        etevs = []
                        for kt in range(NT):
                            bank = (kt % 4) // 2
                            off = (kt % 2) * 256
                            tmm = b.op("tensor",
                                       lambda e, kt=kt, qc=qc, bank=bank,
                                       off=off, QTM=QTM: e.matmul(
                                psf(bank, 256, off=off),
                                KTO1[0:65, kt * 128:(kt + 1) * 128],
                                QTM[0:65, qc * 256:(qc + 1) * 256],
                                start=True, stop=True),
                                wait=([et_last2[bank]] if kt < 4 and kt % 2 == 0
                                      else ([etevs[kt // 2 - 2]]
                                            if kt % 2 == 0 and kt >= 4 else [])),
                                inc=[(s_te, 1)])
                            if kt % 2 == 1:
                                tex = b.op("scalar", lambda e, kt=kt, bank=bank:
                                           e.activation(
                                    ETH1[:, (kt - 1) * 256:(kt + 1) * 256],
                                    psf(bank, 512), AF.Exp),
                                    wait=[tmm[0]], inc=[(s_se, 1)])
                                etevs.append(tex[0])
                        et_last2 = [etevs[-2], etevs[-1]]
                        for ms in range(2):
                            bank = 4 + ms
                            for kt in range(NT):
                                tmm = b.op("tensor",
                                           lambda e, kt=kt, ms=ms, bank=bank:
                                           e.matmul(
                                    PSF[:, bank * 512: bank * 512 + 66],
                                    ETH1[:, kt * 256 + ms * 128:
                                         kt * 256 + ms * 128 + 128],
                                    V1[:, kt * 66:(kt + 1) * 66],
                                    start=(kt == 0), stop=(kt == NT - 1),
                                    skip_group_check=True),
                                    wait=(([etevs[-1]] +
                                           ([avprev[ms]] if avprev[ms] else []))
                                          if kt == 0 else []),
                                    inc=[(s_te, 1)])
                            sl = qc * 2 + ms
                            trz = b.op("vector", lambda e, bank=bank, sl=sl:
                                       e.reciprocal(
                                TCOL2[:, sl:sl + 1],
                                PSF[:, bank * 512 + 64: bank * 512 + 65]),
                                wait=[tmm[0]] + ([b.ohs_free] if sl == 0 and
                                                 b.ohs_free else []),
                                inc=[(s_ve, 1)])
                            tevo = b.op("vector", lambda e, bank=bank, sl=sl:
                                        e.tensor_scalar(
                                OHS1[:, sl * 64:(sl + 1) * 64],
                                PSF[:, bank * 512: bank * 512 + 64],
                                TCOL2[:, sl:sl + 1], None, ALU.mult),
                                wait=[trz[0]], inc=[(s_ve, 1)])
                            avprev[ms] = tevo[0]
                            if sl == 7:
                                for j2 in range(8):
                                    tto = b.op("tensor", lambda e, j2=j2:
                                               e.transpose(
                                        psb(1, 128, off=j2 * 128)[0:64, :],
                                        OHS1[:, j2 * 64:(j2 + 1) * 64],
                                        IDB[:]),
                                        wait=([tevo[0]] + ([b.psb1_free] if
                                              b.psb1_free else []))
                                        if j2 == 0 else [],
                                        inc=[(s_te, 1)])
                                tote = b.op("scalar", lambda e, h=h: e.copy(
                                    OT1H[(h % 2) * 64:(h % 2) * 64 + 64,
                                         (h // 2) * 1024:(h // 2 + 1) * 1024],
                                    psb(1, 1024)[0:64, :]),
                                    wait=[tto[0]], inc=[(s_se, 1)])
                                b.ohs_free = tto[0]
                                b.psb1_free = tote[0]
                                last_ot = tote[0]
                    qtm_free[hh] = et_last2[1]

            two = b.op("gpsimd", lambda e: e.dma_start(
                out=WQ1.rearrange("p (a c) -> p a c", a=ND),
                in_=g["L1_attn_Wo"].rearrange("(a p) c -> p a c", p=128)),
                wait=[qproj_done], inc=[(s_gdma, 16)])
            for t8 in range(8):
                t = hf * 8 + t8
                for dc in range(2):
                    bank = dc
                    for ht in range(8):
                        tmm = b.op("tensor",
                                   lambda e, ht=ht, dc=dc, t8=t8, bank=bank:
                                   e.matmul(
                            psf(bank, 512),
                            OT1H[:, ht * 1024 + t8 * 128:
                                 ht * 1024 + (t8 + 1) * 128],
                            WQ1[:, ht * 1024 + dc * 512:
                                ht * 1024 + (dc + 1) * 512],
                            start=(ht == 0), stop=False,
                            skip_group_check=True),
                            wait=(([last_ot, two[0]] +
                                   ([woadd_prev] if woadd_prev else []))
                                  if ht == 0 else []),
                            inc=[(s_te, 1)])
                    tbb = b.op("tensor", lambda e, dc=dc, bank=bank: e.matmul(
                        psf(bank, 512), ONESB[:],
                        ROWSB[0:1, D + dc * 512: D + (dc + 1) * 512],
                        start=False, stop=True, skip_group_check=True),
                        inc=[(s_te, 1)])
                    tadd = b.op("vector", lambda e, t=t, dc=dc, bank=bank:
                                e.tensor_add(
                        X[:, t * D + dc * 512: t * D + (dc + 1) * 512],
                        X[:, t * D + dc * 512: t * D + (dc + 1) * 512],
                        psf(bank, 512)),
                        wait=[tbb[0]], inc=[(s_ve, 1)])
                    woadd_prev = tadd[0]
            wo_used = tmm[0]
        b.barrier()

    OUTACC = arb(64 * KB, 8 * 1024)
    SILU = arf(80 * KB, 1024)
    XSTG2 = [arf(84 * KB, D), arf(88 * KB, D)]

    def w1buf(p, ag):
        return arb((p * 16 + ag * 8) * KB, ND * 512)

    def w2buf(p):
        return arb(p * 16 * KB, 32 * 256)

    def t3t(kt):
        return arb(96 * KB + kt * 2 * KB, 1024)

    def stage_moe(bi):
        acc_free = None
        wreg_free = None
        t3_free = None
        out_done = []
        for c in range(2):
            tms = b.op("gpsimd", lambda e: e.memset(OUTACC[:], 0.0),
                       wait=[acc_free] if acc_free else [], inc=[(s_gp, 1)])
            acc_prev = tms[0]
            for e_ in range(NEXP):
                last_t3 = None
                pair_ev = {}
                dma_toks = {}
                for grp in range(8):
                    p = grp % 2
                    wa = b.op("gpsimd", lambda e, e_=e_, grp=grp, p=p:
                              e.dma_start(
                        out=w1buf(p, 0).rearrange("q (a c) -> q a c", a=ND),
                        in_=g["L1_ffn_w1"][e_].rearrange(
                            "(a q) f -> q a f", q=128)[:, :,
                            grp * 512:(grp + 1) * 512]),
                        wait=([wreg_free] if wreg_free and grp < 2 else []) +
                             ([dma_toks[grp - 2][2]] if grp >= 2 else []),
                        inc=[(s_gdma, 16)])
                    wg = b.op("gpsimd", lambda e, e_=e_, grp=grp, p=p:
                              e.dma_start(
                        out=w1buf(p, 1).rearrange("q (a c) -> q a c", a=ND),
                        in_=g["L1_ffn_w1"][e_].rearrange(
                            "(a q) f -> q a f", q=128)[:, :,
                            FFN + grp * 512: FFN + (grp + 1) * 512]),
                        inc=[(s_gdma, 16)])
                    grp_last = None
                    for i in range(4):
                        pp = i % 2
                        ab = 4 * pp
                        gb = 4 * pp + 2
                        ja = grp * 4 + i
                        for nch in range(2):
                            for dt in range(ND):
                                tma = b.op("tensor",
                                           lambda e, i=i, dt=dt, nch=nch,
                                           c=c, p=p, ab=ab: e.matmul(
                                    psf(ab + nch, 512),
                                    w1buf(p, 0)[:, dt * 512 + i * 128:
                                                dt * 512 + (i + 1) * 128],
                                    XNT[:, dt * T + c * 1024 + nch * 512:
                                        dt * T + c * 1024 + (nch + 1) * 512],
                                    start=(dt == 0), stop=(dt == ND - 1),
                                    skip_group_check=True),
                                    wait=(([wa[0]] +
                                           ([pair_ev[pp][0]] if pp in pair_ev
                                            else []) +
                                           ([t3_free] if t3_free and grp == 0
                                            and i == 0 else []))
                                          if dt == 0 and nch == 0 else []),
                                    inc=[(s_te, 1)])
                            for dt in range(ND):
                                tmg = b.op("tensor",
                                           lambda e, i=i, dt=dt, nch=nch,
                                           c=c, p=p, gb=gb: e.matmul(
                                    psf(gb + nch, 512),
                                    w1buf(p, 1)[:, dt * 512 + i * 128:
                                                dt * 512 + (i + 1) * 128],
                                    XNT[:, dt * T + c * 1024 + nch * 512:
                                        dt * T + c * 1024 + (nch + 1) * 512],
                                    start=(dt == 0), stop=(dt == ND - 1),
                                    skip_group_check=True),
                                    wait=(([wg[0]] +
                                           ([pair_ev[pp][1]] if pp in pair_ev
                                            else []))
                                          if dt == 0 and nch == 0 else []),
                                    inc=[(s_te, 1)])
                        tsi = b.op("scalar", lambda e, ab=ab: e.activation(
                            SILU[:], PSF[:, ab * 512:(ab + 2) * 512],
                            AF.Silu), wait=[tma[0]], inc=[(s_se, 1)])
                        tt3 = b.op("vector", lambda e, ja=ja, gb=gb: e.tensor_mul(
                            t3t(ja), SILU[:], PSF[:, gb * 512:(gb + 2) * 512]),
                            wait=[tsi[0], tmg[0]], inc=[(s_ve, 1)])
                        pair_ev[pp] = (tt3[0], tt3[0])
                        grp_last = tt3[0]
                        last_t3 = tt3[0]
                    dma_toks[grp] = (wa[0], wg[0], grp_last)
                w2toks = {}
                y2prev = {}
                for nq in range(4):
                    p = nq % 2
                    wts = []
                    for piece in range(4):
                        tw2 = b.op("gpsimd",
                                   lambda e, e_=e_, nq=nq, p=p, piece=piece:
                                   e.dma_start(
                            out=w2buf(p).rearrange(
                                "q (a c) -> q a c", a=32)[:, piece * 8:
                                                          (piece + 1) * 8],
                            in_=g["L1_ffn_w2"][e_].rearrange(
                                "(a q) f -> q a f", q=128)[:, piece * 8:
                                                           (piece + 1) * 8,
                                nq * 256:(nq + 1) * 256]),
                            wait=([last_t3] if nq < 2 and piece == 0 else []) +
                                 ([w2toks[nq - 2]] if nq >= 2 and piece == 0
                                  else []),
                            inc=[(s_gdma, 16)])
                        wts.append(tw2[0])
                    for mt in range(8):
                        bank = mt % 2
                        for kt in range(32):
                            tmy = b.op("tensor",
                                       lambda e, kt=kt, mt=mt, nq=nq, p=p,
                                       bank=bank: e.matmul(
                                psf(bank, 256),
                                t3t(kt)[:, mt * 128:(mt + 1) * 128],
                                w2buf(p)[:, kt * 256:(kt + 1) * 256],
                                start=(kt == 0), stop=(kt == 31),
                                skip_group_check=True),
                                wait=((wts + [last_t3] +
                                       ([y2prev[bank]] if bank in y2prev
                                        else []))
                                      if kt == 0 else []),
                                inc=[(s_te, 1)])
                        tya = b.op("vector",
                                   lambda e, mt=mt, nq=nq, c=c, e_=e_,
                                   bank=bank: e.scalar_tensor_tensor(
                            OUTACC[:, mt * 1024 + nq * 256:
                                   mt * 1024 + (nq + 1) * 256],
                            psf(bank, 256),
                            SELC[:, (8 * c + mt) * NEXP + e_:
                                 (8 * c + mt) * NEXP + e_ + 1],
                            OUTACC[:, mt * 1024 + nq * 256:
                                   mt * 1024 + (nq + 1) * 256],
                            ALU.mult, ALU.add),
                            wait=[tmy[0], acc_prev], inc=[(s_ve, 1)])
                        y2prev[bank] = tya[0]
                    w2toks[nq] = tmy[0]
                acc_prev = tya[0]
                wreg_free = tya[0]
                t3_free = tya[0]
            slot_busy = [None, None]
            for mt in range(8):
                tg = 8 * c + mt
                sl = mt % 2
                tl = b.op("sync", lambda e, tg=tg, sl=sl: e.dma_start(
                    out=XSTG2[sl][:], in_=hbm_tiles(scratch_x1[bi])[tg]),
                    wait=[slot_busy[sl]] if slot_busy[sl] else [],
                    inc=[(s_dma, 16)])
                ta_ = b.op("vector", lambda e, mt=mt, sl=sl: e.tensor_add(
                    XSTG2[sl][:], XSTG2[sl][:],
                    OUTACC[:, mt * 1024:(mt + 1) * 1024]),
                    wait=[tl[0], acc_prev], inc=[(s_ve, 1)])
                tw = b.op("sync", lambda e, tg=tg, sl=sl, bi=bi: e.dma_start(
                    out=hbm_tiles(out[bi])[tg], in_=XSTG2[sl][:]),
                    wait=[ta_[0]], inc=[(s_out, 16)])
                slot_busy[sl] = tw[0]
                out_done.append(tw[0])
            acc_free = ta_[0]
        b.barrier()

    setup_constants()
    for bi in range(NB):
        stage_ln(bi, 0)
        if stop_after == "ln0":
            for t in range(NT):
                b.op("sync", lambda e, t=t, bi=bi: e.dma_start(
                    out=hbm_tiles(out[bi])[t], in_=bcol(X, t, D)),
                    inc=[(s_out, 16)])
            continue
        offload_X(bi)
        stage_ssm(bi)
        stage_yadd(bi)
        rsteps = routing_steps(bi)
        if stop_after == "ssm":
            for r in rsteps:
                r()
            b.barrier()
            b.op("sync", lambda e, bi=bi: e.dma_start(
                out=out[bi], in_=scratch_x1[bi]), inc=[(s_out, 16)])
            continue
        stage_light(bi, rsteps)
        stage_heavy(bi)
        if stop_after == "block0":
            for t in range(NT):
                b.op("sync", lambda e, t=t, bi=bi: e.dma_start(
                    out=hbm_tiles(out[bi])[t], in_=bcol(X, t, D)),
                    inc=[(s_out, 16)])
            continue
        stage_ln(bi, 1)
        stage_l1attn(bi)
        if stop_after == "l1attn":
            for t in range(NT):
                b.op("sync", lambda e, t=t, bi=bi: e.dma_start(
                    out=hbm_tiles(out[bi])[t], in_=bcol(X, t, D)),
                    inc=[(s_out, 16)])
            continue
        offload_X(bi)
        stage_moe(bi)

    b.wait("sync", [(s_out, s_out.n)])
    b.barrier()
    b.finalize()
    es.close()
    return nc, tap_handles


def kernel(**inputs):
    nc, _ = build_program(stop_after=None)
    xs = np.ascontiguousarray(inputs["L0_attn_x"], np.float32)
    common = {k: np.ascontiguousarray(np.asarray(v), np.float32)
              for k, v in inputs.items() if k != "L0_attn_x"}
    in_maps = []
    for c in range(8):
        m = dict(common)
        m["x"] = xs[2 * c:2 * c + 2]
        in_maps.append(m)
    res = run_bass_kernel_spmd(nc, in_maps, core_ids=list(range(8)))
    return np.concatenate([r["out"] for r in res.results], axis=0)
